# version 06
# baseline (speedup 1.0000x reference)
import sys
for _p in ("/opt/trn_rl_repo",):
    if _p not in sys.path:
        sys.path.append(_p)
import numpy as np
import concourse.bass as bass
import concourse.mybir as mybir
from concourse.bass_utils import run_bass_kernel_spmd
from concourse.alu_op_type import AluOpType as Op
from contextlib import ExitStack

F32 = mybir.dt.float32
BF16 = mybir.dt.bfloat16
AF = mybir.ActivationFunctionType
AX = mybir.AxisListType

ENGINES = ("sync", "tensor", "vector", "scalar", "gpsimd")


class Prog:
    def __init__(self, nc, stack):
        self.nc = nc
        e = stack.enter_context
        self.sems = {n: e(nc.semaphore(f"sem_{n}")) for n in
                     ("ld", "st", "te", "ve", "se", "gp", "ld2")}
        self.cnt = {n: 0 for n in self.sems}
        self.ops = {n: [] for n in ENGINES}

    def emit(self, eng, fn, inc=None, amt=1):
        if inc is not None:
            self.cnt[inc] += amt
            sem = self.sems[inc]

            def wrapped(e, fn=fn, sem=sem, amt=amt):
                ins = fn(e)
                ins.then_inc(sem, amt)
            self.ops[eng].append(wrapped)
            return self.cnt[inc]
        else:
            self.ops[eng].append(fn)
            return None

    def wait(self, eng, sem_name, val):
        sem = self.sems[sem_name]
        self.ops[eng].append(lambda e, sem=sem, val=val: e.wait_ge(sem, val))

    def drain(self, eng):
        m = {"vector": "ve", "scalar": "se", "tensor": "te", "gpsimd": "gp"}
        s = m.get(eng)
        if s is not None:
            self.wait(eng, s, self.cnt[s])

    def dma(self, eng, out, in_, inc="ld", **kw):
        return self.emit(eng, lambda e, o=out, i=in_: e.dma_start(out=o, in_=i, **kw),
                         inc=inc, amt=16)

    def replay(self, block):
        for name in ENGINES:
            ops = self.ops[name]

            def make(ops):
                def run(engine):
                    for f in ops:
                        f(engine)
                return run
            getattr(block, name)(make(ops))


T = 8192
D = 1024
S = 128
C = 128
HF = 2048
CH = 512
Q = 1024
EPS = 1e-5
NF = 16
DI = 2048
DS = 16
DTR = 64
RMS_EPS = 1.1920929e-07
LN_EPS = 1e-5


def emit_l0attn(nc, p, x, basis, qc, kc, vc, oc, decay, oscale, x1T,
                own_start, own_end, debug=False):
    n_own0 = own_start // S
    n_all = own_end // S
    with ExitStack() as st_:
        e = st_.enter_context
        sb_basis = e(nc.sbuf_tensor("sb_basis", [128, 8 * 64], F32))
        sb_basisT = e(nc.sbuf_tensor("sb_basisT", [64, D], F32))
        sb_c = {n: e(nc.sbuf_tensor(f"sb_{n}c", [C, 64], F32)) for n in "qkvo"}
        sb_cT = {n: e(nc.sbuf_tensor(f"sb_{n}cT", [64, C], F32)) for n in "qkvo"}
        w_T_f = {n: e(nc.sbuf_tensor(f"w_{n}T_f", [C, D], F32)) for n in "qkv"}
        w_T_b = {n: e(nc.sbuf_tensor(f"w_{n}T_b", [C, D], BF16)) for n in "qkvo"}
        w_st = {n: e(nc.sbuf_tensor(f"w_{n}st", [128, 8 * C], BF16)) for n in "qkv"}
        ident_f = e(nc.sbuf_tensor("ident_f", [128, 128], F32))
        ident_b = e(nc.sbuf_tensor("ident_b", [128, 128], BF16))
        iota_sc = e(nc.sbuf_tensor("iota_sc", [128, 128], F32))
        ones_f = e(nc.sbuf_tensor("ones_f", [1, 128], F32))
        sc11 = {n: e(nc.sbuf_tensor(f"sc11_{n}", [1, 1], F32)) for n in
                ("decay", "oscale", "t", "t1", "sp", "logd")}
        bc = {n: e(nc.sbuf_tensor(f"bc_{n}", [128, 1], F32)) for n in
              ("logd", "dS", "drev", "os", "eps", "mx", "nmx", "ssum", "rs",
               "s1", "mean", "v2", "lnv", "rstd")}
        Dcol = e(nc.sbuf_tensor("Dcol", [128, S], F32))
        maskE = e(nc.sbuf_tensor("maskE", [128, S], F32))
        binm = e(nc.sbuf_tensor("binm", [128, S], F32))
        mask_f = e(nc.sbuf_tensor("mask_f", [128, S], F32))
        M_f = e(nc.sbuf_tensor("M_f", [C, C], F32))
        M_b = e(nc.sbuf_tensor("M_b", [C, C], BF16))
        xf = [e(nc.sbuf_tensor(f"xf{i}", [S, D], F32)) for i in range(2)]
        xb = e(nc.sbuf_tensor("xb", [S, D], BF16))
        xT = e(nc.sbuf_tensor("xT", [128, 8 * S], BF16))
        kT_b = e(nc.sbuf_tensor("kT_b", [C, S], BF16))
        k_s = e(nc.sbuf_tensor("k_s", [S, C], BF16))
        vT_b = e(nc.sbuf_tensor("vT_b", [C, S], BF16))
        v_b = e(nc.sbuf_tensor("v_b", [S, C], BF16))
        qT_b = e(nc.sbuf_tensor("qT_b", [C, S], BF16))
        qTs_b = e(nc.sbuf_tensor("qTs_b", [C, S], BF16))
        P_b = e(nc.sbuf_tensor("P_b", [S, S], BF16))
        ret_b = e(nc.sbuf_tensor("ret_b", [C, S], BF16))
        t_x1 = e(nc.sbuf_tensor("t_x1", [S, D], F32))
        scr = e(nc.sbuf_tensor("scr", [S, D], F32))
        x1c = e(nc.sbuf_tensor("x1c", [S, D], F32))
        x1tc = e(nc.sbuf_tensor("x1tc", [128, 8 * S], F32))
        p_trf = e(nc.psum_tensor("p_trf", [128, 128], F32))
        p_trb = [e(nc.psum_tensor(f"p_trb{i}", [128, 128], BF16)) for i in range(2)]
        p_proj = e(nc.psum_tensor("p_proj", [128, S], F32))
        p_sc = e(nc.psum_tensor("p_sc", [128, S], F32))
        p_ret = e(nc.psum_tensor("p_ret", [C, S], F32))
        p_o = e(nc.psum_tensor("p_o", [128, 1024], F32))


        p.dma("sync", sb_basis.ap().rearrange("p (j n) -> p j n", j=8),
              basis.rearrange("(j p) n -> p j n", j=8))
        for n, t in (("q", qc), ("k", kc), ("v", vc), ("o", oc)):
            p.dma("sync", sb_c[n][:], t)
        p.dma("sync", sc11["decay"][:], decay)
        p.dma("sync", sc11["oscale"][:], oscale)
        ld_setup = p.cnt["ld"]

        p.emit("gpsimd", lambda g: g.iota(iota_sc[:], pattern=[[-1, 128]], base=0,
                                          channel_multiplier=1,
                                          allow_small_or_imprecise_dtypes=True), inc="gp")
        p.wait("vector", "gp", p.cnt["gp"])
        p.emit("vector", lambda v: v.tensor_scalar(out=ident_f[:], in0=iota_sc[:],
                                                   scalar1=0.0, scalar2=None, op0=Op.is_equal), inc="ve")
        p.drain("vector")
        p.emit("vector", lambda v: v.tensor_copy(ident_b[:], ident_f[:]), inc="ve")
        p.emit("vector", lambda v: v.memset(ones_f[:], 1.0), inc="ve")
        ve_ident = p.cnt["ve"]

        p.wait("tensor", "ld", ld_setup)
        p.wait("tensor", "ve", ve_ident)
        ev_after_tr = []
        for j in range(8):
            te = p.emit("tensor", lambda t, j=j: t.transpose(
                p_trf[0:64, 0:128], sb_basis[:, j * 64:(j + 1) * 64], ident_f[:]), inc="te")
            p.wait("vector", "te", te)
            ve = p.emit("vector", lambda v, j=j: v.tensor_copy(
                sb_basisT[:, j * 128:(j + 1) * 128], p_trf[0:64, 0:128]), inc="ve")
            p.wait("tensor", "ve", ve)
        for n in "qkvo":
            te = p.emit("tensor", lambda t, n=n: t.transpose(
                p_trf[0:64, 0:128], sb_c[n][:], ident_f[:]), inc="te")
            p.wait("vector", "te", te)
            ve = p.emit("vector", lambda v, n=n: v.tensor_copy(
                sb_cT[n][:], p_trf[0:64, 0:128]), inc="ve")
            p.wait("tensor", "ve", ve)

        for n in "qkvo":
            for blk in range(2):
                te = p.emit("tensor", lambda t, n=n, blk=blk: t.matmul(
                    p_o[:, blk * 512:(blk + 1) * 512], sb_cT[n][:],
                    sb_basisT[:, blk * 512:(blk + 1) * 512],
                    start=True, stop=True), inc="te")
            p.wait("vector", "te", p.cnt["te"])
            if n == "o":
                ve = p.emit("vector", lambda v: v.tensor_copy(w_T_b["o"][:], p_o[:]), inc="ve")
                p.wait("tensor", "ve", ve)
            else:
                p.emit("vector", lambda v: v.reduce_max(bc["mx"][:], p_o[:], axis=AX.X), inc="ve")
                p.drain("vector")
                ve = p.emit("vector", lambda v: v.tensor_scalar(
                    out=bc["nmx"][:], in0=bc["mx"][:], scalar1=-1.0, scalar2=None, op0=Op.mult), inc="ve")
                p.wait("scalar", "ve", ve)
                se = p.emit("scalar", lambda s, n=n: s.activation(
                    out=w_T_f[n][:], in_=p_o[:], func=AF.Exp, bias=bc["nmx"][:, 0:1],
                    accum_out=bc["ssum"][:, 0:1]), inc="se")
                p.wait("vector", "se", se)
                p.emit("vector", lambda v: v.reciprocal(bc["rs"][:], bc["ssum"][:]), inc="ve")
                p.drain("vector")
                p.emit("vector", lambda v, n=n: v.tensor_scalar(
                    out=w_T_f[n][:], in0=w_T_f[n][:], scalar1=bc["rs"][:, 0:1],
                    scalar2=None, op0=Op.mult), inc="ve")
                p.drain("vector")
                ve = p.emit("vector", lambda v, n=n: v.tensor_copy(w_T_b[n][:], w_T_f[n][:]), inc="ve")
                p.wait("tensor", "ve", ve)

        for n in "qkv":
            for j in range(8):
                te = p.emit("tensor", lambda t, n=n, j=j: t.transpose(
                    p_trb[j % 2][:], w_T_b[n][:, j * 128:(j + 1) * 128], ident_b[:]), inc="te")
                p.wait("vector", "te", te)
                ve = p.emit("vector", lambda v, n=n, j=j: v.tensor_copy(
                    w_st[n][:, j * 128:(j + 1) * 128], p_trb[j % 2][:]), inc="ve")
                if j >= 1:
                    p.wait("tensor", "ve", ve - 1)
            p.wait("tensor", "ve", p.cnt["ve"])

        p.wait("scalar", "ld", ld_setup)
        p.emit("scalar", lambda s: s.activation(out=sc11["t"][:], in_=sc11["decay"][:],
                                                func=AF.Exp, scale=-1.0), inc="se")
        p.wait("vector", "se", p.cnt["se"])
        p.emit("vector", lambda v: v.tensor_scalar(out=sc11["t1"][:], in0=sc11["t"][:],
                                                   scalar1=1.0, scalar2=None, op0=Op.add), inc="ve")
        p.wait("scalar", "ve", p.cnt["ve"])
        p.emit("scalar", lambda s: s.activation(out=sc11["sp"][:], in_=sc11["t1"][:],
                                                func=AF.Ln), inc="se")
        p.wait("vector", "se", p.cnt["se"])
        ve = p.emit("vector", lambda v: v.tensor_scalar(out=sc11["logd"][:], in0=sc11["sp"][:],
                                                        scalar1=-1.0, scalar2=None, op0=Op.mult), inc="ve")
        p.wait("tensor", "ve", ve)
        te = p.emit("tensor", lambda t: t.matmul(p_trf[:, 0:1], ones_f[:], sc11["logd"][:],
                                                 start=True, stop=True), inc="te")
        p.wait("vector", "te", te)
        ve = p.emit("vector", lambda v: v.tensor_copy(bc["logd"][:], p_trf[:, 0:1]), inc="ve")
        p.drain("vector")
        p.wait("tensor", "ve", ve)
        te = p.emit("tensor", lambda t: t.matmul(p_trf[:, 0:1], ones_f[:], sc11["oscale"][:],
                                                 start=True, stop=True), inc="te")
        p.wait("vector", "te", te)
        ve = p.emit("vector", lambda v: v.tensor_copy(bc["os"][:], p_trf[:, 0:1]), inc="ve")
        p.wait("tensor", "ve", ve)

        p.emit("gpsimd", lambda g: g.iota(Dcol[:], pattern=[[1, S]], base=0,
                                          channel_multiplier=0,
                                          allow_small_or_imprecise_dtypes=True), inc="gp")
        p.emit("gpsimd", lambda g: g.iota(maskE[:], pattern=[[1, S]], base=-1,
                                          channel_multiplier=-1,
                                          allow_small_or_imprecise_dtypes=True), inc="gp")
        p.emit("gpsimd", lambda g: g.iota(bc["drev"][:], pattern=[[0, 1]], base=S - 1,
                                          channel_multiplier=-1,
                                          allow_small_or_imprecise_dtypes=True), inc="gp")
        gp_io = p.cnt["gp"]
        p.wait("vector", "gp", gp_io)
        p.emit("vector", lambda v: v.tensor_scalar(out=binm[:], in0=maskE[:],
                                                   scalar1=0.0, scalar2=None, op0=Op.is_ge), inc="ve")
        p.drain("vector")
        p.emit("vector", lambda v: v.tensor_tensor(out=maskE[:], in0=maskE[:], in1=binm[:],
                                                   op=Op.mult), inc="ve")
        p.wait("scalar", "gp", gp_io)
        p.wait("scalar", "ve", p.cnt["ve"])
        p.emit("scalar", lambda s: s.activation(out=Dcol[:], in_=Dcol[:], func=AF.Exp,
                                                scale=bc["logd"][:, 0:1]), inc="se")
        p.emit("scalar", lambda s: s.activation(out=maskE[:], in_=maskE[:], func=AF.Exp,
                                                scale=bc["logd"][:, 0:1]), inc="se")
        p.emit("scalar", lambda s: s.activation(out=bc["drev"][:], in_=bc["drev"][:], func=AF.Exp,
                                                scale=bc["logd"][:, 0:1]), inc="se")
        p.wait("vector", "se", p.cnt["se"])
        p.emit("vector", lambda v: v.tensor_tensor(out=mask_f[:], in0=maskE[:], in1=binm[:],
                                                   op=Op.mult), inc="ve")
        p.emit("vector", lambda v: v.tensor_scalar(out=bc["dS"][:], in0=bc["logd"][:],
                                                   scalar1=float(S), scalar2=None, op0=Op.mult), inc="ve")
        p.wait("scalar", "ve", p.cnt["ve"])
        p.emit("scalar", lambda s: s.activation(out=bc["dS"][:], in_=bc["dS"][:], func=AF.Exp), inc="se")
        p.emit("vector", lambda v: v.memset(bc["eps"][:], 1e-5), inc="ve")
        p.emit("vector", lambda v: v.memset(M_f[:], 0.0), inc="ve")
        p.emit("vector", lambda v: v.memset(M_b[:], 0.0), inc="ve")
        setup_se = p.cnt["se"]
        setup_ve = p.cnt["ve"]
        p.wait("tensor", "se", setup_se)
        p.wait("tensor", "ve", setup_ve)
        p.wait("sync", "ve", setup_ve)
        xf_free = [None, None]
        last_x1_store = 0
        for c in range(n_all):
            own = c >= n_own0
            par = c % 2
            t0 = c * S
            if xf_free[par] is not None:
                p.wait("sync", "ve", xf_free[par])
            p.dma("sync", xf[par][:], x[t0:t0 + S, :])
            ld_x = p.cnt["ld"]
            p.wait("vector", "ld", ld_x)
            ve = p.emit("vector", lambda v, par=par: v.tensor_copy(xb[:], xf[par][:]), inc="ve")
            xf_free[par] = ve
            p.wait("tensor", "ve", ve)
            for j in range(8):
                te = p.emit("tensor", lambda t, j=j: t.transpose(
                    p_trb[j % 2][:], xb[:, j * 128:(j + 1) * 128], ident_b[:]), inc="te")
                p.wait("vector", "te", te)
                ve = p.emit("vector", lambda v, j=j: v.tensor_copy(
                    xT[:, j * S:(j + 1) * S], p_trb[j % 2][:]), inc="ve")
                if j >= 1:
                    p.wait("tensor", "ve", ve - 1)
            p.wait("tensor", "ve", p.cnt["ve"])

            names = ["k", "v"] + (["q"] if own else [])
            for n in names:
                for j in range(8):
                    te = p.emit("tensor", lambda t, n=n, j=j: t.matmul(
                        p_proj[:], w_st[n][:, j * 128:(j + 1) * 128],
                        xT[:, j * S:(j + 1) * S],
                        start=(j == 0), stop=(j == 7)), inc="te")
                p.wait("vector", "te", p.cnt["te"])
                if n == "k":
                    ve = p.emit("vector", lambda v: v.tensor_copy(kT_b[:], p_proj[:]), inc="ve")
                elif n == "v":
                    ve = p.emit("vector", lambda v: v.tensor_copy(vT_b[:], p_proj[:]), inc="ve")
                else:
                    p.emit("vector", lambda v: v.tensor_copy(qT_b[:], p_proj[:]), inc="ve")
                    ve = p.emit("vector", lambda v: v.tensor_tensor(
                        out=qTs_b[:], in0=p_proj[:], in1=Dcol[:], op=Op.mult), inc="ve")
                p.wait("tensor", "ve", ve)
                if n in "kv":
                    te = p.emit("tensor", lambda t, n=n: t.transpose(
                        p_trb[0][:], kT_b[:] if n == "k" else vT_b[:], ident_b[:]), inc="te")
                    p.wait("vector", "te", te)
                    if n == "k":
                        ve = p.emit("vector", lambda v: v.tensor_scalar(
                            out=k_s[:], in0=p_trb[0][:], scalar1=bc["drev"][:, 0:1],
                            scalar2=None, op0=Op.mult), inc="ve")
                    else:
                        ve = p.emit("vector", lambda v: v.tensor_copy(v_b[:], p_trb[0][:]), inc="ve")
                    p.wait("tensor", "ve", ve)

            if own:
                te = p.emit("tensor", lambda t: t.matmul(p_sc[:], kT_b[:], qT_b[:],
                                                         start=True, stop=True), inc="te")
                p.wait("vector", "te", te)
                ve = p.emit("vector", lambda v: v.tensor_tensor(out=P_b[:], in0=p_sc[:],
                                                                in1=mask_f[:], op=Op.mult), inc="ve")
                p.wait("tensor", "ve", ve)
                p.emit("tensor", lambda t: t.matmul(p_ret[:], v_b[:], P_b[:],
                                                    start=True, stop=False), inc="te")
                te = p.emit("tensor", lambda t: t.matmul(p_ret[:], M_b[:], qTs_b[:],
                                                         start=False, stop=True), inc="te")
                p.wait("vector", "te", te)
                ve = p.emit("vector", lambda v: v.tensor_copy(ret_b[:], p_ret[:]), inc="ve")
                p.wait("tensor", "ve", ve)
                for blk in range(2):
                    te = p.emit("tensor", lambda t, blk=blk: t.matmul(
                        p_o[:, blk * 512:(blk + 1) * 512], ret_b[:],
                        w_T_b["o"][:, blk * 512:(blk + 1) * 512],
                        start=True, stop=True), inc="te")
                p.wait("vector", "te", p.cnt["te"])
                xf_free[par] = p.emit("vector", lambda v, par=par: v.scalar_tensor_tensor(
                    out=t_x1[:], in0=p_o[:], scalar=bc["os"][:, 0:1], in1=xf[par][:],
                    op0=Op.mult, op1=Op.add), inc="ve")
                p.drain("vector")
                p.emit("vector", lambda v: v.tensor_reduce(out=bc["s1"][:], in_=t_x1[:],
                                                           axis=AX.X, op=Op.add), inc="ve")
                p.drain("vector")
                p.emit("vector", lambda v: v.tensor_scalar(
                    out=bc["mean"][:], in0=bc["s1"][:], scalar1=1.0 / D, scalar2=None,
                    op0=Op.mult), inc="ve")
                p.drain("vector")
                ve = p.emit("vector", lambda v: v.tensor_scalar(
                    out=t_x1[:], in0=t_x1[:], scalar1=bc["mean"][:, 0:1], scalar2=None,
                    op0=Op.subtract), inc="ve")
                p.wait("scalar", "ve", ve)
                se = p.emit("scalar", lambda s: s.activation(out=scr[:], in_=t_x1[:],
                                                             func=AF.Square,
                                                             accum_out=bc["v2"][:, 0:1]), inc="se")
                p.drain("scalar")
                p.emit("scalar", lambda s: s.activation(out=bc["lnv"][:], in_=bc["v2"][:],
                                                        func=AF.Ln, scale=1.0 / D,
                                                        bias=bc["eps"][:, 0:1]), inc="se")
                p.drain("scalar")
                se = p.emit("scalar", lambda s: s.activation(out=bc["rstd"][:], in_=bc["lnv"][:],
                                                             func=AF.Exp, scale=-0.5), inc="se")
                p.wait("vector", "se", se)
                if last_x1_store:
                    p.wait("vector", "st", last_x1_store)
                ve = p.emit("vector", lambda v: v.tensor_scalar(
                    out=x1c[:], in0=t_x1[:], scalar1=bc["rstd"][:, 0:1], scalar2=None,
                    op0=Op.mult), inc="ve")
                p.drain("vector")
                p.wait("tensor", "ve", ve)
                for j in range(8):
                    te = p.emit("tensor", lambda t, j=j: t.transpose(
                        p_trf[:], x1c[:, j * 128:(j + 1) * 128], ident_f[:]), inc="te")
                    p.wait("vector", "te", te)
                    ve = p.emit("vector", lambda v, j=j: v.tensor_copy(
                        x1tc[:, j * S:(j + 1) * S], p_trf[:]), inc="ve")
                    p.wait("tensor", "ve", ve)
                p.wait("sync", "ve", ve)
                last_x1_store = p.dma(
                    "sync", x1T[:, t0 - own_start:t0 - own_start + S].rearrange(
                        "(j p) t -> p j t", j=8),
                    x1tc.ap().rearrange("p (j t) -> p j t", j=8), inc="st")

            te = p.emit("tensor", lambda t: t.matmul(p_sc[:], k_s[:], v_b[:],
                                                     start=True, stop=True), inc="te")
            p.wait("vector", "te", te)
            p.emit("vector", lambda v: v.tensor_scalar(out=M_f[:], in0=M_f[:],
                                                       scalar1=bc["dS"][:, 0:1], scalar2=None,
                                                       op0=Op.mult), inc="ve")
            p.drain("vector")
            p.emit("vector", lambda v: v.tensor_tensor(out=M_f[:], in0=M_f[:], in1=p_sc[:],
                                                       op=Op.add), inc="ve")
            p.drain("vector")
            ve = p.emit("vector", lambda v: v.tensor_copy(M_b[:], M_f[:]), inc="ve")
            p.wait("tensor", "ve", ve)

    return



def emit_ffn(nc, p, E, pre_ln, T_own, xT, W1, W2, b1r, b2, rW, rb, outT,
             g_r=None, b_r=None, debug=False):
    n_q = T_own // Q
    n_ch = Q // CH

    with ExitStack() as st_:
        e = st_.enter_context
        xq = e(nc.sbuf_tensor("xq", [128, 8 * Q], F32))
        zb = e(nc.sbuf_tensor("zb", [128, 8 * Q], BF16))
        mixT = e(nc.sbuf_tensor("mixT", [128, 8 * Q], F32))
        w1b = e(nc.sbuf_tensor("w1b", [128, 8 * HF], BF16))
        w2b = e(nc.sbuf_tensor("w2b", [128, NF * D], BF16))
        hid = e(nc.sbuf_tensor("hid", [128, NF * CH], BF16))
        b1s = e(nc.sbuf_tensor("b1s", [128, E * NF], F32))
        b2s = e(nc.sbuf_tensor("b2s", [E, D], BF16))
        rWs = e(nc.sbuf_tensor("rWs", [128, 8 * E], F32))
        rbs = e(nc.sbuf_tensor("rbs", [1, E], F32))
        rb_bc = e(nc.sbuf_tensor("rb_bc", [128, E], F32))
        if pre_ln:
            gs = e(nc.sbuf_tensor("gs", [128, 8], F32))
            bs = e(nc.sbuf_tensor("bs", [128, 8], F32))
        ident_f = e(nc.sbuf_tensor("ident_f", [128, 128], F32))
        ones_f = e(nc.sbuf_tensor("ones_f", [1, 128], F32))
        ones_col = e(nc.sbuf_tensor("ones_col", [128, 1], F32))
        scr_g = [e(nc.sbuf_tensor(f"scr_g{i}", [128, CH], F32)) for i in range(2)]
        scr2 = e(nc.sbuf_tensor("scr2", [128, CH], F32))
        lgacc = e(nc.sbuf_tensor("lgacc", [128, 4 * E], F32))
        st2 = e(nc.sbuf_tensor("st2", [1, CH], F32))
        mean1 = e(nc.sbuf_tensor("mean1", [1, CH], F32))
        nmean1 = e(nc.sbuf_tensor("nmean1", [1, CH], F32))
        var1 = e(nc.sbuf_tensor("var1", [1, CH], F32))
        rstd1 = e(nc.sbuf_tensor("rstd1", [1, CH], F32))
        nmr1 = e(nc.sbuf_tensor("nmr1", [1, CH], F32))
        eps_t = e(nc.sbuf_tensor("eps_t", [1, 1], F32))
        nm_bc = e(nc.sbuf_tensor("nm_bc", [128, CH], F32))
        rs_bc = e(nc.sbuf_tensor("rs_bc", [128, CH], F32))
        lg = e(nc.sbuf_tensor("lg", [128, E], F32))
        m1 = e(nc.sbuf_tensor("m1", [128, 1], F32))
        nm1 = e(nc.sbuf_tensor("nm1", [128, 1], F32))
        l2 = e(nc.sbuf_tensor("l2", [128, E], F32))
        m2 = e(nc.sbuf_tensor("m2", [128, 1], F32))
        d21 = e(nc.sbuf_tensor("d21", [128, 1], F32))
        em = e(nc.sbuf_tensor("em", [128, 1], F32))
        den = e(nc.sbuf_tensor("den", [128, 1], F32))
        rden = e(nc.sbuf_tensor("rden", [128, 1], F32))
        sexp = e(nc.sbuf_tensor("sexp", [128, E], F32))
        ind = e(nc.sbuf_tensor("ind", [128, E], F32))
        wv = e(nc.sbuf_tensor("wv", [128, E], F32))
        wT_b = e(nc.sbuf_tensor("wT_b", [E, Q], BF16))
        w_bc = e(nc.sbuf_tensor("w_bc", [128, CH], F32))
        sel_f = e(nc.sbuf_tensor("sel_f", [E, E * 128], BF16))
        out2 = [e(nc.sbuf_tensor(f"out2_{i}", [128, CH], F32)) for i in range(2)]

        p_stat = e(nc.psum_tensor("p_stat", [128, CH], F32))
        p_st1 = p_stat[0:1, :]
        p_st2 = p_stat[64:65, :]
        p_bc = e(nc.psum_tensor("p_bc", [128, CH], F32))
        p_hid = [e(nc.psum_tensor(f"p_hid{i}", [128, CH], F32)) for i in range(3)]
        p_mix = [e(nc.psum_tensor(f"p_mix{i}", [128, CH], F32)) for i in range(2)]
        p_r = e(nc.psum_tensor("p_r", [128, 128], F32))


        p.dma("sync", b1s[:], b1r)
        p.dma("gpsimd", b2s[:], b2, inc="ld2")
        p.dma("gpsimd", rWs.ap().rearrange("p (j e) -> p j e", j=8),
              rW.rearrange("(j p) e -> p j e", j=8), inc="ld2")
        p.dma("sync", rbs[:], rb)
        if pre_ln:
            p.dma("sync", gs[:], g_r)
            p.dma("sync", bs[:], b_r)
        ld0 = p.cnt["ld"]
        ld20 = p.cnt["ld2"]
        p.emit("gpsimd", lambda g: g.iota(ident_f[:], pattern=[[-1, 128]], base=0,
                                          channel_multiplier=1,
                                          allow_small_or_imprecise_dtypes=True), inc="gp")
        p.emit("gpsimd", lambda g: g.iota(sel_f[:], pattern=[[-1, E], [0, 128]], base=0,
                                          channel_multiplier=1,
                                          allow_small_or_imprecise_dtypes=True), inc="gp")
        p.wait("vector", "gp", p.cnt["gp"])
        p.emit("vector", lambda v: v.tensor_scalar(out=sel_f[:], in0=sel_f[:],
                                                   scalar1=0.0, scalar2=None,
                                                   op0=Op.is_equal), inc="ve")
        p.drain("vector")
        p.emit("vector", lambda v: v.tensor_scalar(out=ident_f[:], in0=ident_f[:],
                                                   scalar1=0.0, scalar2=None,
                                                   op0=Op.is_equal), inc="ve")
        p.drain("vector")
        p.emit("vector", lambda v: v.memset(ones_f[:], 1.0), inc="ve")
        p.emit("vector", lambda v: v.memset(ones_col[:], 1.0), inc="ve")
        p.emit("vector", lambda v: v.memset(eps_t[:], EPS), inc="ve")
        ve0 = p.cnt["ve"]
        p.wait("tensor", "ld", ld0)
        p.wait("tensor", "ve", ve0)
        te = p.emit("tensor", lambda t: t.matmul(p_r[:, 0:E], ones_f[:], rbs[:],
                                                 start=True, stop=True), inc="te")
        p.wait("vector", "te", te)
        ve = p.emit("vector", lambda v: v.tensor_copy(rb_bc[:], p_r[:, 0:E]), inc="ve")
        p.drain("vector")
        p.wait("tensor", "ve", ve)

        def ln_stats(tile_of):
            aps = [tile_of(j) for j in range(8)]
            p.wait("tensor", "ve", p.cnt["ve"])
            p.wait("scalar", "ve", p.cnt["ve"])
            for j in range(8):
                p.emit("tensor", lambda t, j=j, ap=aps[j]: t.matmul(
                    p_st1, ones_col[:], ap,
                    start=(j == 0), stop=(j == 7)), inc="te")
            last_sq_mm = None
            for j in range(8):
                if last_sq_mm is not None:
                    p.wait("scalar", "te", last_sq_mm)
                se = p.emit("scalar", lambda s, ap=aps[j]: s.activation(
                    out=scr2[:], in_=ap, func=AF.Square), inc="se")
                p.wait("tensor", "se", se)
                last_sq_mm = p.emit("tensor", lambda t, j=j: t.matmul(
                    p_st2, ones_col[:], scr2[:],
                    start=(j == 0), stop=(j == 7)), inc="te")
            p.wait("vector", "te", p.cnt["te"])
            p.emit("vector", lambda v: v.tensor_scalar(out=mean1[:], in0=p_st1,
                                                       scalar1=1.0 / D, scalar2=None,
                                                       op0=Op.mult), inc="ve")
            p.emit("vector", lambda v: v.tensor_scalar(out=st2[:], in0=p_st2,
                                                       scalar1=1.0 / D, scalar2=None,
                                                       op0=Op.mult), inc="ve")
            p.drain("vector")
            p.emit("vector", lambda v: v.tensor_tensor(out=var1[:], in0=mean1[:], in1=mean1[:],
                                                       op=Op.mult), inc="ve")
            p.drain("vector")
            p.emit("vector", lambda v: v.tensor_tensor(out=var1[:], in0=st2[:], in1=var1[:],
                                                       op=Op.subtract), inc="ve")
            p.emit("vector", lambda v: v.tensor_scalar(out=nmean1[:], in0=mean1[:],
                                                       scalar1=-1.0, scalar2=None,
                                                       op0=Op.mult), inc="ve")
            ve = p.cnt["ve"]
            p.drain("vector")
            p.wait("scalar", "ve", ve)
            p.emit("scalar", lambda s: s.activation(out=rstd1[:], in_=var1[:], func=AF.Ln,
                                                    bias=eps_t[:, 0:1]), inc="se")
            p.drain("scalar")
            se = p.emit("scalar", lambda s: s.activation(out=rstd1[:], in_=rstd1[:],
                                                         func=AF.Exp, scale=-0.5), inc="se")
            p.wait("tensor", "se", se)
            p.wait("tensor", "ve", p.cnt["ve"])
            for src, dst in ((nmean1, nm_bc), (rstd1, rs_bc)):
                te = p.emit("tensor", lambda t, src=src: t.matmul(
                    p_bc[:], ones_f[:], src[:], start=True, stop=True), inc="te")
                p.wait("vector", "te", te)
                ve = p.emit("vector", lambda v, dst=dst: v.tensor_copy(dst[:], p_bc[:]), inc="ve")
                p.drain("vector")
                p.wait("tensor", "ve", ve)

        last_store = 0
        w1_prev = None
        w2_prev = None
        prev_gelu = {0: None, 1: None, 2: None}
        prev_add = {0: None, 1: None}
        for q in range(n_q):
            tq = q * Q
            p.wait("sync", "ve", p.cnt["ve"])
            p.wait("sync", "te", p.cnt["te"])
            p.dma("sync", xq.ap().rearrange("p (j t) -> p j t", j=8),
                  xT[:, tq:tq + Q].rearrange("(j p) t -> p j t", j=8))
            ldq = p.cnt["ld"]
            p.wait("vector", "ld", ldq)
            p.wait("tensor", "ld", ldq)
            p.wait("scalar", "ld", ldq)
            p.wait("vector", "te", p.cnt["te"])

            for ch in range(n_ch):
                co = ch * CH

                def xq_tile(j, co=co):
                    return xq[:, j * Q + co: j * Q + co + CH]
                if pre_ln:
                    ln_stats(xq_tile)
                    p.emit("vector", lambda v: v.tensor_tensor(out=nmr1[:], in0=nmean1[:],
                                                               in1=rstd1[:], op=Op.mult), inc="ve")
                    ve = p.cnt["ve"]
                    p.drain("vector")
                    p.wait("tensor", "ve", ve)
                    te = p.emit("tensor", lambda t: t.matmul(p_bc[:], ones_f[:], nmr1[:],
                                                             start=True, stop=True), inc="te")
                    p.wait("vector", "te", te)
                    ve = p.emit("vector", lambda v: v.tensor_copy(nm_bc[:], p_bc[:]), inc="ve")
                    p.drain("vector")
                    p.wait("tensor", "ve", ve)
                p.wait("tensor", "ld2", ld20)
                rbank = [p_r, p_bc]
                for j in range(8):
                    if pre_ln:
                        p.emit("vector", lambda v, ap=xq_tile(j): v.tensor_tensor(
                            out=scr2[:], in0=ap, in1=rs_bc[:], op=Op.mult), inc="ve")
                        p.drain("vector")
                        p.emit("vector", lambda v: v.tensor_tensor(
                            out=scr2[:], in0=scr2[:], in1=nm_bc[:], op=Op.add), inc="ve")
                        p.drain("vector")
                        p.emit("vector", lambda v, j=j: v.tensor_scalar(
                            out=scr2[:], in0=scr2[:],
                            scalar1=gs[:, j:j + 1], scalar2=bs[:, j:j + 1],
                            op0=Op.mult, op1=Op.add), inc="ve")
                        p.drain("vector")
                    else:
                        p.emit("vector", lambda v, ap=xq_tile(j): v.tensor_copy(
                            scr2[:], ap), inc="ve")
                        p.drain("vector")
                    ve_z = p.emit("vector", lambda v, j=j, co=co: v.tensor_copy(
                        zb[:, j * Q + co: j * Q + co + CH], scr2[:]), inc="ve")
                    p.drain("vector")
                    p.wait("tensor", "ve", ve_z)
                    for sub in range(4):
                        bk = rbank[sub % 2]
                        te = p.emit("tensor", lambda t, j=j, sub=sub, bk=bk: t.matmul(
                            bk[:, 0:E], scr2[:, sub * 128:(sub + 1) * 128],
                            rWs[:, j * E:(j + 1) * E],
                            start=True, stop=True), inc="te")
                        p.wait("vector", "te", te)
                        if j == 0:
                            ve = p.emit("vector", lambda v, sub=sub, bk=bk: v.tensor_copy(
                                lgacc[:, sub * E:(sub + 1) * E], bk[:, 0:E]), inc="ve")
                        else:
                            ve = p.emit("vector", lambda v, sub=sub, bk=bk: v.tensor_tensor(
                                out=lgacc[:, sub * E:(sub + 1) * E],
                                in0=lgacc[:, sub * E:(sub + 1) * E],
                                in1=bk[:, 0:E], op=Op.add), inc="ve")
                        p.drain("vector")
                        if sub >= 1:
                            p.wait("tensor", "ve", ve - 1)
                    p.wait("tensor", "ve", p.cnt["ve"])

                for sc_ in range(CH // 128):
                    so = co + sc_ * 128
                    p.emit("vector", lambda v, sc_=sc_: v.tensor_tensor(
                        out=lg[:], in0=lgacc[:, sc_ * E:(sc_ + 1) * E],
                        in1=rb_bc[:], op=Op.add), inc="ve")
                    ve = p.cnt["ve"]
                    p.drain("vector")
                    p.emit("vector", lambda v: v.reduce_max(m1[:], lg[:], axis=AX.X), inc="ve")
                    p.drain("vector")
                    p.emit("vector", lambda v: v.tensor_scalar(out=nm1[:], in0=m1[:],
                                                               scalar1=-1.0, scalar2=None,
                                                               op0=Op.mult), inc="ve")
                    p.emit("vector", lambda v: v.scalar_tensor_tensor(
                        out=l2[:], in0=lg[:], scalar=m1[:, 0:1], in1=lg[:],
                        op0=Op.is_ge, op1=Op.bypass), inc="ve")
                    p.drain("vector")
                    p.emit("vector", lambda v: v.scalar_tensor_tensor(
                        out=l2[:], in0=l2[:], scalar=-1e30, in1=lg[:],
                        op0=Op.mult, op1=Op.add), inc="ve")
                    p.drain("vector")
                    p.emit("vector", lambda v: v.reduce_max(m2[:], l2[:], axis=AX.X), inc="ve")
                    p.drain("vector")
                    p.emit("vector", lambda v: v.tensor_scalar(out=d21[:], in0=m2[:],
                                                               scalar1=m1[:, 0:1], scalar2=None,
                                                               op0=Op.subtract), inc="ve")
                    p.emit("vector", lambda v: v.tensor_scalar(out=ind[:], in0=lg[:],
                                                               scalar1=m2[:, 0:1], scalar2=None,
                                                               op0=Op.is_ge), inc="ve")
                    ve = p.cnt["ve"]
                    p.drain("vector")
                    p.wait("scalar", "ve", ve)
                    p.emit("scalar", lambda s: s.activation(out=sexp[:], in_=lg[:], func=AF.Exp,
                                                            bias=nm1[:, 0:1]), inc="se")
                    p.drain("scalar")
                    se = p.emit("scalar", lambda s: s.activation(out=em[:], in_=d21[:],
                                                                 func=AF.Exp), inc="se")
                    p.wait("vector", "se", se)
                    p.emit("vector", lambda v: v.tensor_scalar(out=den[:], in0=em[:],
                                                               scalar1=1.0, scalar2=None,
                                                               op0=Op.add), inc="ve")
                    p.drain("vector")
                    p.emit("vector", lambda v: v.reciprocal(rden[:], den[:]), inc="ve")
                    p.drain("vector")
                    p.emit("vector", lambda v: v.tensor_tensor(out=wv[:], in0=sexp[:], in1=ind[:],
                                                               op=Op.mult), inc="ve")
                    p.drain("vector")
                    ve = p.emit("vector", lambda v: v.tensor_scalar(out=wv[:], in0=wv[:],
                                                                    scalar1=rden[:, 0:1],
                                                                    scalar2=None, op0=Op.mult), inc="ve")
                    p.drain("vector")
                    p.wait("tensor", "ve", ve)
                    te = p.emit("tensor", lambda t: t.transpose(p_r[0:E, 0:128], wv[:, 0:E],
                                                                ident_f[:]), inc="te")
                    p.wait("vector", "te", te)
                    ve = p.emit("vector", lambda v, so=so: v.tensor_copy(
                        wT_b[0:E, so:so + 128], p_r[0:E, 0:128]), inc="ve")
                    p.drain("vector")
                    p.wait("tensor", "ve", ve)


            for ch in range(n_ch):
                co = ch * CH
                for h in range(8):
                    p.wait("tensor", "ve", p.cnt["ve"])
                    p.emit("tensor", lambda t, h=h, co=co: t.matmul(
                        p_mix[h % 2][:], b2s[:, h * 128:(h + 1) * 128],
                        wT_b[0:E, co:co + CH], start=True, stop=True), inc="te")
                    p.wait("vector", "te", p.cnt["te"])
                    p.emit("vector", lambda v, h=h, co=co: v.tensor_copy(
                        mixT[:, h * Q + co:h * Q + co + CH], p_mix[h % 2][:]), inc="ve")
            p.wait("tensor", "ve", p.cnt["ve"])


            for ex in range(E):
                if w1_prev is not None:
                    p.wait("gpsimd", "te", w1_prev)
                p.dma("gpsimd", w1b.ap().rearrange("p (j f) -> p j f", j=8),
                      W1[ex].rearrange("(j p) f -> p j f", j=8), inc="ld2")
                w1_ld = p.cnt["ld2"]
                if w2_prev is not None:
                    p.wait("gpsimd", "te", w2_prev)
                p.dma("gpsimd", w2b.ap().rearrange("p (j d) -> p j d", j=NF),
                      W2[ex].rearrange("(j p) d -> p j d", j=NF), inc="ld2")
                w2_ld = p.cnt["ld2"]
                p.wait("tensor", "ld2", w1_ld)
                for ch in range(n_ch):
                    co = ch * CH
                    p.wait("tensor", "ve", p.cnt["ve"])
                    te = p.emit("tensor", lambda t, ex=ex, co=co: t.matmul(
                        p_bc[:], sel_f[0:E, ex * 128:(ex + 1) * 128],
                        wT_b[0:E, co:co + CH],
                        start=True, stop=True), inc="te")
                    p.wait("vector", "te", te)
                    ve = p.emit("vector", lambda v: v.tensor_copy(w_bc[:], p_bc[:]), inc="ve")
                    p.drain("vector")
                    p.wait("tensor", "ve", ve)
                    mult_of = {}
                    for f in range(NF):
                        if prev_gelu[f % 3] is not None:
                            p.wait("tensor", "se", prev_gelu[f % 3])
                        for j in range(8):
                            p.emit("tensor", lambda t, f=f, j=j, co=co: t.matmul(
                                p_hid[f % 3][:],
                                w1b[:, j * HF + f * 128: j * HF + f * 128 + 128],
                                zb[:, j * Q + co: j * Q + co + CH],
                                start=(j == 0), stop=(j == 7)), inc="te")
                        p.wait("scalar", "te", p.cnt["te"])
                        if f >= 2:
                            p.wait("scalar", "ve", mult_of[f - 2])
                        se = p.emit("scalar", lambda s, f=f, ex=ex: s.activation(
                            out=scr_g[f % 2][:], in_=p_hid[f % 3][:], func=AF.Gelu,
                            bias=b1s[:, ex * NF + f: ex * NF + f + 1]), inc="se")
                        prev_gelu[f % 3] = se
                        p.wait("vector", "se", se)
                        mult_of[f] = p.emit("vector", lambda v, f=f: v.tensor_tensor(
                            out=hid[:, f * CH:(f + 1) * CH], in0=scr_g[f % 2][:],
                            in1=w_bc[:], op=Op.mult), inc="ve")
                    p.wait("tensor", "ve", p.cnt["ve"])
                    if ch == n_ch - 1:
                        w1_prev = p.cnt["te"]
                    p.wait("tensor", "ld2", w2_ld)
                    for h in range(8):
                        if prev_add[h % 2] is not None:
                            p.wait("tensor", "ve", prev_add[h % 2])
                        for f in range(NF):
                            p.emit("tensor", lambda t, h=h, f=f: t.matmul(
                                p_mix[h % 2][:],
                                w2b[:, f * D + h * 128: f * D + h * 128 + 128],
                                hid[:, f * CH:(f + 1) * CH],
                                start=(f == 0), stop=(f == NF - 1)), inc="te")
                        p.wait("vector", "te", p.cnt["te"])
                        prev_add[h % 2] = p.emit("vector", lambda v, h=h, co=co: v.tensor_tensor(
                            out=mixT[:, h * Q + co:h * Q + co + CH],
                            in0=mixT[:, h * Q + co:h * Q + co + CH],
                            in1=p_mix[h % 2][:], op=Op.add), inc="ve")
                    p.wait("tensor", "ve", p.cnt["ve"])
                w2_prev = p.cnt["te"]


            p.drain("vector")
            for ch in range(n_ch):
                co = ch * CH
                for j in range(8):
                    p.emit("vector", lambda v, j=j, co=co: v.scalar_tensor_tensor(
                        out=mixT[:, j * Q + co:j * Q + co + CH],
                        in0=xq[:, j * Q + co:j * Q + co + CH], scalar=2.0,
                        in1=mixT[:, j * Q + co:j * Q + co + CH],
                        op0=Op.mult, op1=Op.add), inc="ve")
                p.drain("vector")
                p.wait("tensor", "ve", p.cnt["ve"])

                def mix_tile(j, co=co):
                    return mixT[:, j * Q + co: j * Q + co + CH]
                ln_stats(mix_tile)
                st_of = {}
                for j in range(8):
                    if j >= 2:
                        p.wait("vector", "st", st_of[j - 2])
                    elif last_store:
                        p.wait("vector", "st", last_store)
                    p.emit("vector", lambda v, ap=mix_tile(j): v.tensor_tensor(
                        out=scr2[:], in0=ap, in1=nm_bc[:], op=Op.add), inc="ve")
                    p.drain("vector")
                    ve = p.emit("vector", lambda v, j=j: v.tensor_tensor(
                        out=out2[j % 2][:], in0=scr2[:], in1=rs_bc[:],
                        op=Op.mult), inc="ve")
                    p.drain("vector")
                    p.wait("sync", "ve", ve)
                    st_of[j] = p.dma(
                        "sync", outT[j * 128:(j + 1) * 128, tq + co:tq + co + CH],
                        out2[j % 2][:], inc="st")
                last_store = st_of[7]

    return


def emit_min(nc, p, T_own, x2T, ipT, convw_r, convb_r, xpT, dtpT, dtb_r,
             normw_r, uT, deltaT, xdbc, zT):

    n_ch = T_own // CH
    with ExitStack() as st_:
        e = st_.enter_context
        ipb = e(nc.sbuf_tensor("ipb", [128, 8 * 2 * DI], BF16))
        xpb = e(nc.sbuf_tensor("xpb", [128, 16 * 96], BF16))
        dtpb = e(nc.sbuf_tensor("dtpb", [DTR, DI], BF16))
        cw = e(nc.sbuf_tensor("cw", [128, 16 * 4], F32))
        cb = e(nc.sbuf_tensor("cb", [128, 16], F32))
        dtb = e(nc.sbuf_tensor("dtb", [128, 16], F32))
        nw = e(nc.sbuf_tensor("nw", [128, 8], F32))
        xq = e(nc.sbuf_tensor("xq", [128, 8 * CH], F32))
        xh3 = e(nc.sbuf_tensor("xh3", [128, 8 * 3], F32))
        xnb = e(nc.sbuf_tensor("xnb", [128, 8 * (CH + 3)], BF16))
        scr2 = e(nc.sbuf_tensor("scr2", [128, CH], F32))
        scr3 = e(nc.sbuf_tensor("scr3", [128, 3], F32))
        ones_col = e(nc.sbuf_tensor("ones_col", [128, 1], F32))
        eps_t = e(nc.sbuf_tensor("eps_t", [1, 1], F32))
        ms1 = e(nc.sbuf_tensor("ms1", [1, CH], F32))
        rs_bc = e(nc.sbuf_tensor("rs_bc", [128, CH], F32))
        ones_f = e(nc.sbuf_tensor("ones_f", [1, 128], F32))
        xsb = e(nc.sbuf_tensor("xsb", [128, CH + 3], F32))
        uacc = e(nc.sbuf_tensor("uacc", [128, CH], F32))
        uf2 = [e(nc.sbuf_tensor(f"uf{i}", [128, CH], F32)) for i in range(2)]
        ub = e(nc.sbuf_tensor("ub", [128, 16 * CH], BF16))
        dtrb = e(nc.sbuf_tensor("dtrb", [DTR, CH], BF16))
        dxb = e(nc.sbuf_tensor("dxb", [96, CH], F32))
        dscr = {n: e(nc.sbuf_tensor(f"dscr_{n}", [128, CH], F32)) for n in
                ("m", "a", "l", "dl")}
        p_xz = e(nc.psum_tensor("p_xz", [128, CH], F32))
        p_h3 = e(nc.psum_tensor("p_h3", [128, 8], F32))
        p_st = e(nc.psum_tensor("p_st", [1, CH], F32))
        p_bc = e(nc.psum_tensor("p_bc", [128, CH], F32))
        p_dbc = e(nc.psum_tensor("p_dbc", [96, CH], F32))
        p_dt = e(nc.psum_tensor("p_dt", [128, CH], F32))

        p.dma("gpsimd", ipb.ap().rearrange("p (j k) -> p j k", j=8),
              ipT.rearrange("(j p) k -> p j k", j=8), inc="ld2")
        p.dma("gpsimd", xpb.ap().rearrange("p (j k) -> p j k", j=16),
              xpT.rearrange("(j p) k -> p j k", j=16), inc="ld2")
        p.dma("gpsimd", dtpb[:], dtpT, inc="ld2")
        ld2_w = p.cnt["ld2"]
        p.dma("sync", cw[:], convw_r)
        p.dma("sync", cb[:], convb_r)
        p.dma("sync", dtb[:], dtb_r)
        p.dma("sync", nw[:], normw_r)
        ld0 = p.cnt["ld"]
        p.emit("vector", lambda v: v.memset(ones_col[:], 1.0), inc="ve")
        p.emit("vector", lambda v: v.memset(ones_f[:], 1.0), inc="ve")
        p.emit("vector", lambda v: v.memset(eps_t[:], RMS_EPS), inc="ve")
        ve0 = p.cnt["ve"]
        p.wait("tensor", "ve", ve0)
        p.wait("tensor", "ld", ld0)
        p.wait("tensor", "ld2", ld2_w)
        p.wait("scalar", "ld", ld0)
        p.wait("vector", "ld", ld0)

        st_guard = {}
        p.dma("sync", xh3.ap().rearrange("p (j t) -> p j t", j=8),
              x2T[:, 0:3].rearrange("(j p) t -> p j t", j=8))
        ldh = p.cnt["ld"]
        p.wait("vector", "ld", ldh)
        p.wait("tensor", "ld", ldh)
        p.wait("scalar", "ld", ldh)
        last_sq = None
        for j in range(8):
            if last_sq is not None:
                p.wait("scalar", "te", last_sq)
            se = p.emit("scalar", lambda s, j=j: s.activation(
                out=scr3[:], in_=xh3[:, j * 3:(j + 1) * 3], func=AF.Square), inc="se")
            p.wait("tensor", "se", se)
            last_sq = p.emit("tensor", lambda t, j=j: t.matmul(
                p_st[:, 0:3], ones_col[:], scr3[:],
                start=(j == 0), stop=(j == 7)), inc="te")
        p.wait("vector", "te", p.cnt["te"])
        p.emit("vector", lambda v: v.tensor_scalar(out=ms1[:, 0:3], in0=p_st[:, 0:3],
                                                   scalar1=1.0 / D, scalar2=None,
                                                   op0=Op.mult), inc="ve")
        ve = p.cnt["ve"]
        p.drain("vector")
        p.wait("scalar", "ve", ve)
        p.emit("scalar", lambda s: s.activation(out=ms1[:, 0:3], in_=ms1[:, 0:3], func=AF.Ln,
                                                bias=eps_t[:, 0:1]), inc="se")
        p.drain("scalar")
        se = p.emit("scalar", lambda s: s.activation(out=ms1[:, 0:3], in_=ms1[:, 0:3],
                                                     func=AF.Exp, scale=-0.5), inc="se")
        p.wait("tensor", "se", se)
        te = p.emit("tensor", lambda t: t.matmul(p_bc[:, 0:3], ones_f[:], ms1[:, 0:3],
                                                 start=True, stop=True), inc="te")
        p.wait("vector", "te", te)
        ve = p.emit("vector", lambda v: v.tensor_copy(rs_bc[:, 0:3], p_bc[:, 0:3]), inc="ve")
        p.drain("vector")
        p.wait("tensor", "ve", ve)
        for j in range(8):
            p.emit("vector", lambda v, j=j: v.tensor_tensor(
                out=scr3[:], in0=xh3[:, j * 3:(j + 1) * 3],
                in1=rs_bc[:, 0:3], op=Op.mult), inc="ve")
            p.drain("vector")
            p.emit("vector", lambda v, j=j: v.tensor_scalar(
                out=xnb[:, j * (CH + 3): j * (CH + 3) + 3], in0=scr3[:],
                scalar1=nw[:, j:j + 1], scalar2=None, op0=Op.mult), inc="ve")
            p.drain("vector")

        halo_guard = [None]
        for c in range(n_ch):
            t0 = c * CH
            if c >= 2:
                p.wait("sync", "ve", halo_guard[0])
            p.dma("sync", xq.ap().rearrange("p (j t) -> p j t", j=8),
                  x2T[:, 3 + t0:3 + t0 + CH].rearrange("(j p) t -> p j t", j=8))
            ldc = p.cnt["ld"]
            p.wait("vector", "ld", ldc)
            p.wait("tensor", "ld", ldc)
            p.wait("scalar", "ld", ldc)

            if c > 0:
                p.wait("vector", "te", p.cnt["te"])
                for j in range(8):
                    p.emit("vector", lambda v, j=j: v.tensor_copy(
                        xnb[:, j * (CH + 3): j * (CH + 3) + 3],
                        xnb[:, j * (CH + 3) + CH: j * (CH + 3) + CH + 3]), inc="ve")
                p.drain("vector")

            p.wait("tensor", "ve", p.cnt["ve"])
            p.wait("scalar", "ve", p.cnt["ve"])
            last_sq = None
            for j in range(8):
                if last_sq is not None:
                    p.wait("scalar", "te", last_sq)
                se = p.emit("scalar", lambda s, j=j: s.activation(
                    out=scr2[:], in_=xq[:, j * CH:(j + 1) * CH],
                    func=AF.Square), inc="se")
                p.wait("tensor", "se", se)
                last_sq = p.emit("tensor", lambda t, j=j: t.matmul(
                    p_st[:], ones_col[:], scr2[:],
                    start=(j == 0), stop=(j == 7)), inc="te")
            p.wait("vector", "te", p.cnt["te"])
            p.emit("vector", lambda v: v.tensor_scalar(out=ms1[:], in0=p_st[:],
                                                       scalar1=1.0 / D, scalar2=None,
                                                       op0=Op.mult), inc="ve")
            ve = p.cnt["ve"]
            p.drain("vector")
            p.wait("scalar", "ve", ve)
            p.emit("scalar", lambda s: s.activation(out=ms1[:], in_=ms1[:], func=AF.Ln,
                                                    bias=eps_t[:, 0:1]), inc="se")
            p.drain("scalar")
            se = p.emit("scalar", lambda s: s.activation(out=ms1[:], in_=ms1[:],
                                                         func=AF.Exp, scale=-0.5), inc="se")
            p.wait("tensor", "se", se)
            te = p.emit("tensor", lambda t: t.matmul(p_bc[:], ones_f[:], ms1[:],
                                                     start=True, stop=True), inc="te")
            p.wait("vector", "te", te)
            ve = p.emit("vector", lambda v: v.tensor_copy(rs_bc[:], p_bc[:]), inc="ve")
            p.drain("vector")
            p.wait("tensor", "ve", ve)
            for j in range(8):
                p.emit("vector", lambda v, j=j: v.tensor_tensor(
                    out=scr2[:], in0=xq[:, j * CH:(j + 1) * CH],
                    in1=rs_bc[:], op=Op.mult), inc="ve")
                p.drain("vector")
                p.emit("vector", lambda v, j=j: v.tensor_scalar(
                    out=xnb[:, j * (CH + 3) + 3:(j + 1) * (CH + 3)], in0=scr2[:],
                    scalar1=nw[:, j:j + 1], scalar2=None, op0=Op.mult), inc="ve")
                p.drain("vector")
            ve_xn = p.cnt["ve"]
            halo_guard[0] = ve_xn
            p.wait("tensor", "ve", ve_xn)

            for jt in range(32):
                is_xs = jt < 16
                it = jt if is_xs else jt - 16
                p.wait("tensor", "ve", p.cnt["ve"])
                for j in range(8):
                    p.emit("tensor", lambda t, j=j, jt=jt: t.matmul(
                        p_xz[:],
                        ipb[:, j * 2 * DI + jt * 128: j * 2 * DI + jt * 128 + 128],
                        xnb[:, j * (CH + 3) + 3: j * (CH + 3) + 3 + CH],
                        start=(j == 0), stop=(j == 7)), inc="te")
                if is_xs:
                    for j in range(8):
                        p.emit("tensor", lambda t, j=j, jt=jt: t.matmul(
                            p_h3[:, 0:3],
                            ipb[:, j * 2 * DI + jt * 128: j * 2 * DI + jt * 128 + 128],
                            xnb[:, j * (CH + 3): j * (CH + 3) + 3],
                            start=(j == 0), stop=(j == 7)), inc="te")
                te = p.cnt["te"]
                p.wait("vector", "te", te)
                if not is_xs:
                    uf = uf2[jt % 2]
                    if ("uf", jt % 2) in st_guard:
                        p.wait("vector", "st", st_guard[("uf", jt % 2)])
                    ve = p.emit("vector", lambda v, uf=uf: v.tensor_copy(uf[:], p_xz[:]), inc="ve")
                    p.drain("vector")
                    p.wait("sync", "ve", ve)
                    st_guard[("uf", jt % 2)] = p.dma(
                        "sync", zT[it * 128:(it + 1) * 128, t0:t0 + CH], uf[:], inc="st")
                    p.wait("tensor", "ve", ve)
                    continue
                p.emit("vector", lambda v: v.tensor_copy(xsb[:, 0:3], p_h3[:, 0:3]), inc="ve")
                ve = p.emit("vector", lambda v: v.tensor_copy(xsb[:, 3:CH + 3], p_xz[:]), inc="ve")
                p.drain("vector")
                p.wait("tensor", "ve", ve)
                p.emit("vector", lambda v, it=it: v.tensor_scalar(
                    out=uacc[:], in0=xsb[:, 0:CH], scalar1=cw[:, it * 4:it * 4 + 1],
                    scalar2=None, op0=Op.mult), inc="ve")
                p.drain("vector")
                for k in range(1, 4):
                    p.emit("vector", lambda v, it=it, k=k: v.scalar_tensor_tensor(
                        out=uacc[:], in0=xsb[:, k:k + CH], scalar=cw[:, it * 4 + k:it * 4 + k + 1],
                        in1=uacc[:], op0=Op.mult, op1=Op.add), inc="ve")
                    p.drain("vector")
                ve = p.cnt["ve"]
                p.wait("scalar", "ve", ve)
                uf = uf2[jt % 2]
                if ("uf", jt % 2) in st_guard:
                    p.wait("scalar", "st", st_guard[("uf", jt % 2)])
                se = p.emit("scalar", lambda s, it=it, uf=uf: s.activation(
                    out=uf[:], in_=uacc[:], func=AF.Silu, bias=cb[:, it:it + 1]), inc="se")
                p.drain("scalar")
                p.wait("vector", "se", se)
                ve = p.emit("vector", lambda v, it=it, uf=uf: v.tensor_copy(
                    ub[:, it * CH:(it + 1) * CH], uf[:]), inc="ve")
                p.drain("vector")
                p.wait("sync", "se", se)
                st_guard[("uf", jt % 2)] = p.dma(
                    "sync", uT[it * 128:(it + 1) * 128, t0:t0 + CH], uf[:], inc="st")
            p.wait("tensor", "ve", p.cnt["ve"])
            for it in range(16):
                p.emit("tensor", lambda t, it=it: t.matmul(
                    p_dbc[:], xpb[:, it * 96:(it + 1) * 96],
                    ub[:, it * CH:(it + 1) * CH],
                    start=(it == 0), stop=(it == 15)), inc="te")
            te = p.cnt["te"]
            p.wait("vector", "te", te)
            p.emit("vector", lambda v: v.tensor_copy(dxb[:], p_dbc[:]), inc="ve")
            p.drain("vector")
            ve = p.emit("vector", lambda v: v.tensor_copy(dtrb[:], dxb[0:DTR, :]), inc="ve")
            p.drain("vector")
            p.wait("sync", "ve", ve)
            if "dbc" in st_guard:
                p.wait("sync", "st", st_guard["dbc"])
            st_guard["dbc"] = p.dma("sync", xdbc[:, t0:t0 + CH], dxb[:], inc="st")
            p.wait("tensor", "ve", ve)
            for it in range(16):
                p.emit("tensor", lambda t, it=it: t.matmul(
                    p_dt[:], dtpb[:, it * 128:(it + 1) * 128], dtrb[:],
                    start=True, stop=True), inc="te")
                te = p.cnt["te"]
                p.wait("scalar", "te", te)
                p.emit("scalar", lambda s, it=it: s.activation(
                    out=dscr["m"][:], in_=p_dt[:], func=AF.Relu,
                    bias=dtb[:, it:it + 1]), inc="se")
                p.drain("scalar")
                p.emit("scalar", lambda s, it=it: s.activation(
                    out=dscr["a"][:], in_=p_dt[:], func=AF.Abs,
                    bias=dtb[:, it:it + 1]), inc="se")
                p.drain("scalar")
                p.emit("scalar", lambda s: s.activation(
                    out=dscr["a"][:], in_=dscr["a"][:], func=AF.Exp, scale=-1.0), inc="se")
                p.drain("scalar")
                se = p.emit("scalar", lambda s: s.activation(
                    out=dscr["l"][:], in_=dscr["a"][:], func=AF.Ln,
                    bias=ones_col[:, 0:1]), inc="se")
                p.drain("scalar")
                p.wait("vector", "se", se)
                if "dl" in st_guard:
                    p.wait("vector", "st", st_guard["dl"])
                ve = p.emit("vector", lambda v: v.tensor_tensor(
                    out=dscr["dl"][:], in0=dscr["m"][:], in1=dscr["l"][:], op=Op.add), inc="ve")
                p.drain("vector")
                p.wait("tensor", "se", se)
                p.wait("sync", "ve", ve)
                st_guard["dl"] = p.dma(
                    "sync", deltaT[it * 128:(it + 1) * 128, t0:t0 + CH],
                    dscr["dl"][:], inc="st")
    return




def emit_mout(nc, p, T_own, y2T, x2T, opT, x3T):
    n_ch = T_own // CH
    with ExitStack() as st_:
        e = st_.enter_context
        opb = e(nc.sbuf_tensor("opb", [128, 16 * D], BF16))
        yq = e(nc.sbuf_tensor("yq", [128, 16 * CH], F32))
        yb = e(nc.sbuf_tensor("yb", [128, 16 * CH], BF16))
        xq = e(nc.sbuf_tensor("xq", [128, 8 * CH], F32))
        mix = e(nc.sbuf_tensor("mix", [128, 8 * CH], F32))
        scr2 = e(nc.sbuf_tensor("scr2", [128, CH], F32))
        ones_col = e(nc.sbuf_tensor("ones_col", [128, 1], F32))
        ones_f = e(nc.sbuf_tensor("ones_f", [1, 128], F32))
        eps_t = e(nc.sbuf_tensor("eps_t", [1, 1], F32))
        st2 = e(nc.sbuf_tensor("st2", [1, CH], F32))
        mean1 = e(nc.sbuf_tensor("mean1", [1, CH], F32))
        nmean1 = e(nc.sbuf_tensor("nmean1", [1, CH], F32))
        var1 = e(nc.sbuf_tensor("var1", [1, CH], F32))
        rstd1 = e(nc.sbuf_tensor("rstd1", [1, CH], F32))
        nm_bc = e(nc.sbuf_tensor("nm_bc", [128, CH], F32))
        rs_bc = e(nc.sbuf_tensor("rs_bc", [128, CH], F32))
        out2 = [e(nc.sbuf_tensor(f"out2_{i}", [128, CH], F32)) for i in range(2)]
        p_st1 = e(nc.psum_tensor("p_st1", [1, CH], F32))
        p_st2 = e(nc.psum_tensor("p_st2", [1, CH], F32))
        p_bc = e(nc.psum_tensor("p_bc", [128, CH], F32))
        p_o = [e(nc.psum_tensor(f"p_o{i}", [128, CH], F32)) for i in range(2)]

        p.dma("gpsimd", opb.ap().rearrange("p (j d) -> p j d", j=16),
              opT.rearrange("(j p) d -> p j d", j=16), inc="ld2")
        ld2w = p.cnt["ld2"]
        p.emit("vector", lambda v: v.memset(ones_col[:], 1.0), inc="ve")
        p.emit("vector", lambda v: v.memset(ones_f[:], 1.0), inc="ve")
        p.emit("vector", lambda v: v.memset(eps_t[:], LN_EPS), inc="ve")
        ve0 = p.cnt["ve"]
        p.wait("tensor", "ve", ve0)
        p.wait("tensor", "ld2", ld2w)

        def ln_stats(aps):
            p.wait("tensor", "ve", p.cnt["ve"])
            p.wait("scalar", "ve", p.cnt["ve"])
            for j in range(8):
                p.emit("tensor", lambda t, ap=aps[j], j=j: t.matmul(
                    p_st1[:], ones_col[:], ap, start=(j == 0), stop=(j == 7)), inc="te")
            last_sq = None
            for j in range(8):
                if last_sq is not None:
                    p.wait("scalar", "te", last_sq)
                se = p.emit("scalar", lambda s, ap=aps[j]: s.activation(
                    out=scr2[:], in_=ap, func=AF.Square), inc="se")
                p.wait("tensor", "se", se)
                last_sq = p.emit("tensor", lambda t, j=j: t.matmul(
                    p_st2[:], ones_col[:], scr2[:], start=(j == 0), stop=(j == 7)), inc="te")
            p.wait("vector", "te", p.cnt["te"])
            p.emit("vector", lambda v: v.tensor_scalar(out=mean1[:], in0=p_st1[:],
                                                       scalar1=1.0 / D, scalar2=None,
                                                       op0=Op.mult), inc="ve")
            p.emit("vector", lambda v: v.tensor_scalar(out=st2[:], in0=p_st2[:],
                                                       scalar1=1.0 / D, scalar2=None,
                                                       op0=Op.mult), inc="ve")
            p.drain("vector")
            p.emit("vector", lambda v: v.tensor_tensor(out=var1[:], in0=mean1[:], in1=mean1[:],
                                                       op=Op.mult), inc="ve")
            p.drain("vector")
            p.emit("vector", lambda v: v.tensor_tensor(out=var1[:], in0=st2[:], in1=var1[:],
                                                       op=Op.subtract), inc="ve")
            p.emit("vector", lambda v: v.tensor_scalar(out=nmean1[:], in0=mean1[:],
                                                       scalar1=-1.0, scalar2=None,
                                                       op0=Op.mult), inc="ve")
            ve = p.cnt["ve"]
            p.drain("vector")
            p.wait("scalar", "ve", ve)
            p.emit("scalar", lambda s: s.activation(out=rstd1[:], in_=var1[:], func=AF.Ln,
                                                    bias=eps_t[:, 0:1]), inc="se")
            p.drain("scalar")
            se = p.emit("scalar", lambda s: s.activation(out=rstd1[:], in_=rstd1[:],
                                                         func=AF.Exp, scale=-0.5), inc="se")
            p.wait("tensor", "se", se)
            p.wait("tensor", "ve", p.cnt["ve"])
            for src_, dst in ((nmean1, nm_bc), (rstd1, rs_bc)):
                te = p.emit("tensor", lambda t, src_=src_: t.matmul(
                    p_bc[:], ones_f[:], src_[:], start=True, stop=True), inc="te")
                p.wait("vector", "te", te)
                ve = p.emit("vector", lambda v, dst=dst: v.tensor_copy(dst[:], p_bc[:]), inc="ve")
                p.drain("vector")
                p.wait("tensor", "ve", ve)

        last_store = 0
        for c in range(n_ch):
            t0 = c * CH
            p.wait("sync", "ve", p.cnt["ve"])
            p.wait("sync", "te", p.cnt["te"])
            p.dma("sync", yq.ap().rearrange("p (j t) -> p j t", j=16),
                  y2T[:, t0:t0 + CH].rearrange("(j p) t -> p j t", j=16))
            p.dma("sync", xq.ap().rearrange("p (j t) -> p j t", j=8),
                  x2T[:, t0:t0 + CH].rearrange("(j p) t -> p j t", j=8))
            ldc = p.cnt["ld"]
            p.wait("vector", "ld", ldc)
            for it in range(16):
                p.emit("vector", lambda v, it=it: v.tensor_copy(
                    yb[:, it * CH:(it + 1) * CH], yq[:, it * CH:(it + 1) * CH]), inc="ve")
            p.drain("vector")
            p.wait("tensor", "ve", p.cnt["ve"])
            for h in range(8):
                for it in range(16):
                    p.emit("tensor", lambda t, h=h, it=it: t.matmul(
                        p_o[h % 2][:], opb[:, it * D + h * 128: it * D + h * 128 + 128],
                        yb[:, it * CH:(it + 1) * CH],
                        start=(it == 0), stop=(it == 15)), inc="te")
                p.wait("vector", "te", p.cnt["te"])
                ve = p.emit("vector", lambda v, h=h: v.scalar_tensor_tensor(
                    out=mix[:, h * CH:(h + 1) * CH], in0=xq[:, h * CH:(h + 1) * CH],
                    scalar=2.0, in1=p_o[h % 2][:], op0=Op.mult, op1=Op.add), inc="ve")
                p.drain("vector")
                if h >= 1:
                    p.wait("tensor", "ve", ve - 1)
            p.wait("tensor", "ve", p.cnt["ve"])
            aps = [mix[:, j * CH:(j + 1) * CH] for j in range(8)]
            ln_stats(aps)
            st_of = {}
            for j in range(8):
                if j >= 2:
                    p.wait("vector", "st", st_of[j - 2])
                elif last_store:
                    p.wait("vector", "st", last_store)
                p.emit("vector", lambda v, ap=aps[j]: v.tensor_tensor(
                    out=scr2[:], in0=ap, in1=nm_bc[:], op=Op.add), inc="ve")
                p.drain("vector")
                ve = p.emit("vector", lambda v, j=j: v.tensor_tensor(
                    out=out2[j % 2][:], in0=scr2[:], in1=rs_bc[:], op=Op.mult), inc="ve")
                p.drain("vector")
                p.wait("sync", "ve", ve)
                st_of[j] = p.dma("sync", x3T[j * 128:(j + 1) * 128, t0:t0 + CH],
                                 out2[j % 2][:], inc="st")
            last_store = st_of[7]
    return


def emit_scan(nc, p, T, NIT, deltaT, uT, zT, Bx, Cx, A_r, D_r, y2T):
    n_ch = T // CH
    with ExitStack() as st_:
        e = st_.enter_context
        dl = e(nc.sbuf_tensor("dl", [128, NIT * CH], F32))
        uu = e(nc.sbuf_tensor("uu", [128, NIT * CH], F32))
        zz = e(nc.sbuf_tensor("zz", [128, NIT * CH], F32))
        bc_in = e(nc.sbuf_tensor("bcin", [DS, 2 * CH], F32))
        du = e(nc.sbuf_tensor("du", [128, NIT * CH], BF16))
        y_acc = e(nc.sbuf_tensor("y_acc", [128, NIT * CH], F32))
        A_s = e(nc.sbuf_tensor("A_s", [128, NIT * DS], F32))
        D_s = e(nc.sbuf_tensor("D_s", [128, NIT], F32))
        carry = e(nc.sbuf_tensor("carry", [128, NIT * DS], F32))
        sel16 = e(nc.sbuf_tensor("sel16", [DS, DS * 128], F32))
        B_bc = e(nc.sbuf_tensor("B_bc", [128, CH], BF16))
        C_bc = e(nc.sbuf_tensor("C_bc", [128, CH], BF16))
        dAb = [e(nc.sbuf_tensor(f"dAb{i}", [128, CH], BF16)) for i in range(2)]
        dbu = e(nc.sbuf_tensor("dbu", [128, CH], BF16))
        hh = [e(nc.sbuf_tensor(f"hh{i}", [128, CH], BF16)) for i in range(2)]
        tmp = [e(nc.sbuf_tensor(f"tmp{i}", [128, CH], BF16)) for i in range(2)]
        yy = e(nc.sbuf_tensor("yy", [128, CH], F32))
        sz = e(nc.sbuf_tensor("sz", [128, CH], F32))
        p_b = e(nc.psum_tensor("p_b", [128, CH], F32))
        p_c = e(nc.psum_tensor("p_c", [128, CH], F32))

        p.dma("sync", A_s[:], A_r)
        p.dma("sync", D_s[:], D_r)
        ld0 = p.cnt["ld"]
        p.emit("gpsimd", lambda g: g.iota(sel16[:], pattern=[[-1, DS], [0, 128]], base=0,
                                          channel_multiplier=1,
                                          allow_small_or_imprecise_dtypes=True), inc="gp")
        p.wait("vector", "gp", p.cnt["gp"])
        p.emit("vector", lambda v: v.tensor_scalar(out=sel16[:], in0=sel16[:],
                                                   scalar1=0.0, scalar2=None,
                                                   op0=Op.is_equal), inc="ve")
        p.drain("vector")
        p.emit("vector", lambda v: v.memset(carry[:], 0.0), inc="ve")
        p.wait("scalar", "ld", ld0)
        se = p.emit("scalar", lambda s: s.activation(out=A_s[:], in_=A_s[:], func=AF.Exp), inc="se")
        p.wait("vector", "se", se)
        p.emit("vector", lambda v: v.tensor_scalar(out=A_s[:], in0=A_s[:], scalar1=-1.0,
                                                   scalar2=None, op0=Op.mult), inc="ve")
        p.drain("vector")
        ve0 = p.cnt["ve"]
        p.wait("tensor", "ve", ve0)
        p.wait("scalar", "ve", ve0)

        scan_ve_of = {}
        gp_done_of = {}
        g_all = 0
        free_at = None
        last_st = 0
        last_cbc_read = None

        for c in range(n_ch):
            t0 = c * CH
            if free_at is not None:
                p.wait("sync", "ve", free_at)
                p.wait("sync", "se", p.cnt["se"])
                p.wait("sync", "gp", p.cnt["gp"])
            p.dma("sync", dl.ap().rearrange("p (j t) -> p j t", j=NIT),
                  deltaT[:, t0:t0 + CH].rearrange("(j p) t -> p j t", j=NIT))
            p.dma("sync", uu.ap().rearrange("p (j t) -> p j t", j=NIT),
                  uT[:, t0:t0 + CH].rearrange("(j p) t -> p j t", j=NIT))
            p.dma("sync", zz.ap().rearrange("p (j t) -> p j t", j=NIT),
                  zT[:, t0:t0 + CH].rearrange("(j p) t -> p j t", j=NIT))
            p.dma("sync", bc_in[:, 0:CH], Bx[:, t0:t0 + CH])
            p.dma("sync", bc_in[:, CH:2 * CH], Cx[:, t0:t0 + CH])
            ldc = p.cnt["ld"]
            p.wait("vector", "ld", ldc)
            p.wait("tensor", "ld", ldc)
            p.wait("scalar", "ld", ldc)
            p.wait("gpsimd", "ld", ldc)
            p.wait("vector", "st", last_st)
            for j in range(NIT):
                p.emit("vector", lambda v, j=j: v.tensor_tensor(
                    out=du[:, j * CH:(j + 1) * CH], in0=dl[:, j * CH:(j + 1) * CH],
                    in1=uu[:, j * CH:(j + 1) * CH], op=Op.mult), inc="ve")
            p.drain("vector")

            for s in range(DS):
                p.wait("tensor", "ve", p.cnt["ve"])
                if last_cbc_read is not None:
                    p.wait("tensor", "gp", last_cbc_read)
                p.emit("tensor", lambda t, s=s: t.matmul(
                    p_b[:], sel16[:, s * 128:(s + 1) * 128], bc_in[:, 0:CH],
                    start=True, stop=True), inc="te")
                te = p.emit("tensor", lambda t, s=s: t.matmul(
                    p_c[:], sel16[:, s * 128:(s + 1) * 128], bc_in[:, CH:2 * CH],
                    start=True, stop=True), inc="te")
                p.wait("vector", "te", te)
                if last_cbc_read is not None:
                    p.wait("vector", "gp", last_cbc_read)
                p.emit("vector", lambda v: v.tensor_copy(B_bc[:], p_b[:]), inc="ve")
                ve = p.emit("vector", lambda v: v.tensor_copy(C_bc[:], p_c[:]), inc="ve")
                p.drain("vector")
                p.wait("tensor", "ve", ve)
                p.wait("gpsimd", "ve", ve)
                for it in range(NIT):
                    g = g_all
                    bi = g % 2
                    if g - 2 in scan_ve_of:
                        p.wait("scalar", "ve", scan_ve_of[g - 2])
                    se_last = p.emit("scalar", lambda sc, it=it, s=s, bi=bi: sc.activation(
                        out=dAb[bi][:], in_=dl[:, it * CH:(it + 1) * CH], func=AF.Exp,
                        scale=A_s[:, it * DS + s: it * DS + s + 1]), inc="se")
                    p.emit("vector", lambda v, it=it: v.tensor_tensor(
                        out=dbu[:], in0=du[:, it * CH:(it + 1) * CH], in1=B_bc[:],
                        op=Op.mult), inc="ve")
                    p.drain("vector")
                    p.wait("vector", "se", se_last)
                    if g - 2 in gp_done_of:
                        p.wait("vector", "gp", gp_done_of[g - 2])
                    sv = p.emit("vector", lambda v, it=it, s=s, bi=bi: v.tensor_tensor_scan(
                        out=hh[bi][:], data0=dAb[bi][:], data1=dbu[:],
                        initial=carry[:, it * DS + s: it * DS + s + 1],
                        op0=Op.mult, op1=Op.add), inc="ve")
                    scan_ve_of[g] = sv
                    p.drain("vector")
                    p.emit("vector", lambda v, it=it, s=s, bi=bi: v.tensor_copy(
                        carry[:, it * DS + s: it * DS + s + 1], hh[bi][:, CH - 1:CH]), inc="ve")
                    tv = p.emit("vector", lambda v, bi=bi: v.tensor_tensor(
                        out=tmp[bi][:], in0=hh[bi][:], in1=C_bc[:], op=Op.mult), inc="ve")
                    scan_ve_of[g] = tv
                    p.wait("gpsimd", "ve", tv)
                    if s == 0:
                        gd = p.emit("gpsimd", lambda gq, it=it, bi=bi: gq.tensor_copy(
                            y_acc[:, it * CH:(it + 1) * CH], tmp[bi][:]), inc="gp")
                    else:
                        gd = p.emit("gpsimd", lambda gq, it=it, bi=bi: gq.tensor_tensor(
                            out=y_acc[:, it * CH:(it + 1) * CH],
                            in0=y_acc[:, it * CH:(it + 1) * CH], in1=tmp[bi][:],
                            op=Op.add), inc="gp")
                    gp_done_of[g] = gd
                    g_all += 1
                last_cbc_read = p.cnt["gp"]
            p.wait("vector", "gp", p.cnt["gp"])
            for it in range(NIT):
                p.wait("scalar", "ve", p.cnt["ve"])
                se_t = p.emit("scalar", lambda sc, it=it: sc.activation(
                    out=sz[:], in_=zz[:, it * CH:(it + 1) * CH], func=AF.Silu), inc="se")
                p.emit("vector", lambda v, it=it: v.scalar_tensor_tensor(
                    out=yy[:], in0=uu[:, it * CH:(it + 1) * CH],
                    scalar=D_s[:, it:it + 1], in1=y_acc[:, it * CH:(it + 1) * CH],
                    op0=Op.mult, op1=Op.add), inc="ve")
                p.drain("vector")
                p.wait("vector", "se", se_t)
                ve = p.emit("vector", lambda v: v.tensor_tensor(
                    out=yy[:], in0=yy[:], in1=sz[:], op=Op.mult), inc="ve")
                p.drain("vector")
                p.wait("sync", "ve", ve)
                last_st = p.dma("sync", y2T[it * 128:(it + 1) * 128, t0:t0 + CH],
                                yy[:], inc="st")
                p.wait("vector", "st", last_st)
            free_at = p.cnt["ve"]
    return


I32 = mybir.dt.int32
H = T // 2


def build_fused(debug=False):
    nc = bass.Bass("TRN2", target_bir_lowering=False)
    gi = lambda n, s: nc.dram_tensor(n, list(s), F32, kind="ExternalInput").ap()
    x = gi("x", (T, D))
    own0 = nc.dram_tensor("own0", [1, 1], I32, kind="ExternalInput").ap()
    basis = gi("basis", (D, 64))
    qc, kc, vc, oc = (gi(n, (128, 64)) for n in ("qc", "kc", "vc", "oc"))
    decay = gi("decay", (1, 1))
    oscale = gi("oscale", (1, 1))
    f0_W1 = gi("f0_W1", (8, D, 2048))
    f0_W2 = gi("f0_W2", (8, 2048, D))
    f0_b1r = gi("f0_b1r", (128, 8 * NF))
    f0_b2 = gi("f0_b2", (8, D))
    f0_rW = gi("f0_rW", (D, 8))
    f0_rb = gi("f0_rb", (1, 8))
    f0_g = gi("f0_g", (128, 8))
    f0_b = gi("f0_b", (128, 8))
    ipT = gi("ipT", (D, 2 * DI))
    convw_r = gi("convw_r", (128, 64))
    convb_r = gi("convb_r", (128, 16))
    xpT = gi("xpT", (DI, 96))
    dtpT = gi("dtpT", (64, DI))
    dtb_r = gi("dtb_r", (128, 16))
    normw_r = gi("normw_r", (128, 8))
    A_r = gi("A_r", (128, 16 * DS))
    D_r = gi("D_r", (128, 16))
    opT = gi("opT", (DI, D))
    f1_W1 = gi("f1_W1", (16, D, 2048))
    f1_W2 = gi("f1_W2", (16, 2048, D))
    f1_b1r = gi("f1_b1r", (128, 16 * NF))
    f1_b2 = gi("f1_b2", (16, D))
    f1_rW = gi("f1_rW", (D, 16))
    f1_rb = gi("f1_rb", (1, 16))

    kind_i = "ExternalOutput" if debug else "Internal"
    it_ = lambda n, s: nc.dram_tensor(n, list(s), F32, kind=kind_i).ap()
    x1T = it_("x1T", (D, T))
    x2T = it_("x2T", (D, 3 + T))
    uT = it_("uT", (DI, T))
    deltaT = it_("deltaT", (DI, T))
    zTg = it_("zTg", (DI, T))
    xdbc = it_("xdbc", (96, T))
    y2T = it_("y2T", (DI, T))
    y2own = it_("y2own", (DI, H))
    x2own = it_("x2own", (D, H))
    x3T = it_("x3T", (D, H))
    x4T = nc.dram_tensor("x4T", [D, H], F32, kind="ExternalOutput").ap()

    from contextlib import contextmanager

    @contextmanager
    def stage_names(tag):
        orig_sb, orig_ps = nc.sbuf_tensor, nc.psum_tensor
        nc.sbuf_tensor = lambda name, *a, **k: orig_sb(f"{tag}_{name}", *a, **k)
        nc.psum_tensor = lambda name, *a, **k: orig_ps(f"{tag}_{name}", *a, **k)
        try:
            yield
        finally:
            del nc.sbuf_tensor, nc.psum_tensor

    with ExitStack() as st_:
        e = st_.enter_context
        block = e(nc.Block())
        p = Prog(nc, st_)
        offs = e(nc.sbuf_tensor("offs", [1, 1], I32))
        z3 = e(nc.sbuf_tensor("z3", [128, 8 * 3], F32))
        own_cell = {}

        def barrier():
            snap = dict(p.cnt)
            for eng in ENGINES:
                for s_, v in snap.items():
                    if v:
                        p.wait(eng, s_, v)

        p.dma("sync", offs[:], own0)
        p.wait("sync", "ld", p.cnt["ld"])

        def _load_off(s):
            own_cell["v"] = nc.values_load(
                offs[0:1, 0:1], engines=[mybir.EngineType.SP],
                min_val=0, max_val=H, skip_runtime_bounds_check=True)
        p.emit("sync", _load_off)
        ve = p.emit("vector", lambda v: v.memset(z3[:], 0.0), inc="ve")
        p.wait("sync", "ve", ve)
        p.dma("sync", x2T[:, 0:3].rearrange("(j p) t -> p j t", j=8),
              z3.ap().rearrange("p (j t) -> p j t", j=8), inc="st")
        barrier()

        with stage_names("s1"):
            emit_l0attn(nc, p, x, basis, qc, kc, vc, oc, decay, oscale, x1T, 0, T)
        barrier()
        with stage_names("s2"):
            emit_ffn(nc, p, 8, True, T, x1T, f0_W1, f0_W2, f0_b1r, f0_b2,
                     f0_rW, f0_rb, x2T[:, 3:3 + T], g_r=f0_g, b_r=f0_b)
        barrier()
        with stage_names("s3"):
            emit_min(nc, p, T, x2T, ipT, convw_r, convb_r, xpT, dtpT, dtb_r,
                     normw_r, uT, deltaT, xdbc, zTg)
        barrier()
        with stage_names("s4"):
            emit_scan(nc, p, T, 16, deltaT, uT, zTg, xdbc[64:80, :],
                      xdbc[80:96, :], A_r, D_r, y2T)
        barrier()
        p.emit("sync", lambda s: s.dma_start(
            out=y2own, in_=y2T[:, bass.ds(own_cell["v"], H)]), inc="ld", amt=16)
        p.emit("sync", lambda s: s.dma_start(
            out=x2own, in_=x2T[:, bass.ds(own_cell["v"] + 3, H)]), inc="ld", amt=16)
        p.wait("sync", "ld", p.cnt["ld"])
        barrier()
        with stage_names("s5"):
            emit_mout(nc, p, H, y2own, x2own, opT, x3T)
        barrier()
        with stage_names("s6"):
            emit_ffn(nc, p, 16, False, H, x3T, f1_W1, f1_W2, f1_b1r, f1_b2,
                     f1_rW, f1_rb, x4T)
        p.wait("gpsimd", "st", p.cnt["st"])
        p.replay(block)
    return nc


def make_in_maps(ins, x_batches):
    E0, E1 = 8, 16
    cw = ins["L1_attn_conv_w"][:, 0, :]
    A_log = ins["L1_attn_A_log"]
    common = dict(
        basis=ins["L0_attn_basis"],
        qc=ins["L0_attn_q_coeffs"], kc=ins["L0_attn_k_coeffs"],
        vc=ins["L0_attn_v_coeffs"], oc=ins["L0_attn_o_coeffs"],
        decay=np.asarray(ins["L0_attn_decay"]).reshape(1, 1),
        oscale=np.asarray(ins["L0_attn_out_scale"]).reshape(1, 1),
        f0_W1=ins["L0_ffn_W1"], f0_W2=ins["L0_ffn_W2"],
        f0_b1r=ins["L0_ffn_b1"].reshape(E0, NF, 128).transpose(2, 0, 1).reshape(128, E0 * NF).copy(),
        f0_b2=ins["L0_ffn_b2"], f0_rW=ins["L0_ffn_rW"],
        f0_rb=ins["L0_ffn_rb"].reshape(1, E0),
        f0_g=ins["L0_ffn_ln_g"].reshape(8, 128).T.copy(),
        f0_b=ins["L0_ffn_ln_b"].reshape(8, 128).T.copy(),
        ipT=np.ascontiguousarray(ins["L1_attn_in_proj_w"].T),
        convw_r=cw.reshape(16, 128, 4).transpose(1, 0, 2).reshape(128, 64).copy(),
        convb_r=ins["L1_attn_conv_b"].reshape(16, 128).T.copy(),
        xpT=np.ascontiguousarray(ins["L1_attn_x_proj_w"].T),
        dtpT=np.ascontiguousarray(ins["L1_attn_dt_proj_w"].T),
        dtb_r=ins["L1_attn_dt_proj_b"].reshape(16, 128).T.copy(),
        normw_r=ins["L1_attn_norm_w"].reshape(8, 128).T.copy(),
        A_r=A_log.reshape(16, 128, DS).transpose(1, 0, 2).reshape(128, 16 * DS).copy(),
        D_r=ins["L1_attn_D_param"].reshape(16, 128).T.copy(),
        opT=np.ascontiguousarray(ins["L1_attn_out_proj_w"].T),
        f1_W1=ins["L1_ffn_W1"], f1_W2=ins["L1_ffn_W2"],
        f1_b1r=ins["L1_ffn_b1"].reshape(E1, NF, 128).transpose(2, 0, 1).reshape(128, E1 * NF).copy(),
        f1_b2=ins["L1_ffn_b2"], f1_rW=ins["L1_ffn_Wr"],
        f1_rb=ins["L1_ffn_br"].reshape(1, E1),
    )
    maps = []
    for c in range(8):
        b, h = c // 2, c % 2
        maps.append(dict(x=np.ascontiguousarray(x_batches[b]),
                         own0=np.array([[h * H]], np.int32), **common))
    return maps



_NC_CACHE = {}


def kernel(**ins):
    x = np.asarray(ins["L0_attn_x"])
    if "fused" not in _NC_CACHE:
        _NC_CACHE["fused"] = build_fused()
    nc = _NC_CACHE["fused"]
    maps = make_in_maps(ins, x)
    res = run_bass_kernel_spmd(nc, maps, core_ids=list(range(8)))
    x4 = np.empty_like(x)
    for c in range(8):
        b, h = c // 2, c % 2
        x4[b, h * H:(h + 1) * H] = res.results[c]["x4T"].T
    return x4


# version 07
# speedup vs baseline: 1.0720x; 1.0720x over previous
import sys
for _p in ("/opt/trn_rl_repo",):
    if _p not in sys.path:
        sys.path.append(_p)
import numpy as np
import concourse.bass as bass
import concourse.mybir as mybir
from concourse.bass_utils import run_bass_kernel_spmd
from concourse.alu_op_type import AluOpType as Op
from contextlib import ExitStack

F32 = mybir.dt.float32
BF16 = mybir.dt.bfloat16
AF = mybir.ActivationFunctionType
AX = mybir.AxisListType

ENGINES = ("sync", "tensor", "vector", "scalar", "gpsimd")


class Prog:
    def __init__(self, nc, stack):
        self.nc = nc
        e = stack.enter_context
        self.sems = {n: e(nc.semaphore(f"sem_{n}")) for n in
                     ("ld", "st", "te", "ve", "se", "gp", "ld2")}
        self.cnt = {n: 0 for n in self.sems}
        self.ops = {n: [] for n in ENGINES}

    def emit(self, eng, fn, inc=None, amt=1):
        if inc is not None:
            self.cnt[inc] += amt
            sem = self.sems[inc]

            def wrapped(e, fn=fn, sem=sem, amt=amt):
                ins = fn(e)
                ins.then_inc(sem, amt)
            self.ops[eng].append(wrapped)
            return self.cnt[inc]
        else:
            self.ops[eng].append(fn)
            return None

    def wait(self, eng, sem_name, val):
        sem = self.sems[sem_name]
        self.ops[eng].append(lambda e, sem=sem, val=val: e.wait_ge(sem, val))

    def drain(self, eng):
        m = {"vector": "ve", "scalar": "se", "tensor": "te", "gpsimd": "gp"}
        s = m.get(eng)
        if s is not None:
            self.wait(eng, s, self.cnt[s])

    def dma(self, eng, out, in_, inc="ld", **kw):
        return self.emit(eng, lambda e, o=out, i=in_: e.dma_start(out=o, in_=i, **kw),
                         inc=inc, amt=16)

    def replay(self, block):
        for name in ENGINES:
            ops = self.ops[name]

            def make(ops):
                def run(engine):
                    for f in ops:
                        f(engine)
                return run
            getattr(block, name)(make(ops))


T = 8192
D = 1024
S = 128
C = 128
HF = 2048
CH = 512
Q = 1024
EPS = 1e-5
NF = 16
DI = 2048
DS = 16
DTR = 64
RMS_EPS = 1.1920929e-07
LN_EPS = 1e-5


def emit_l0attn(nc, p, x, basis, qc, kc, vc, oc, decay, oscale, x1T,
                own_start, own_end, debug=False):
    n_own0 = own_start // S
    n_all = own_end // S
    with ExitStack() as st_:
        e = st_.enter_context
        sb_basis = e(nc.sbuf_tensor("sb_basis", [128, 8 * 64], F32))
        sb_basisT = e(nc.sbuf_tensor("sb_basisT", [64, D], F32))
        sb_c = {n: e(nc.sbuf_tensor(f"sb_{n}c", [C, 64], F32)) for n in "qkvo"}
        sb_cT = {n: e(nc.sbuf_tensor(f"sb_{n}cT", [64, C], F32)) for n in "qkvo"}
        w_T_f = {n: e(nc.sbuf_tensor(f"w_{n}T_f", [C, D], F32)) for n in "qkv"}
        w_T_b = {n: e(nc.sbuf_tensor(f"w_{n}T_b", [C, D], BF16)) for n in "qkvo"}
        w_st = {n: e(nc.sbuf_tensor(f"w_{n}st", [128, 8 * C], BF16)) for n in "qkv"}
        ident_f = e(nc.sbuf_tensor("ident_f", [128, 128], F32))
        ident_b = e(nc.sbuf_tensor("ident_b", [128, 128], BF16))
        iota_sc = e(nc.sbuf_tensor("iota_sc", [128, 128], F32))
        ones_f = e(nc.sbuf_tensor("ones_f", [1, 128], F32))
        sc11 = {n: e(nc.sbuf_tensor(f"sc11_{n}", [1, 1], F32)) for n in
                ("decay", "oscale", "t", "t1", "sp", "logd")}
        bc = {n: e(nc.sbuf_tensor(f"bc_{n}", [128, 1], F32)) for n in
              ("logd", "dS", "drev", "os", "eps", "mx", "nmx", "ssum", "rs",
               "s1", "mean", "v2", "lnv", "rstd")}
        Dcol = e(nc.sbuf_tensor("Dcol", [128, S], F32))
        maskE = e(nc.sbuf_tensor("maskE", [128, S], F32))
        binm = e(nc.sbuf_tensor("binm", [128, S], F32))
        mask_f = e(nc.sbuf_tensor("mask_f", [128, S], F32))
        M_f = e(nc.sbuf_tensor("M_f", [C, C], F32))
        M_b = e(nc.sbuf_tensor("M_b", [C, C], BF16))
        xf = [e(nc.sbuf_tensor(f"xf{i}", [S, D], F32)) for i in range(2)]
        xb = e(nc.sbuf_tensor("xb", [S, D], BF16))
        xT = e(nc.sbuf_tensor("xT", [128, 8 * S], BF16))
        kT_b = e(nc.sbuf_tensor("kT_b", [C, S], BF16))
        k_s = e(nc.sbuf_tensor("k_s", [S, C], BF16))
        vT_b = e(nc.sbuf_tensor("vT_b", [C, S], BF16))
        v_b = e(nc.sbuf_tensor("v_b", [S, C], BF16))
        qT_b = e(nc.sbuf_tensor("qT_b", [C, S], BF16))
        qTs_b = e(nc.sbuf_tensor("qTs_b", [C, S], BF16))
        P_b = e(nc.sbuf_tensor("P_b", [S, S], BF16))
        ret_b = e(nc.sbuf_tensor("ret_b", [C, S], BF16))
        t_x1 = e(nc.sbuf_tensor("t_x1", [S, D], F32))
        scr = e(nc.sbuf_tensor("scr", [S, D], F32))
        x1c = e(nc.sbuf_tensor("x1c", [S, D], F32))
        x1tc = e(nc.sbuf_tensor("x1tc", [128, 8 * S], F32))
        p_trf = e(nc.psum_tensor("p_trf", [128, 128], F32))
        p_trb = [e(nc.psum_tensor(f"p_trb{i}", [128, 128], BF16)) for i in range(2)]
        p_proj = e(nc.psum_tensor("p_proj", [128, S], F32))
        p_sc = e(nc.psum_tensor("p_sc", [128, S], F32))
        p_ret = e(nc.psum_tensor("p_ret", [C, S], F32))
        p_o = e(nc.psum_tensor("p_o", [128, 1024], F32))


        p.dma("sync", sb_basis.ap().rearrange("p (j n) -> p j n", j=8),
              basis.rearrange("(j p) n -> p j n", j=8))
        for n, t in (("q", qc), ("k", kc), ("v", vc), ("o", oc)):
            p.dma("sync", sb_c[n][:], t)
        p.dma("sync", sc11["decay"][:], decay)
        p.dma("sync", sc11["oscale"][:], oscale)
        ld_setup = p.cnt["ld"]

        p.emit("gpsimd", lambda g: g.iota(iota_sc[:], pattern=[[-1, 128]], base=0,
                                          channel_multiplier=1,
                                          allow_small_or_imprecise_dtypes=True), inc="gp")
        p.wait("vector", "gp", p.cnt["gp"])
        p.emit("vector", lambda v: v.tensor_scalar(out=ident_f[:], in0=iota_sc[:],
                                                   scalar1=0.0, scalar2=None, op0=Op.is_equal), inc="ve")
        p.drain("vector")
        p.emit("vector", lambda v: v.tensor_copy(ident_b[:], ident_f[:]), inc="ve")
        p.emit("vector", lambda v: v.memset(ones_f[:], 1.0), inc="ve")
        ve_ident = p.cnt["ve"]

        p.wait("tensor", "ld", ld_setup)
        p.wait("tensor", "ve", ve_ident)
        ev_after_tr = []
        for j in range(8):
            te = p.emit("tensor", lambda t, j=j: t.transpose(
                p_trf[0:64, 0:128], sb_basis[:, j * 64:(j + 1) * 64], ident_f[:]), inc="te")
            p.wait("vector", "te", te)
            ve = p.emit("vector", lambda v, j=j: v.tensor_copy(
                sb_basisT[:, j * 128:(j + 1) * 128], p_trf[0:64, 0:128]), inc="ve")
            p.wait("tensor", "ve", ve)
        for n in "qkvo":
            te = p.emit("tensor", lambda t, n=n: t.transpose(
                p_trf[0:64, 0:128], sb_c[n][:], ident_f[:]), inc="te")
            p.wait("vector", "te", te)
            ve = p.emit("vector", lambda v, n=n: v.tensor_copy(
                sb_cT[n][:], p_trf[0:64, 0:128]), inc="ve")
            p.wait("tensor", "ve", ve)

        for n in "qkvo":
            for blk in range(2):
                te = p.emit("tensor", lambda t, n=n, blk=blk: t.matmul(
                    p_o[:, blk * 512:(blk + 1) * 512], sb_cT[n][:],
                    sb_basisT[:, blk * 512:(blk + 1) * 512],
                    start=True, stop=True), inc="te")
            p.wait("vector", "te", p.cnt["te"])
            if n == "o":
                ve = p.emit("vector", lambda v: v.tensor_copy(w_T_b["o"][:], p_o[:]), inc="ve")
                p.wait("tensor", "ve", ve)
            else:
                p.emit("vector", lambda v: v.reduce_max(bc["mx"][:], p_o[:], axis=AX.X), inc="ve")
                p.drain("vector")
                ve = p.emit("vector", lambda v: v.tensor_scalar(
                    out=bc["nmx"][:], in0=bc["mx"][:], scalar1=-1.0, scalar2=None, op0=Op.mult), inc="ve")
                p.wait("scalar", "ve", ve)
                se = p.emit("scalar", lambda s, n=n: s.activation(
                    out=w_T_f[n][:], in_=p_o[:], func=AF.Exp, bias=bc["nmx"][:, 0:1],
                    accum_out=bc["ssum"][:, 0:1]), inc="se")
                p.wait("vector", "se", se)
                p.emit("vector", lambda v: v.reciprocal(bc["rs"][:], bc["ssum"][:]), inc="ve")
                p.drain("vector")
                p.emit("vector", lambda v, n=n: v.tensor_scalar(
                    out=w_T_f[n][:], in0=w_T_f[n][:], scalar1=bc["rs"][:, 0:1],
                    scalar2=None, op0=Op.mult), inc="ve")
                p.drain("vector")
                ve = p.emit("vector", lambda v, n=n: v.tensor_copy(w_T_b[n][:], w_T_f[n][:]), inc="ve")
                p.wait("tensor", "ve", ve)

        for n in "qkv":
            for j in range(8):
                te = p.emit("tensor", lambda t, n=n, j=j: t.transpose(
                    p_trb[j % 2][:], w_T_b[n][:, j * 128:(j + 1) * 128], ident_b[:]), inc="te")
                p.wait("vector", "te", te)
                ve = p.emit("vector", lambda v, n=n, j=j: v.tensor_copy(
                    w_st[n][:, j * 128:(j + 1) * 128], p_trb[j % 2][:]), inc="ve")
                if j >= 1:
                    p.wait("tensor", "ve", ve - 1)
            p.wait("tensor", "ve", p.cnt["ve"])

        p.wait("scalar", "ld", ld_setup)
        p.emit("scalar", lambda s: s.activation(out=sc11["t"][:], in_=sc11["decay"][:],
                                                func=AF.Exp, scale=-1.0), inc="se")
        p.wait("vector", "se", p.cnt["se"])
        p.emit("vector", lambda v: v.tensor_scalar(out=sc11["t1"][:], in0=sc11["t"][:],
                                                   scalar1=1.0, scalar2=None, op0=Op.add), inc="ve")
        p.wait("scalar", "ve", p.cnt["ve"])
        p.emit("scalar", lambda s: s.activation(out=sc11["sp"][:], in_=sc11["t1"][:],
                                                func=AF.Ln), inc="se")
        p.wait("vector", "se", p.cnt["se"])
        ve = p.emit("vector", lambda v: v.tensor_scalar(out=sc11["logd"][:], in0=sc11["sp"][:],
                                                        scalar1=-1.0, scalar2=None, op0=Op.mult), inc="ve")
        p.wait("tensor", "ve", ve)
        te = p.emit("tensor", lambda t: t.matmul(p_trf[:, 0:1], ones_f[:], sc11["logd"][:],
                                                 start=True, stop=True), inc="te")
        p.wait("vector", "te", te)
        ve = p.emit("vector", lambda v: v.tensor_copy(bc["logd"][:], p_trf[:, 0:1]), inc="ve")
        p.drain("vector")
        p.wait("tensor", "ve", ve)
        te = p.emit("tensor", lambda t: t.matmul(p_trf[:, 0:1], ones_f[:], sc11["oscale"][:],
                                                 start=True, stop=True), inc="te")
        p.wait("vector", "te", te)
        ve = p.emit("vector", lambda v: v.tensor_copy(bc["os"][:], p_trf[:, 0:1]), inc="ve")
        p.wait("tensor", "ve", ve)

        p.emit("gpsimd", lambda g: g.iota(Dcol[:], pattern=[[1, S]], base=0,
                                          channel_multiplier=0,
                                          allow_small_or_imprecise_dtypes=True), inc="gp")
        p.emit("gpsimd", lambda g: g.iota(maskE[:], pattern=[[1, S]], base=-1,
                                          channel_multiplier=-1,
                                          allow_small_or_imprecise_dtypes=True), inc="gp")
        p.emit("gpsimd", lambda g: g.iota(bc["drev"][:], pattern=[[0, 1]], base=S - 1,
                                          channel_multiplier=-1,
                                          allow_small_or_imprecise_dtypes=True), inc="gp")
        gp_io = p.cnt["gp"]
        p.wait("vector", "gp", gp_io)
        p.emit("vector", lambda v: v.tensor_scalar(out=binm[:], in0=maskE[:],
                                                   scalar1=0.0, scalar2=None, op0=Op.is_ge), inc="ve")
        p.drain("vector")
        p.emit("vector", lambda v: v.tensor_tensor(out=maskE[:], in0=maskE[:], in1=binm[:],
                                                   op=Op.mult), inc="ve")
        p.wait("scalar", "gp", gp_io)
        p.wait("scalar", "ve", p.cnt["ve"])
        p.emit("scalar", lambda s: s.activation(out=Dcol[:], in_=Dcol[:], func=AF.Exp,
                                                scale=bc["logd"][:, 0:1]), inc="se")
        p.emit("scalar", lambda s: s.activation(out=maskE[:], in_=maskE[:], func=AF.Exp,
                                                scale=bc["logd"][:, 0:1]), inc="se")
        p.emit("scalar", lambda s: s.activation(out=bc["drev"][:], in_=bc["drev"][:], func=AF.Exp,
                                                scale=bc["logd"][:, 0:1]), inc="se")
        p.wait("vector", "se", p.cnt["se"])
        p.emit("vector", lambda v: v.tensor_tensor(out=mask_f[:], in0=maskE[:], in1=binm[:],
                                                   op=Op.mult), inc="ve")
        p.emit("vector", lambda v: v.tensor_scalar(out=bc["dS"][:], in0=bc["logd"][:],
                                                   scalar1=float(S), scalar2=None, op0=Op.mult), inc="ve")
        p.wait("scalar", "ve", p.cnt["ve"])
        p.emit("scalar", lambda s: s.activation(out=bc["dS"][:], in_=bc["dS"][:], func=AF.Exp), inc="se")
        p.emit("vector", lambda v: v.memset(bc["eps"][:], 1e-5), inc="ve")
        p.emit("vector", lambda v: v.memset(M_f[:], 0.0), inc="ve")
        p.emit("vector", lambda v: v.memset(M_b[:], 0.0), inc="ve")
        setup_se = p.cnt["se"]
        setup_ve = p.cnt["ve"]
        p.wait("tensor", "se", setup_se)
        p.wait("tensor", "ve", setup_ve)
        p.wait("sync", "ve", setup_ve)
        xf_free = [None, None]
        last_x1_store = 0
        for c in range(n_all):
            own = c >= n_own0
            par = c % 2
            t0 = c * S
            if xf_free[par] is not None:
                p.wait("sync", "ve", xf_free[par])
            p.dma("sync", xf[par][:], x[t0:t0 + S, :])
            ld_x = p.cnt["ld"]
            p.wait("vector", "ld", ld_x)
            ve = p.emit("vector", lambda v, par=par: v.tensor_copy(xb[:], xf[par][:]), inc="ve")
            xf_free[par] = ve
            p.wait("tensor", "ve", ve)
            for j in range(8):
                te = p.emit("tensor", lambda t, j=j: t.transpose(
                    p_trb[j % 2][:], xb[:, j * 128:(j + 1) * 128], ident_b[:]), inc="te")
                p.wait("vector", "te", te)
                ve = p.emit("vector", lambda v, j=j: v.tensor_copy(
                    xT[:, j * S:(j + 1) * S], p_trb[j % 2][:]), inc="ve")
                if j >= 1:
                    p.wait("tensor", "ve", ve - 1)
            p.wait("tensor", "ve", p.cnt["ve"])

            names = ["k", "v"] + (["q"] if own else [])
            for n in names:
                for j in range(8):
                    te = p.emit("tensor", lambda t, n=n, j=j: t.matmul(
                        p_proj[:], w_st[n][:, j * 128:(j + 1) * 128],
                        xT[:, j * S:(j + 1) * S],
                        start=(j == 0), stop=(j == 7)), inc="te")
                p.wait("vector", "te", p.cnt["te"])
                if n == "k":
                    ve = p.emit("vector", lambda v: v.tensor_copy(kT_b[:], p_proj[:]), inc="ve")
                elif n == "v":
                    ve = p.emit("vector", lambda v: v.tensor_copy(vT_b[:], p_proj[:]), inc="ve")
                else:
                    p.emit("vector", lambda v: v.tensor_copy(qT_b[:], p_proj[:]), inc="ve")
                    ve = p.emit("vector", lambda v: v.tensor_tensor(
                        out=qTs_b[:], in0=p_proj[:], in1=Dcol[:], op=Op.mult), inc="ve")
                p.wait("tensor", "ve", ve)
                if n in "kv":
                    te = p.emit("tensor", lambda t, n=n: t.transpose(
                        p_trb[0][:], kT_b[:] if n == "k" else vT_b[:], ident_b[:]), inc="te")
                    p.wait("vector", "te", te)
                    if n == "k":
                        ve = p.emit("vector", lambda v: v.tensor_scalar(
                            out=k_s[:], in0=p_trb[0][:], scalar1=bc["drev"][:, 0:1],
                            scalar2=None, op0=Op.mult), inc="ve")
                    else:
                        ve = p.emit("vector", lambda v: v.tensor_copy(v_b[:], p_trb[0][:]), inc="ve")
                    p.wait("tensor", "ve", ve)

            if own:
                te = p.emit("tensor", lambda t: t.matmul(p_sc[:], kT_b[:], qT_b[:],
                                                         start=True, stop=True), inc="te")
                p.wait("vector", "te", te)
                ve = p.emit("vector", lambda v: v.tensor_tensor(out=P_b[:], in0=p_sc[:],
                                                                in1=mask_f[:], op=Op.mult), inc="ve")
                p.wait("tensor", "ve", ve)
                p.emit("tensor", lambda t: t.matmul(p_ret[:], v_b[:], P_b[:],
                                                    start=True, stop=False), inc="te")
                te = p.emit("tensor", lambda t: t.matmul(p_ret[:], M_b[:], qTs_b[:],
                                                         start=False, stop=True), inc="te")
                p.wait("vector", "te", te)
                ve = p.emit("vector", lambda v: v.tensor_copy(ret_b[:], p_ret[:]), inc="ve")
                p.wait("tensor", "ve", ve)
                for blk in range(2):
                    te = p.emit("tensor", lambda t, blk=blk: t.matmul(
                        p_o[:, blk * 512:(blk + 1) * 512], ret_b[:],
                        w_T_b["o"][:, blk * 512:(blk + 1) * 512],
                        start=True, stop=True), inc="te")
                p.wait("vector", "te", p.cnt["te"])
                xf_free[par] = p.emit("vector", lambda v, par=par: v.scalar_tensor_tensor(
                    out=t_x1[:], in0=p_o[:], scalar=bc["os"][:, 0:1], in1=xf[par][:],
                    op0=Op.mult, op1=Op.add), inc="ve")
                p.drain("vector")
                p.emit("vector", lambda v: v.tensor_reduce(out=bc["s1"][:], in_=t_x1[:],
                                                           axis=AX.X, op=Op.add), inc="ve")
                p.drain("vector")
                p.emit("vector", lambda v: v.tensor_scalar(
                    out=bc["mean"][:], in0=bc["s1"][:], scalar1=1.0 / D, scalar2=None,
                    op0=Op.mult), inc="ve")
                p.drain("vector")
                ve = p.emit("vector", lambda v: v.tensor_scalar(
                    out=t_x1[:], in0=t_x1[:], scalar1=bc["mean"][:, 0:1], scalar2=None,
                    op0=Op.subtract), inc="ve")
                p.wait("scalar", "ve", ve)
                se = p.emit("scalar", lambda s: s.activation(out=scr[:], in_=t_x1[:],
                                                             func=AF.Square,
                                                             accum_out=bc["v2"][:, 0:1]), inc="se")
                p.drain("scalar")
                p.emit("scalar", lambda s: s.activation(out=bc["lnv"][:], in_=bc["v2"][:],
                                                        func=AF.Ln, scale=1.0 / D,
                                                        bias=bc["eps"][:, 0:1]), inc="se")
                p.drain("scalar")
                se = p.emit("scalar", lambda s: s.activation(out=bc["rstd"][:], in_=bc["lnv"][:],
                                                             func=AF.Exp, scale=-0.5), inc="se")
                p.wait("vector", "se", se)
                if last_x1_store:
                    p.wait("vector", "st", last_x1_store)
                ve = p.emit("vector", lambda v: v.tensor_scalar(
                    out=x1c[:], in0=t_x1[:], scalar1=bc["rstd"][:, 0:1], scalar2=None,
                    op0=Op.mult), inc="ve")
                p.drain("vector")
                p.wait("tensor", "ve", ve)
                for j in range(8):
                    te = p.emit("tensor", lambda t, j=j: t.transpose(
                        p_trf[:], x1c[:, j * 128:(j + 1) * 128], ident_f[:]), inc="te")
                    p.wait("vector", "te", te)
                    ve = p.emit("vector", lambda v, j=j: v.tensor_copy(
                        x1tc[:, j * S:(j + 1) * S], p_trf[:]), inc="ve")
                    p.wait("tensor", "ve", ve)
                p.wait("sync", "ve", ve)
                last_x1_store = p.dma(
                    "sync", x1T[:, t0 - own_start:t0 - own_start + S].rearrange(
                        "(j p) t -> p j t", j=8),
                    x1tc.ap().rearrange("p (j t) -> p j t", j=8), inc="st")

            te = p.emit("tensor", lambda t: t.matmul(p_sc[:], k_s[:], v_b[:],
                                                     start=True, stop=True), inc="te")
            p.wait("vector", "te", te)
            p.emit("vector", lambda v: v.tensor_scalar(out=M_f[:], in0=M_f[:],
                                                       scalar1=bc["dS"][:, 0:1], scalar2=None,
                                                       op0=Op.mult), inc="ve")
            p.drain("vector")
            p.emit("vector", lambda v: v.tensor_tensor(out=M_f[:], in0=M_f[:], in1=p_sc[:],
                                                       op=Op.add), inc="ve")
            p.drain("vector")
            ve = p.emit("vector", lambda v: v.tensor_copy(M_b[:], M_f[:]), inc="ve")
            p.wait("tensor", "ve", ve)

    return



def emit_ffn(nc, p, E, pre_ln, T_own, xT, W1, W2, b1r, b2, rW, rb, outT,
             g_r=None, b_r=None, debug=False):
    n_q = T_own // Q
    n_ch = Q // CH

    with ExitStack() as st_:
        e = st_.enter_context
        xq = e(nc.sbuf_tensor("xq", [128, 8 * Q], F32))
        zb = e(nc.sbuf_tensor("zb", [128, 8 * Q], BF16))
        mixT = e(nc.sbuf_tensor("mixT", [128, 8 * Q], F32))
        w1b = e(nc.sbuf_tensor("w1b", [128, 8 * HF], BF16))
        w2b = e(nc.sbuf_tensor("w2b", [128, NF * D], BF16))
        hid = e(nc.sbuf_tensor("hid", [128, NF * CH], BF16))
        b1s = e(nc.sbuf_tensor("b1s", [128, E * NF], F32))
        b2s = e(nc.sbuf_tensor("b2s", [E, D], BF16))
        rWs = e(nc.sbuf_tensor("rWs", [128, 8 * E], F32))
        rbs = e(nc.sbuf_tensor("rbs", [1, E], F32))
        rb_bc = e(nc.sbuf_tensor("rb_bc", [128, E], F32))
        if pre_ln:
            gs = e(nc.sbuf_tensor("gs", [128, 8], F32))
            bs = e(nc.sbuf_tensor("bs", [128, 8], F32))
        ident_f = e(nc.sbuf_tensor("ident_f", [128, 128], F32))
        ones_f = e(nc.sbuf_tensor("ones_f", [1, 128], F32))
        ones_col = e(nc.sbuf_tensor("ones_col", [128, 1], F32))
        scr_g = [e(nc.sbuf_tensor(f"scr_g{i}", [128, CH], F32)) for i in range(2)]
        scr2 = e(nc.sbuf_tensor("scr2", [128, CH], F32))
        lgacc = e(nc.sbuf_tensor("lgacc", [128, 4 * E], F32))
        st2 = e(nc.sbuf_tensor("st2", [1, CH], F32))
        mean1 = e(nc.sbuf_tensor("mean1", [1, CH], F32))
        nmean1 = e(nc.sbuf_tensor("nmean1", [1, CH], F32))
        var1 = e(nc.sbuf_tensor("var1", [1, CH], F32))
        rstd1 = e(nc.sbuf_tensor("rstd1", [1, CH], F32))
        nmr1 = e(nc.sbuf_tensor("nmr1", [1, CH], F32))
        eps_t = e(nc.sbuf_tensor("eps_t", [1, 1], F32))
        nm_bc = e(nc.sbuf_tensor("nm_bc", [128, CH], F32))
        rs_bc = e(nc.sbuf_tensor("rs_bc", [128, CH], F32))
        lg = e(nc.sbuf_tensor("lg", [128, E], F32))
        m1 = e(nc.sbuf_tensor("m1", [128, 1], F32))
        nm1 = e(nc.sbuf_tensor("nm1", [128, 1], F32))
        l2 = e(nc.sbuf_tensor("l2", [128, E], F32))
        m2 = e(nc.sbuf_tensor("m2", [128, 1], F32))
        d21 = e(nc.sbuf_tensor("d21", [128, 1], F32))
        em = e(nc.sbuf_tensor("em", [128, 1], F32))
        den = e(nc.sbuf_tensor("den", [128, 1], F32))
        rden = e(nc.sbuf_tensor("rden", [128, 1], F32))
        sexp = e(nc.sbuf_tensor("sexp", [128, E], F32))
        ind = e(nc.sbuf_tensor("ind", [128, E], F32))
        wv = e(nc.sbuf_tensor("wv", [128, E], F32))
        wT_b = e(nc.sbuf_tensor("wT_b", [E, Q], BF16))
        w_bc = e(nc.sbuf_tensor("w_bc", [128, CH], F32))
        sel_f = e(nc.sbuf_tensor("sel_f", [E, E * 128], BF16))
        out2 = [e(nc.sbuf_tensor(f"out2_{i}", [128, CH], F32)) for i in range(2)]

        p_stat = e(nc.psum_tensor("p_stat", [128, CH], F32))
        p_st1 = p_stat[0:1, :]
        p_st2 = p_stat[64:65, :]
        p_bc = e(nc.psum_tensor("p_bc", [128, CH], F32))
        p_hid = [e(nc.psum_tensor(f"p_hid{i}", [128, CH], F32)) for i in range(3)]
        p_mix = [e(nc.psum_tensor(f"p_mix{i}", [128, CH], F32)) for i in range(2)]
        p_r = e(nc.psum_tensor("p_r", [128, 128], F32))


        p.dma("sync", b1s[:], b1r)
        p.dma("gpsimd", b2s[:], b2, inc="ld2")
        p.dma("gpsimd", rWs.ap().rearrange("p (j e) -> p j e", j=8),
              rW.rearrange("(j p) e -> p j e", j=8), inc="ld2")
        p.dma("sync", rbs[:], rb)
        if pre_ln:
            p.dma("sync", gs[:], g_r)
            p.dma("sync", bs[:], b_r)
        ld0 = p.cnt["ld"]
        ld20 = p.cnt["ld2"]
        p.emit("gpsimd", lambda g: g.iota(ident_f[:], pattern=[[-1, 128]], base=0,
                                          channel_multiplier=1,
                                          allow_small_or_imprecise_dtypes=True), inc="gp")
        p.emit("gpsimd", lambda g: g.iota(sel_f[:], pattern=[[-1, E], [0, 128]], base=0,
                                          channel_multiplier=1,
                                          allow_small_or_imprecise_dtypes=True), inc="gp")
        p.wait("vector", "gp", p.cnt["gp"])
        p.emit("vector", lambda v: v.tensor_scalar(out=sel_f[:], in0=sel_f[:],
                                                   scalar1=0.0, scalar2=None,
                                                   op0=Op.is_equal), inc="ve")
        p.drain("vector")
        p.emit("vector", lambda v: v.tensor_scalar(out=ident_f[:], in0=ident_f[:],
                                                   scalar1=0.0, scalar2=None,
                                                   op0=Op.is_equal), inc="ve")
        p.drain("vector")
        p.emit("vector", lambda v: v.memset(ones_f[:], 1.0), inc="ve")
        p.emit("vector", lambda v: v.memset(ones_col[:], 1.0), inc="ve")
        p.emit("vector", lambda v: v.memset(eps_t[:], EPS), inc="ve")
        ve0 = p.cnt["ve"]
        p.wait("tensor", "ld", ld0)
        p.wait("tensor", "ve", ve0)
        te = p.emit("tensor", lambda t: t.matmul(p_r[:, 0:E], ones_f[:], rbs[:],
                                                 start=True, stop=True), inc="te")
        p.wait("vector", "te", te)
        ve = p.emit("vector", lambda v: v.tensor_copy(rb_bc[:], p_r[:, 0:E]), inc="ve")
        p.drain("vector")
        p.wait("tensor", "ve", ve)

        def ln_stats(tile_of):
            aps = [tile_of(j) for j in range(8)]
            p.wait("tensor", "ve", p.cnt["ve"])
            p.wait("scalar", "ve", p.cnt["ve"])
            for j in range(8):
                p.emit("tensor", lambda t, j=j, ap=aps[j]: t.matmul(
                    p_st1, ones_col[:], ap,
                    start=(j == 0), stop=(j == 7)), inc="te")
            last_sq_mm = None
            for j in range(8):
                if last_sq_mm is not None:
                    p.wait("scalar", "te", last_sq_mm)
                se = p.emit("scalar", lambda s, ap=aps[j]: s.activation(
                    out=scr2[:], in_=ap, func=AF.Square), inc="se")
                p.wait("tensor", "se", se)
                last_sq_mm = p.emit("tensor", lambda t, j=j: t.matmul(
                    p_st2, ones_col[:], scr2[:],
                    start=(j == 0), stop=(j == 7)), inc="te")
            p.wait("vector", "te", p.cnt["te"])
            p.emit("vector", lambda v: v.tensor_scalar(out=mean1[:], in0=p_st1,
                                                       scalar1=1.0 / D, scalar2=None,
                                                       op0=Op.mult), inc="ve")
            p.emit("vector", lambda v: v.tensor_scalar(out=st2[:], in0=p_st2,
                                                       scalar1=1.0 / D, scalar2=None,
                                                       op0=Op.mult), inc="ve")
            p.drain("vector")
            p.emit("vector", lambda v: v.tensor_tensor(out=var1[:], in0=mean1[:], in1=mean1[:],
                                                       op=Op.mult), inc="ve")
            p.drain("vector")
            p.emit("vector", lambda v: v.tensor_tensor(out=var1[:], in0=st2[:], in1=var1[:],
                                                       op=Op.subtract), inc="ve")
            p.emit("vector", lambda v: v.tensor_scalar(out=nmean1[:], in0=mean1[:],
                                                       scalar1=-1.0, scalar2=None,
                                                       op0=Op.mult), inc="ve")
            ve = p.cnt["ve"]
            p.drain("vector")
            p.wait("scalar", "ve", ve)
            p.emit("scalar", lambda s: s.activation(out=rstd1[:], in_=var1[:], func=AF.Ln,
                                                    bias=eps_t[:, 0:1]), inc="se")
            p.drain("scalar")
            se = p.emit("scalar", lambda s: s.activation(out=rstd1[:], in_=rstd1[:],
                                                         func=AF.Exp, scale=-0.5), inc="se")
            p.wait("tensor", "se", se)
            p.wait("tensor", "ve", p.cnt["ve"])
            for src, dst in ((nmean1, nm_bc), (rstd1, rs_bc)):
                te = p.emit("tensor", lambda t, src=src: t.matmul(
                    p_bc[:], ones_f[:], src[:], start=True, stop=True), inc="te")
                p.wait("vector", "te", te)
                ve = p.emit("vector", lambda v, dst=dst: v.tensor_copy(dst[:], p_bc[:]), inc="ve")
                p.drain("vector")
                p.wait("tensor", "ve", ve)

        last_store = 0
        w1_prev = None
        w2_prev = None
        prev_gelu = {0: None, 1: None, 2: None}
        prev_add = {0: None, 1: None}
        for q in range(n_q):
            tq = q * Q
            p.wait("sync", "ve", p.cnt["ve"])
            p.wait("sync", "te", p.cnt["te"])
            p.dma("sync", xq.ap().rearrange("p (j t) -> p j t", j=8),
                  xT[:, tq:tq + Q].rearrange("(j p) t -> p j t", j=8))
            ldq = p.cnt["ld"]
            p.wait("vector", "ld", ldq)
            p.wait("tensor", "ld", ldq)
            p.wait("scalar", "ld", ldq)
            p.wait("vector", "te", p.cnt["te"])

            for ch in range(n_ch):
                co = ch * CH

                def xq_tile(j, co=co):
                    return xq[:, j * Q + co: j * Q + co + CH]
                if pre_ln:
                    ln_stats(xq_tile)
                    p.emit("vector", lambda v: v.tensor_tensor(out=nmr1[:], in0=nmean1[:],
                                                               in1=rstd1[:], op=Op.mult), inc="ve")
                    ve = p.cnt["ve"]
                    p.drain("vector")
                    p.wait("tensor", "ve", ve)
                    te = p.emit("tensor", lambda t: t.matmul(p_bc[:], ones_f[:], nmr1[:],
                                                             start=True, stop=True), inc="te")
                    p.wait("vector", "te", te)
                    ve = p.emit("vector", lambda v: v.tensor_copy(nm_bc[:], p_bc[:]), inc="ve")
                    p.drain("vector")
                    p.wait("tensor", "ve", ve)
                p.wait("tensor", "ld2", ld20)
                rbank = [p_r, p_bc]
                for j in range(8):
                    if pre_ln:
                        p.emit("vector", lambda v, ap=xq_tile(j): v.tensor_tensor(
                            out=scr2[:], in0=ap, in1=rs_bc[:], op=Op.mult), inc="ve")
                        p.drain("vector")
                        p.emit("vector", lambda v: v.tensor_tensor(
                            out=scr2[:], in0=scr2[:], in1=nm_bc[:], op=Op.add), inc="ve")
                        p.drain("vector")
                        p.emit("vector", lambda v, j=j: v.tensor_scalar(
                            out=scr2[:], in0=scr2[:],
                            scalar1=gs[:, j:j + 1], scalar2=bs[:, j:j + 1],
                            op0=Op.mult, op1=Op.add), inc="ve")
                        p.drain("vector")
                    else:
                        p.emit("vector", lambda v, ap=xq_tile(j): v.tensor_copy(
                            scr2[:], ap), inc="ve")
                        p.drain("vector")
                    ve_z = p.emit("vector", lambda v, j=j, co=co: v.tensor_copy(
                        zb[:, j * Q + co: j * Q + co + CH], scr2[:]), inc="ve")
                    p.drain("vector")
                    p.wait("tensor", "ve", ve_z)
                    for sub in range(4):
                        bk = rbank[sub % 2]
                        te = p.emit("tensor", lambda t, j=j, sub=sub, bk=bk: t.matmul(
                            bk[:, 0:E], scr2[:, sub * 128:(sub + 1) * 128],
                            rWs[:, j * E:(j + 1) * E],
                            start=True, stop=True), inc="te")
                        p.wait("vector", "te", te)
                        if j == 0:
                            ve = p.emit("vector", lambda v, sub=sub, bk=bk: v.tensor_copy(
                                lgacc[:, sub * E:(sub + 1) * E], bk[:, 0:E]), inc="ve")
                        else:
                            ve = p.emit("vector", lambda v, sub=sub, bk=bk: v.tensor_tensor(
                                out=lgacc[:, sub * E:(sub + 1) * E],
                                in0=lgacc[:, sub * E:(sub + 1) * E],
                                in1=bk[:, 0:E], op=Op.add), inc="ve")
                        p.drain("vector")
                        if sub >= 1:
                            p.wait("tensor", "ve", ve - 1)
                    p.wait("tensor", "ve", p.cnt["ve"])

                for sc_ in range(CH // 128):
                    so = co + sc_ * 128
                    p.emit("vector", lambda v, sc_=sc_: v.tensor_tensor(
                        out=lg[:], in0=lgacc[:, sc_ * E:(sc_ + 1) * E],
                        in1=rb_bc[:], op=Op.add), inc="ve")
                    ve = p.cnt["ve"]
                    p.drain("vector")
                    p.emit("vector", lambda v: v.reduce_max(m1[:], lg[:], axis=AX.X), inc="ve")
                    p.drain("vector")
                    p.emit("vector", lambda v: v.tensor_scalar(out=nm1[:], in0=m1[:],
                                                               scalar1=-1.0, scalar2=None,
                                                               op0=Op.mult), inc="ve")
                    p.emit("vector", lambda v: v.scalar_tensor_tensor(
                        out=l2[:], in0=lg[:], scalar=m1[:, 0:1], in1=lg[:],
                        op0=Op.is_ge, op1=Op.bypass), inc="ve")
                    p.drain("vector")
                    p.emit("vector", lambda v: v.scalar_tensor_tensor(
                        out=l2[:], in0=l2[:], scalar=-1e30, in1=lg[:],
                        op0=Op.mult, op1=Op.add), inc="ve")
                    p.drain("vector")
                    p.emit("vector", lambda v: v.reduce_max(m2[:], l2[:], axis=AX.X), inc="ve")
                    p.drain("vector")
                    p.emit("vector", lambda v: v.tensor_scalar(out=d21[:], in0=m2[:],
                                                               scalar1=m1[:, 0:1], scalar2=None,
                                                               op0=Op.subtract), inc="ve")
                    p.emit("vector", lambda v: v.tensor_scalar(out=ind[:], in0=lg[:],
                                                               scalar1=m2[:, 0:1], scalar2=None,
                                                               op0=Op.is_ge), inc="ve")
                    ve = p.cnt["ve"]
                    p.drain("vector")
                    p.wait("scalar", "ve", ve)
                    p.emit("scalar", lambda s: s.activation(out=sexp[:], in_=lg[:], func=AF.Exp,
                                                            bias=nm1[:, 0:1]), inc="se")
                    p.drain("scalar")
                    se = p.emit("scalar", lambda s: s.activation(out=em[:], in_=d21[:],
                                                                 func=AF.Exp), inc="se")
                    p.wait("vector", "se", se)
                    p.emit("vector", lambda v: v.tensor_scalar(out=den[:], in0=em[:],
                                                               scalar1=1.0, scalar2=None,
                                                               op0=Op.add), inc="ve")
                    p.drain("vector")
                    p.emit("vector", lambda v: v.reciprocal(rden[:], den[:]), inc="ve")
                    p.drain("vector")
                    p.emit("vector", lambda v: v.tensor_tensor(out=wv[:], in0=sexp[:], in1=ind[:],
                                                               op=Op.mult), inc="ve")
                    p.drain("vector")
                    ve = p.emit("vector", lambda v: v.tensor_scalar(out=wv[:], in0=wv[:],
                                                                    scalar1=rden[:, 0:1],
                                                                    scalar2=None, op0=Op.mult), inc="ve")
                    p.drain("vector")
                    p.wait("tensor", "ve", ve)
                    te = p.emit("tensor", lambda t: t.transpose(p_r[0:E, 0:128], wv[:, 0:E],
                                                                ident_f[:]), inc="te")
                    p.wait("vector", "te", te)
                    ve = p.emit("vector", lambda v, so=so: v.tensor_copy(
                        wT_b[0:E, so:so + 128], p_r[0:E, 0:128]), inc="ve")
                    p.drain("vector")
                    p.wait("tensor", "ve", ve)


            for ch in range(n_ch):
                co = ch * CH
                for h in range(8):
                    p.wait("tensor", "ve", p.cnt["ve"])
                    p.emit("tensor", lambda t, h=h, co=co: t.matmul(
                        p_mix[h % 2][:], b2s[:, h * 128:(h + 1) * 128],
                        wT_b[0:E, co:co + CH], start=True, stop=True), inc="te")
                    p.wait("vector", "te", p.cnt["te"])
                    p.emit("vector", lambda v, h=h, co=co: v.tensor_copy(
                        mixT[:, h * Q + co:h * Q + co + CH], p_mix[h % 2][:]), inc="ve")
            p.wait("tensor", "ve", p.cnt["ve"])


            for ex in range(E):
                if w1_prev is not None:
                    p.wait("gpsimd", "te", w1_prev)
                p.dma("gpsimd", w1b.ap().rearrange("p (j f) -> p j f", j=8),
                      W1[ex].rearrange("(j p) f -> p j f", j=8), inc="ld2")
                w1_ld = p.cnt["ld2"]
                if w2_prev is not None:
                    p.wait("gpsimd", "te", w2_prev)
                p.dma("gpsimd", w2b.ap().rearrange("p (j d) -> p j d", j=NF),
                      W2[ex].rearrange("(j p) d -> p j d", j=NF), inc="ld2")
                w2_ld = p.cnt["ld2"]
                p.wait("tensor", "ld2", w1_ld)
                for ch in range(n_ch):
                    co = ch * CH
                    p.wait("tensor", "ve", p.cnt["ve"])
                    te = p.emit("tensor", lambda t, ex=ex, co=co: t.matmul(
                        p_bc[:], sel_f[0:E, ex * 128:(ex + 1) * 128],
                        wT_b[0:E, co:co + CH],
                        start=True, stop=True), inc="te")
                    p.wait("vector", "te", te)
                    ve = p.emit("vector", lambda v: v.tensor_copy(w_bc[:], p_bc[:]), inc="ve")
                    p.drain("vector")
                    p.wait("tensor", "ve", ve)
                    mult_of = {}
                    for f in range(NF):
                        if prev_gelu[f % 3] is not None:
                            p.wait("tensor", "se", prev_gelu[f % 3])
                        for j in range(8):
                            p.emit("tensor", lambda t, f=f, j=j, co=co: t.matmul(
                                p_hid[f % 3][:],
                                w1b[:, j * HF + f * 128: j * HF + f * 128 + 128],
                                zb[:, j * Q + co: j * Q + co + CH],
                                start=(j == 0), stop=(j == 7)), inc="te")
                        p.wait("scalar", "te", p.cnt["te"])
                        if f >= 2:
                            p.wait("scalar", "ve", mult_of[f - 2])
                        se = p.emit("scalar", lambda s, f=f, ex=ex: s.activation(
                            out=scr_g[f % 2][:], in_=p_hid[f % 3][:], func=AF.Gelu,
                            bias=b1s[:, ex * NF + f: ex * NF + f + 1]), inc="se")
                        prev_gelu[f % 3] = se
                        p.wait("vector", "se", se)
                        mult_of[f] = p.emit("vector", lambda v, f=f: v.tensor_tensor(
                            out=hid[:, f * CH:(f + 1) * CH], in0=scr_g[f % 2][:],
                            in1=w_bc[:], op=Op.mult), inc="ve")
                    p.wait("tensor", "ve", p.cnt["ve"])
                    if ch == n_ch - 1:
                        w1_prev = p.cnt["te"]
                    p.wait("tensor", "ld2", w2_ld)
                    for h in range(8):
                        if prev_add[h % 2] is not None:
                            p.wait("tensor", "ve", prev_add[h % 2])
                        for f in range(NF):
                            p.emit("tensor", lambda t, h=h, f=f: t.matmul(
                                p_mix[h % 2][:],
                                w2b[:, f * D + h * 128: f * D + h * 128 + 128],
                                hid[:, f * CH:(f + 1) * CH],
                                start=(f == 0), stop=(f == NF - 1)), inc="te")
                        p.wait("vector", "te", p.cnt["te"])
                        prev_add[h % 2] = p.emit("vector", lambda v, h=h, co=co: v.tensor_tensor(
                            out=mixT[:, h * Q + co:h * Q + co + CH],
                            in0=mixT[:, h * Q + co:h * Q + co + CH],
                            in1=p_mix[h % 2][:], op=Op.add), inc="ve")
                    p.wait("tensor", "ve", p.cnt["ve"])
                w2_prev = p.cnt["te"]


            p.drain("vector")
            for ch in range(n_ch):
                co = ch * CH
                for j in range(8):
                    p.emit("vector", lambda v, j=j, co=co: v.scalar_tensor_tensor(
                        out=mixT[:, j * Q + co:j * Q + co + CH],
                        in0=xq[:, j * Q + co:j * Q + co + CH], scalar=2.0,
                        in1=mixT[:, j * Q + co:j * Q + co + CH],
                        op0=Op.mult, op1=Op.add), inc="ve")
                p.drain("vector")
                p.wait("tensor", "ve", p.cnt["ve"])

                def mix_tile(j, co=co):
                    return mixT[:, j * Q + co: j * Q + co + CH]
                ln_stats(mix_tile)
                st_of = {}
                for j in range(8):
                    if j >= 2:
                        p.wait("vector", "st", st_of[j - 2])
                    elif last_store:
                        p.wait("vector", "st", last_store)
                    p.emit("vector", lambda v, ap=mix_tile(j): v.tensor_tensor(
                        out=scr2[:], in0=ap, in1=nm_bc[:], op=Op.add), inc="ve")
                    p.drain("vector")
                    ve = p.emit("vector", lambda v, j=j: v.tensor_tensor(
                        out=out2[j % 2][:], in0=scr2[:], in1=rs_bc[:],
                        op=Op.mult), inc="ve")
                    p.drain("vector")
                    p.wait("sync", "ve", ve)
                    st_of[j] = p.dma(
                        "sync", outT[j * 128:(j + 1) * 128, tq + co:tq + co + CH],
                        out2[j % 2][:], inc="st")
                last_store = st_of[7]

    return


def emit_min(nc, p, T_own, x2T, ipT, convw_r, convb_r, xpT, dtpT, dtb_r,
             normw_r, uT, deltaT, xdbc, zT):

    n_ch = T_own // CH
    with ExitStack() as st_:
        e = st_.enter_context
        ipb = e(nc.sbuf_tensor("ipb", [128, 8 * 2 * DI], BF16))
        xpb = e(nc.sbuf_tensor("xpb", [128, 16 * 96], BF16))
        dtpb = e(nc.sbuf_tensor("dtpb", [DTR, DI], BF16))
        cw = e(nc.sbuf_tensor("cw", [128, 16 * 4], F32))
        cb = e(nc.sbuf_tensor("cb", [128, 16], F32))
        dtb = e(nc.sbuf_tensor("dtb", [128, 16], F32))
        nw = e(nc.sbuf_tensor("nw", [128, 8], F32))
        xq = e(nc.sbuf_tensor("xq", [128, 8 * CH], F32))
        xh3 = e(nc.sbuf_tensor("xh3", [128, 8 * 3], F32))
        xnb = e(nc.sbuf_tensor("xnb", [128, 8 * (CH + 3)], BF16))
        scr2 = e(nc.sbuf_tensor("scr2", [128, CH], F32))
        scr3 = e(nc.sbuf_tensor("scr3", [128, 3], F32))
        ones_col = e(nc.sbuf_tensor("ones_col", [128, 1], F32))
        eps_t = e(nc.sbuf_tensor("eps_t", [1, 1], F32))
        ms1 = e(nc.sbuf_tensor("ms1", [1, CH], F32))
        rs_bc = e(nc.sbuf_tensor("rs_bc", [128, CH], F32))
        ones_f = e(nc.sbuf_tensor("ones_f", [1, 128], F32))
        xsb = e(nc.sbuf_tensor("xsb", [128, CH + 3], F32))
        uacc = e(nc.sbuf_tensor("uacc", [128, CH], F32))
        uf2 = [e(nc.sbuf_tensor(f"uf{i}", [128, CH], F32)) for i in range(2)]
        ub = e(nc.sbuf_tensor("ub", [128, 16 * CH], BF16))
        dtrb = e(nc.sbuf_tensor("dtrb", [DTR, CH], BF16))
        dxb = e(nc.sbuf_tensor("dxb", [96, CH], F32))
        dscr = {n: e(nc.sbuf_tensor(f"dscr_{n}", [128, CH], F32)) for n in
                ("m", "a", "l", "dl")}
        p_xz = e(nc.psum_tensor("p_xz", [128, CH], F32))
        p_h3 = e(nc.psum_tensor("p_h3", [128, 8], F32))
        p_st = e(nc.psum_tensor("p_st", [1, CH], F32))
        p_bc = e(nc.psum_tensor("p_bc", [128, CH], F32))
        p_dbc = e(nc.psum_tensor("p_dbc", [96, CH], F32))
        p_dt = e(nc.psum_tensor("p_dt", [128, CH], F32))

        p.dma("gpsimd", ipb.ap().rearrange("p (j k) -> p j k", j=8),
              ipT.rearrange("(j p) k -> p j k", j=8), inc="ld2")
        p.dma("gpsimd", xpb.ap().rearrange("p (j k) -> p j k", j=16),
              xpT.rearrange("(j p) k -> p j k", j=16), inc="ld2")
        p.dma("gpsimd", dtpb[:], dtpT, inc="ld2")
        ld2_w = p.cnt["ld2"]
        p.dma("sync", cw[:], convw_r)
        p.dma("sync", cb[:], convb_r)
        p.dma("sync", dtb[:], dtb_r)
        p.dma("sync", nw[:], normw_r)
        ld0 = p.cnt["ld"]
        p.emit("vector", lambda v: v.memset(ones_col[:], 1.0), inc="ve")
        p.emit("vector", lambda v: v.memset(ones_f[:], 1.0), inc="ve")
        p.emit("vector", lambda v: v.memset(eps_t[:], RMS_EPS), inc="ve")
        ve0 = p.cnt["ve"]
        p.wait("tensor", "ve", ve0)
        p.wait("tensor", "ld", ld0)
        p.wait("tensor", "ld2", ld2_w)
        p.wait("scalar", "ld", ld0)
        p.wait("vector", "ld", ld0)

        st_guard = {}
        p.dma("sync", xh3.ap().rearrange("p (j t) -> p j t", j=8),
              x2T[:, 0:3].rearrange("(j p) t -> p j t", j=8))
        ldh = p.cnt["ld"]
        p.wait("vector", "ld", ldh)
        p.wait("tensor", "ld", ldh)
        p.wait("scalar", "ld", ldh)
        last_sq = None
        for j in range(8):
            if last_sq is not None:
                p.wait("scalar", "te", last_sq)
            se = p.emit("scalar", lambda s, j=j: s.activation(
                out=scr3[:], in_=xh3[:, j * 3:(j + 1) * 3], func=AF.Square), inc="se")
            p.wait("tensor", "se", se)
            last_sq = p.emit("tensor", lambda t, j=j: t.matmul(
                p_st[:, 0:3], ones_col[:], scr3[:],
                start=(j == 0), stop=(j == 7)), inc="te")
        p.wait("vector", "te", p.cnt["te"])
        p.emit("vector", lambda v: v.tensor_scalar(out=ms1[:, 0:3], in0=p_st[:, 0:3],
                                                   scalar1=1.0 / D, scalar2=None,
                                                   op0=Op.mult), inc="ve")
        ve = p.cnt["ve"]
        p.drain("vector")
        p.wait("scalar", "ve", ve)
        p.emit("scalar", lambda s: s.activation(out=ms1[:, 0:3], in_=ms1[:, 0:3], func=AF.Ln,
                                                bias=eps_t[:, 0:1]), inc="se")
        p.drain("scalar")
        se = p.emit("scalar", lambda s: s.activation(out=ms1[:, 0:3], in_=ms1[:, 0:3],
                                                     func=AF.Exp, scale=-0.5), inc="se")
        p.wait("tensor", "se", se)
        te = p.emit("tensor", lambda t: t.matmul(p_bc[:, 0:3], ones_f[:], ms1[:, 0:3],
                                                 start=True, stop=True), inc="te")
        p.wait("vector", "te", te)
        ve = p.emit("vector", lambda v: v.tensor_copy(rs_bc[:, 0:3], p_bc[:, 0:3]), inc="ve")
        p.drain("vector")
        p.wait("tensor", "ve", ve)
        for j in range(8):
            p.emit("vector", lambda v, j=j: v.tensor_tensor(
                out=scr3[:], in0=xh3[:, j * 3:(j + 1) * 3],
                in1=rs_bc[:, 0:3], op=Op.mult), inc="ve")
            p.drain("vector")
            p.emit("vector", lambda v, j=j: v.tensor_scalar(
                out=xnb[:, j * (CH + 3): j * (CH + 3) + 3], in0=scr3[:],
                scalar1=nw[:, j:j + 1], scalar2=None, op0=Op.mult), inc="ve")
            p.drain("vector")

        halo_guard = [None]
        for c in range(n_ch):
            t0 = c * CH
            if c >= 2:
                p.wait("sync", "ve", halo_guard[0])
            p.dma("sync", xq.ap().rearrange("p (j t) -> p j t", j=8),
                  x2T[:, 3 + t0:3 + t0 + CH].rearrange("(j p) t -> p j t", j=8))
            ldc = p.cnt["ld"]
            p.wait("vector", "ld", ldc)
            p.wait("tensor", "ld", ldc)
            p.wait("scalar", "ld", ldc)

            if c > 0:
                p.wait("vector", "te", p.cnt["te"])
                for j in range(8):
                    p.emit("vector", lambda v, j=j: v.tensor_copy(
                        xnb[:, j * (CH + 3): j * (CH + 3) + 3],
                        xnb[:, j * (CH + 3) + CH: j * (CH + 3) + CH + 3]), inc="ve")
                p.drain("vector")

            p.wait("tensor", "ve", p.cnt["ve"])
            p.wait("scalar", "ve", p.cnt["ve"])
            last_sq = None
            for j in range(8):
                if last_sq is not None:
                    p.wait("scalar", "te", last_sq)
                se = p.emit("scalar", lambda s, j=j: s.activation(
                    out=scr2[:], in_=xq[:, j * CH:(j + 1) * CH],
                    func=AF.Square), inc="se")
                p.wait("tensor", "se", se)
                last_sq = p.emit("tensor", lambda t, j=j: t.matmul(
                    p_st[:], ones_col[:], scr2[:],
                    start=(j == 0), stop=(j == 7)), inc="te")
            p.wait("vector", "te", p.cnt["te"])
            p.emit("vector", lambda v: v.tensor_scalar(out=ms1[:], in0=p_st[:],
                                                       scalar1=1.0 / D, scalar2=None,
                                                       op0=Op.mult), inc="ve")
            ve = p.cnt["ve"]
            p.drain("vector")
            p.wait("scalar", "ve", ve)
            p.emit("scalar", lambda s: s.activation(out=ms1[:], in_=ms1[:], func=AF.Ln,
                                                    bias=eps_t[:, 0:1]), inc="se")
            p.drain("scalar")
            se = p.emit("scalar", lambda s: s.activation(out=ms1[:], in_=ms1[:],
                                                         func=AF.Exp, scale=-0.5), inc="se")
            p.wait("tensor", "se", se)
            te = p.emit("tensor", lambda t: t.matmul(p_bc[:], ones_f[:], ms1[:],
                                                     start=True, stop=True), inc="te")
            p.wait("vector", "te", te)
            ve = p.emit("vector", lambda v: v.tensor_copy(rs_bc[:], p_bc[:]), inc="ve")
            p.drain("vector")
            p.wait("tensor", "ve", ve)
            for j in range(8):
                p.emit("vector", lambda v, j=j: v.tensor_tensor(
                    out=scr2[:], in0=xq[:, j * CH:(j + 1) * CH],
                    in1=rs_bc[:], op=Op.mult), inc="ve")
                p.drain("vector")
                p.emit("vector", lambda v, j=j: v.tensor_scalar(
                    out=xnb[:, j * (CH + 3) + 3:(j + 1) * (CH + 3)], in0=scr2[:],
                    scalar1=nw[:, j:j + 1], scalar2=None, op0=Op.mult), inc="ve")
                p.drain("vector")
            ve_xn = p.cnt["ve"]
            halo_guard[0] = ve_xn
            p.wait("tensor", "ve", ve_xn)

            for jt in range(32):
                is_xs = jt < 16
                it = jt if is_xs else jt - 16
                p.wait("tensor", "ve", p.cnt["ve"])
                for j in range(8):
                    p.emit("tensor", lambda t, j=j, jt=jt: t.matmul(
                        p_xz[:],
                        ipb[:, j * 2 * DI + jt * 128: j * 2 * DI + jt * 128 + 128],
                        xnb[:, j * (CH + 3) + 3: j * (CH + 3) + 3 + CH],
                        start=(j == 0), stop=(j == 7)), inc="te")
                if is_xs:
                    for j in range(8):
                        p.emit("tensor", lambda t, j=j, jt=jt: t.matmul(
                            p_h3[:, 0:3],
                            ipb[:, j * 2 * DI + jt * 128: j * 2 * DI + jt * 128 + 128],
                            xnb[:, j * (CH + 3): j * (CH + 3) + 3],
                            start=(j == 0), stop=(j == 7)), inc="te")
                te = p.cnt["te"]
                p.wait("vector", "te", te)
                if not is_xs:
                    uf = uf2[jt % 2]
                    if ("uf", jt % 2) in st_guard:
                        p.wait("vector", "st", st_guard[("uf", jt % 2)])
                    ve = p.emit("vector", lambda v, uf=uf: v.tensor_copy(uf[:], p_xz[:]), inc="ve")
                    p.drain("vector")
                    p.wait("sync", "ve", ve)
                    st_guard[("uf", jt % 2)] = p.dma(
                        "sync", zT[it * 128:(it + 1) * 128, t0:t0 + CH], uf[:], inc="st")
                    p.wait("tensor", "ve", ve)
                    continue
                p.emit("vector", lambda v: v.tensor_copy(xsb[:, 0:3], p_h3[:, 0:3]), inc="ve")
                ve = p.emit("vector", lambda v: v.tensor_copy(xsb[:, 3:CH + 3], p_xz[:]), inc="ve")
                p.drain("vector")
                p.wait("tensor", "ve", ve)
                p.emit("vector", lambda v, it=it: v.tensor_scalar(
                    out=uacc[:], in0=xsb[:, 0:CH], scalar1=cw[:, it * 4:it * 4 + 1],
                    scalar2=None, op0=Op.mult), inc="ve")
                p.drain("vector")
                for k in range(1, 4):
                    p.emit("vector", lambda v, it=it, k=k: v.scalar_tensor_tensor(
                        out=uacc[:], in0=xsb[:, k:k + CH], scalar=cw[:, it * 4 + k:it * 4 + k + 1],
                        in1=uacc[:], op0=Op.mult, op1=Op.add), inc="ve")
                    p.drain("vector")
                ve = p.cnt["ve"]
                p.wait("scalar", "ve", ve)
                uf = uf2[jt % 2]
                if ("uf", jt % 2) in st_guard:
                    p.wait("scalar", "st", st_guard[("uf", jt % 2)])
                se = p.emit("scalar", lambda s, it=it, uf=uf: s.activation(
                    out=uf[:], in_=uacc[:], func=AF.Silu, bias=cb[:, it:it + 1]), inc="se")
                p.drain("scalar")
                p.wait("vector", "se", se)
                ve = p.emit("vector", lambda v, it=it, uf=uf: v.tensor_copy(
                    ub[:, it * CH:(it + 1) * CH], uf[:]), inc="ve")
                p.drain("vector")
                p.wait("sync", "se", se)
                st_guard[("uf", jt % 2)] = p.dma(
                    "sync", uT[it * 128:(it + 1) * 128, t0:t0 + CH], uf[:], inc="st")
            p.wait("tensor", "ve", p.cnt["ve"])
            for it in range(16):
                p.emit("tensor", lambda t, it=it: t.matmul(
                    p_dbc[:], xpb[:, it * 96:(it + 1) * 96],
                    ub[:, it * CH:(it + 1) * CH],
                    start=(it == 0), stop=(it == 15)), inc="te")
            te = p.cnt["te"]
            p.wait("vector", "te", te)
            p.emit("vector", lambda v: v.tensor_copy(dxb[:], p_dbc[:]), inc="ve")
            p.drain("vector")
            ve = p.emit("vector", lambda v: v.tensor_copy(dtrb[:], dxb[0:DTR, :]), inc="ve")
            p.drain("vector")
            p.wait("sync", "ve", ve)
            if "dbc" in st_guard:
                p.wait("sync", "st", st_guard["dbc"])
            st_guard["dbc"] = p.dma("sync", xdbc[:, t0:t0 + CH], dxb[:], inc="st")
            p.wait("tensor", "ve", ve)
            for it in range(16):
                p.emit("tensor", lambda t, it=it: t.matmul(
                    p_dt[:], dtpb[:, it * 128:(it + 1) * 128], dtrb[:],
                    start=True, stop=True), inc="te")
                te = p.cnt["te"]
                p.wait("scalar", "te", te)
                p.emit("scalar", lambda s, it=it: s.activation(
                    out=dscr["m"][:], in_=p_dt[:], func=AF.Relu,
                    bias=dtb[:, it:it + 1]), inc="se")
                p.drain("scalar")
                p.emit("scalar", lambda s, it=it: s.activation(
                    out=dscr["a"][:], in_=p_dt[:], func=AF.Abs,
                    bias=dtb[:, it:it + 1]), inc="se")
                p.drain("scalar")
                p.emit("scalar", lambda s: s.activation(
                    out=dscr["a"][:], in_=dscr["a"][:], func=AF.Exp, scale=-1.0), inc="se")
                p.drain("scalar")
                se = p.emit("scalar", lambda s: s.activation(
                    out=dscr["l"][:], in_=dscr["a"][:], func=AF.Ln,
                    bias=ones_col[:, 0:1]), inc="se")
                p.drain("scalar")
                p.wait("vector", "se", se)
                if "dl" in st_guard:
                    p.wait("vector", "st", st_guard["dl"])
                ve = p.emit("vector", lambda v: v.tensor_tensor(
                    out=dscr["dl"][:], in0=dscr["m"][:], in1=dscr["l"][:], op=Op.add), inc="ve")
                p.drain("vector")
                p.wait("tensor", "se", se)
                p.wait("sync", "ve", ve)
                st_guard["dl"] = p.dma(
                    "sync", deltaT[it * 128:(it + 1) * 128, t0:t0 + CH],
                    dscr["dl"][:], inc="st")
    return




def emit_mout(nc, p, T_own, y2T, x2T, opT, x3T):
    n_ch = T_own // CH
    with ExitStack() as st_:
        e = st_.enter_context
        opb = e(nc.sbuf_tensor("opb", [128, 16 * D], BF16))
        yq = e(nc.sbuf_tensor("yq", [128, 16 * CH], F32))
        yb = e(nc.sbuf_tensor("yb", [128, 16 * CH], BF16))
        xq = e(nc.sbuf_tensor("xq", [128, 8 * CH], F32))
        mix = e(nc.sbuf_tensor("mix", [128, 8 * CH], F32))
        scr2 = e(nc.sbuf_tensor("scr2", [128, CH], F32))
        ones_col = e(nc.sbuf_tensor("ones_col", [128, 1], F32))
        ones_f = e(nc.sbuf_tensor("ones_f", [1, 128], F32))
        eps_t = e(nc.sbuf_tensor("eps_t", [1, 1], F32))
        st2 = e(nc.sbuf_tensor("st2", [1, CH], F32))
        mean1 = e(nc.sbuf_tensor("mean1", [1, CH], F32))
        nmean1 = e(nc.sbuf_tensor("nmean1", [1, CH], F32))
        var1 = e(nc.sbuf_tensor("var1", [1, CH], F32))
        rstd1 = e(nc.sbuf_tensor("rstd1", [1, CH], F32))
        nm_bc = e(nc.sbuf_tensor("nm_bc", [128, CH], F32))
        rs_bc = e(nc.sbuf_tensor("rs_bc", [128, CH], F32))
        out2 = [e(nc.sbuf_tensor(f"out2_{i}", [128, CH], F32)) for i in range(2)]
        p_st1 = e(nc.psum_tensor("p_st1", [1, CH], F32))
        p_st2 = e(nc.psum_tensor("p_st2", [1, CH], F32))
        p_bc = e(nc.psum_tensor("p_bc", [128, CH], F32))
        p_o = [e(nc.psum_tensor(f"p_o{i}", [128, CH], F32)) for i in range(2)]

        p.dma("gpsimd", opb.ap().rearrange("p (j d) -> p j d", j=16),
              opT.rearrange("(j p) d -> p j d", j=16), inc="ld2")
        ld2w = p.cnt["ld2"]
        p.emit("vector", lambda v: v.memset(ones_col[:], 1.0), inc="ve")
        p.emit("vector", lambda v: v.memset(ones_f[:], 1.0), inc="ve")
        p.emit("vector", lambda v: v.memset(eps_t[:], LN_EPS), inc="ve")
        ve0 = p.cnt["ve"]
        p.wait("tensor", "ve", ve0)
        p.wait("tensor", "ld2", ld2w)

        def ln_stats(aps):
            p.wait("tensor", "ve", p.cnt["ve"])
            p.wait("scalar", "ve", p.cnt["ve"])
            for j in range(8):
                p.emit("tensor", lambda t, ap=aps[j], j=j: t.matmul(
                    p_st1[:], ones_col[:], ap, start=(j == 0), stop=(j == 7)), inc="te")
            last_sq = None
            for j in range(8):
                if last_sq is not None:
                    p.wait("scalar", "te", last_sq)
                se = p.emit("scalar", lambda s, ap=aps[j]: s.activation(
                    out=scr2[:], in_=ap, func=AF.Square), inc="se")
                p.wait("tensor", "se", se)
                last_sq = p.emit("tensor", lambda t, j=j: t.matmul(
                    p_st2[:], ones_col[:], scr2[:], start=(j == 0), stop=(j == 7)), inc="te")
            p.wait("vector", "te", p.cnt["te"])
            p.emit("vector", lambda v: v.tensor_scalar(out=mean1[:], in0=p_st1[:],
                                                       scalar1=1.0 / D, scalar2=None,
                                                       op0=Op.mult), inc="ve")
            p.emit("vector", lambda v: v.tensor_scalar(out=st2[:], in0=p_st2[:],
                                                       scalar1=1.0 / D, scalar2=None,
                                                       op0=Op.mult), inc="ve")
            p.drain("vector")
            p.emit("vector", lambda v: v.tensor_tensor(out=var1[:], in0=mean1[:], in1=mean1[:],
                                                       op=Op.mult), inc="ve")
            p.drain("vector")
            p.emit("vector", lambda v: v.tensor_tensor(out=var1[:], in0=st2[:], in1=var1[:],
                                                       op=Op.subtract), inc="ve")
            p.emit("vector", lambda v: v.tensor_scalar(out=nmean1[:], in0=mean1[:],
                                                       scalar1=-1.0, scalar2=None,
                                                       op0=Op.mult), inc="ve")
            ve = p.cnt["ve"]
            p.drain("vector")
            p.wait("scalar", "ve", ve)
            p.emit("scalar", lambda s: s.activation(out=rstd1[:], in_=var1[:], func=AF.Ln,
                                                    bias=eps_t[:, 0:1]), inc="se")
            p.drain("scalar")
            se = p.emit("scalar", lambda s: s.activation(out=rstd1[:], in_=rstd1[:],
                                                         func=AF.Exp, scale=-0.5), inc="se")
            p.wait("tensor", "se", se)
            p.wait("tensor", "ve", p.cnt["ve"])
            for src_, dst in ((nmean1, nm_bc), (rstd1, rs_bc)):
                te = p.emit("tensor", lambda t, src_=src_: t.matmul(
                    p_bc[:], ones_f[:], src_[:], start=True, stop=True), inc="te")
                p.wait("vector", "te", te)
                ve = p.emit("vector", lambda v, dst=dst: v.tensor_copy(dst[:], p_bc[:]), inc="ve")
                p.drain("vector")
                p.wait("tensor", "ve", ve)

        last_store = 0
        for c in range(n_ch):
            t0 = c * CH
            p.wait("sync", "ve", p.cnt["ve"])
            p.wait("sync", "te", p.cnt["te"])
            p.dma("sync", yq.ap().rearrange("p (j t) -> p j t", j=16),
                  y2T[:, t0:t0 + CH].rearrange("(j p) t -> p j t", j=16))
            p.dma("sync", xq.ap().rearrange("p (j t) -> p j t", j=8),
                  x2T[:, t0:t0 + CH].rearrange("(j p) t -> p j t", j=8))
            ldc = p.cnt["ld"]
            p.wait("vector", "ld", ldc)
            for it in range(16):
                p.emit("vector", lambda v, it=it: v.tensor_copy(
                    yb[:, it * CH:(it + 1) * CH], yq[:, it * CH:(it + 1) * CH]), inc="ve")
            p.drain("vector")
            p.wait("tensor", "ve", p.cnt["ve"])
            for h in range(8):
                for it in range(16):
                    p.emit("tensor", lambda t, h=h, it=it: t.matmul(
                        p_o[h % 2][:], opb[:, it * D + h * 128: it * D + h * 128 + 128],
                        yb[:, it * CH:(it + 1) * CH],
                        start=(it == 0), stop=(it == 15)), inc="te")
                p.wait("vector", "te", p.cnt["te"])
                ve = p.emit("vector", lambda v, h=h: v.scalar_tensor_tensor(
                    out=mix[:, h * CH:(h + 1) * CH], in0=xq[:, h * CH:(h + 1) * CH],
                    scalar=2.0, in1=p_o[h % 2][:], op0=Op.mult, op1=Op.add), inc="ve")
                p.drain("vector")
                if h >= 1:
                    p.wait("tensor", "ve", ve - 1)
            p.wait("tensor", "ve", p.cnt["ve"])
            aps = [mix[:, j * CH:(j + 1) * CH] for j in range(8)]
            ln_stats(aps)
            st_of = {}
            for j in range(8):
                if j >= 2:
                    p.wait("vector", "st", st_of[j - 2])
                elif last_store:
                    p.wait("vector", "st", last_store)
                p.emit("vector", lambda v, ap=aps[j]: v.tensor_tensor(
                    out=scr2[:], in0=ap, in1=nm_bc[:], op=Op.add), inc="ve")
                p.drain("vector")
                ve = p.emit("vector", lambda v, j=j: v.tensor_tensor(
                    out=out2[j % 2][:], in0=scr2[:], in1=rs_bc[:], op=Op.mult), inc="ve")
                p.drain("vector")
                p.wait("sync", "ve", ve)
                st_of[j] = p.dma("sync", x3T[j * 128:(j + 1) * 128, t0:t0 + CH],
                                 out2[j % 2][:], inc="st")
            last_store = st_of[7]
    return


def emit_scan(nc, p, T, NIT, deltaT, uT, zT, Bx, Cx, A_r, D_r, y2T):
    n_ch = T // CH
    NP = NIT // 2
    with ExitStack() as st_:
        e = st_.enter_context
        dl = e(nc.sbuf_tensor("dl", [128, NIT * CH], F32))
        uu = e(nc.sbuf_tensor("uu", [128, NIT * CH], F32))
        zz = e(nc.sbuf_tensor("zz", [128, NIT * CH], F32))
        bc_in = e(nc.sbuf_tensor("bcin", [DS, 2 * CH], F32))
        du = e(nc.sbuf_tensor("du", [128, NIT * CH], BF16))
        y_acc = e(nc.sbuf_tensor("y_acc", [128, NIT * CH], F32))
        A_s = e(nc.sbuf_tensor("A_s", [128, NIT * DS], F32))
        D_s = e(nc.sbuf_tensor("D_s", [128, NIT], F32))
        carry = e(nc.sbuf_tensor("carry", [128, NIT * DS], F32))
        sel16 = e(nc.sbuf_tensor("sel16", [DS, DS * 128], F32))
        B_bc = e(nc.sbuf_tensor("B_bc", [128, CH], BF16))
        C_bc = e(nc.sbuf_tensor("C_bc", [128, CH], BF16))
        dAb = [e(nc.sbuf_tensor(f"dAb{i}", [128, CH], BF16)) for i in range(4)]
        dbu = [e(nc.sbuf_tensor(f"dbu{i}", [128, CH], BF16)) for i in range(2)]
        hh = [e(nc.sbuf_tensor(f"hh{i}", [128, CH], BF16)) for i in range(2)]
        tmp = [e(nc.sbuf_tensor(f"tmp{i}", [128, CH], BF16)) for i in range(4)]
        yy = [e(nc.sbuf_tensor(f"yy{i}", [128, CH], F32)) for i in range(2)]
        sz = [e(nc.sbuf_tensor(f"sz{i}", [128, CH], F32)) for i in range(2)]
        p_b = e(nc.psum_tensor("p_b", [128, CH], F32))
        p_c = e(nc.psum_tensor("p_c", [128, CH], F32))

        p.dma("sync", A_s[:], A_r)
        p.dma("sync", D_s[:], D_r)
        ld0 = p.cnt["ld"]
        p.emit("gpsimd", lambda g: g.iota(sel16[:], pattern=[[-1, DS], [0, 128]], base=0,
                                          channel_multiplier=1,
                                          allow_small_or_imprecise_dtypes=True), inc="gp")
        p.wait("vector", "gp", p.cnt["gp"])
        p.emit("vector", lambda v: v.tensor_scalar(out=sel16[:], in0=sel16[:],
                                                   scalar1=0.0, scalar2=None,
                                                   op0=Op.is_equal), inc="ve")
        p.drain("vector")
        p.emit("vector", lambda v: v.memset(carry[:], 0.0), inc="ve")
        p.wait("scalar", "ld", ld0)
        se = p.emit("scalar", lambda s: s.activation(out=A_s[:], in_=A_s[:], func=AF.Exp), inc="se")
        p.wait("vector", "se", se)
        p.emit("vector", lambda v: v.tensor_scalar(out=A_s[:], in0=A_s[:], scalar1=-1.0,
                                                   scalar2=None, op0=Op.mult), inc="ve")
        p.drain("vector")
        ve0 = p.cnt["ve"]
        p.wait("tensor", "ve", ve0)
        p.wait("scalar", "ve", ve0)

        pair_ve = {}
        gp_pair = {}
        se_pair = {}
        g_all = 0
        free_at = None
        st_pair = {}
        tg = 0

        for c in range(n_ch):
            t0 = c * CH
            if free_at is not None:
                p.wait("sync", "ve", free_at)
                p.wait("sync", "se", p.cnt["se"])
                p.wait("sync", "gp", p.cnt["gp"])
            p.dma("sync", dl.ap().rearrange("p (j t) -> p j t", j=NIT),
                  deltaT[:, t0:t0 + CH].rearrange("(j p) t -> p j t", j=NIT))
            p.dma("sync", uu.ap().rearrange("p (j t) -> p j t", j=NIT),
                  uT[:, t0:t0 + CH].rearrange("(j p) t -> p j t", j=NIT))
            p.dma("sync", zz.ap().rearrange("p (j t) -> p j t", j=NIT),
                  zT[:, t0:t0 + CH].rearrange("(j p) t -> p j t", j=NIT))
            p.dma("sync", bc_in[:, 0:CH], Bx[:, t0:t0 + CH])
            p.dma("sync", bc_in[:, CH:2 * CH], Cx[:, t0:t0 + CH])
            ldc = p.cnt["ld"]
            p.wait("vector", "ld", ldc)
            p.wait("tensor", "ld", ldc)
            p.wait("scalar", "ld", ldc)
            p.wait("gpsimd", "ld", ldc)
            for j in range(NIT):
                p.emit("vector", lambda v, j=j: v.tensor_tensor(
                    out=du[:, j * CH:(j + 1) * CH], in0=dl[:, j * CH:(j + 1) * CH],
                    in1=uu[:, j * CH:(j + 1) * CH], op=Op.mult), inc="ve")

            for s in range(DS):
                p.wait("tensor", "ve", p.cnt["ve"])
                p.emit("tensor", lambda t, s=s: t.matmul(
                    p_b[:], sel16[:, s * 128:(s + 1) * 128], bc_in[:, 0:CH],
                    start=True, stop=True), inc="te")
                te = p.emit("tensor", lambda t, s=s: t.matmul(
                    p_c[:], sel16[:, s * 128:(s + 1) * 128], bc_in[:, CH:2 * CH],
                    start=True, stop=True), inc="te")
                p.wait("vector", "te", te)
                p.emit("vector", lambda v: v.tensor_copy(B_bc[:], p_b[:]), inc="ve")
                p.emit("vector", lambda v: v.tensor_copy(C_bc[:], p_c[:]), inc="ve")
                for pk in range(NP):
                    g = g_all
                    itA, itB = 2 * pk, 2 * pk + 1
                    sA, sB = (2 * g) % 4, (2 * g + 1) % 4
                    if g - 2 in pair_ve:
                        p.wait("scalar", "ve", pair_ve[g - 2])
                    p.emit("scalar", lambda sc, it=itA, s=s, b=sA: sc.activation(
                        out=dAb[b][:], in_=dl[:, it * CH:(it + 1) * CH], func=AF.Exp,
                        scale=A_s[:, it * DS + s: it * DS + s + 1]), inc="se")
                    se_pair[g] = p.emit("scalar", lambda sc, it=itB, s=s, b=sB: sc.activation(
                        out=dAb[b][:], in_=dl[:, it * CH:(it + 1) * CH], func=AF.Exp,
                        scale=A_s[:, it * DS + s: it * DS + s + 1]), inc="se")
                    p.emit("vector", lambda v, it=itA: v.tensor_tensor(
                        out=dbu[0][:], in0=du[:, it * CH:(it + 1) * CH], in1=B_bc[:],
                        op=Op.mult), inc="ve")
                    p.emit("vector", lambda v, it=itB: v.tensor_tensor(
                        out=dbu[1][:], in0=du[:, it * CH:(it + 1) * CH], in1=B_bc[:],
                        op=Op.mult), inc="ve")
                    p.wait("vector", "se", se_pair[g])
                    p.emit("vector", lambda v, it=itA, s=s, b=sA: v.tensor_tensor_scan(
                        out=hh[0][:], data0=dAb[b][:], data1=dbu[0][:],
                        initial=carry[:, it * DS + s: it * DS + s + 1],
                        op0=Op.mult, op1=Op.add), inc="ve")
                    p.emit("vector", lambda v, it=itB, s=s, b=sB: v.tensor_tensor_scan(
                        out=hh[1][:], data0=dAb[b][:], data1=dbu[1][:],
                        initial=carry[:, it * DS + s: it * DS + s + 1],
                        op0=Op.mult, op1=Op.add), inc="ve")
                    p.emit("vector", lambda v, it=itA, s=s: v.tensor_copy(
                        carry[:, it * DS + s: it * DS + s + 1], hh[0][:, CH - 1:CH]), inc="ve")
                    p.emit("vector", lambda v, it=itB, s=s: v.tensor_copy(
                        carry[:, it * DS + s: it * DS + s + 1], hh[1][:, CH - 1:CH]), inc="ve")
                    if g - 2 in gp_pair:
                        p.wait("vector", "gp", gp_pair[g - 2])
                    p.emit("vector", lambda v, b=sA: v.tensor_tensor(
                        out=tmp[b][:], in0=hh[0][:], in1=C_bc[:], op=Op.mult), inc="ve")
                    tv = p.emit("vector", lambda v, b=sB: v.tensor_tensor(
                        out=tmp[b][:], in0=hh[1][:], in1=C_bc[:], op=Op.mult), inc="ve")
                    pair_ve[g] = tv
                    p.wait("gpsimd", "ve", tv)
                    for it, b in ((itA, sA), (itB, sB)):
                        if s == 0:
                            gd = p.emit("gpsimd", lambda gq, it=it, b=b: gq.tensor_copy(
                                y_acc[:, it * CH:(it + 1) * CH], tmp[b][:]), inc="gp")
                        else:
                            gd = p.emit("gpsimd", lambda gq, it=it, b=b: gq.tensor_tensor(
                                out=y_acc[:, it * CH:(it + 1) * CH],
                                in0=y_acc[:, it * CH:(it + 1) * CH], in1=tmp[b][:],
                                op=Op.add), inc="gp")
                    gp_pair[g] = gd
                    g_all += 1
            p.wait("vector", "gp", p.cnt["gp"])
            p.wait("scalar", "ve", p.cnt["ve"])
            tail_mult = {}
            for pk in range(NP):
                itA, itB = 2 * pk, 2 * pk + 1
                if pk - 1 in tail_mult:
                    p.wait("scalar", "ve", tail_mult[pk - 1])
                for i, it in ((0, itA), (1, itB)):
                    p.emit("scalar", lambda sc, it=it, i=i: sc.activation(
                        out=sz[i][:], in_=zz[:, it * CH:(it + 1) * CH], func=AF.Silu), inc="se")
                se_t = p.cnt["se"]
                if tg - 1 in st_pair:
                    p.wait("vector", "st", st_pair[tg - 1])
                for i, it in ((0, itA), (1, itB)):
                    p.emit("vector", lambda v, it=it, i=i: v.scalar_tensor_tensor(
                        out=yy[i][:], in0=uu[:, it * CH:(it + 1) * CH],
                        scalar=D_s[:, it:it + 1], in1=y_acc[:, it * CH:(it + 1) * CH],
                        op0=Op.mult, op1=Op.add), inc="ve")
                p.wait("vector", "se", se_t)
                for i, it in ((0, itA), (1, itB)):
                    p.emit("vector", lambda v, i=i: v.tensor_tensor(
                        out=yy[i][:], in0=yy[i][:], in1=sz[i][:], op=Op.mult), inc="ve")
                tail_mult[pk] = ve = p.cnt["ve"]
                p.wait("sync", "ve", ve)
                for i, it in ((0, itA), (1, itB)):
                    last_st = p.dma("sync", y2T[it * 128:(it + 1) * 128, t0:t0 + CH],
                                    yy[i][:], inc="st")
                st_pair[tg] = last_st
                tg += 1
            free_at = p.cnt["ve"]
    return


I32 = mybir.dt.int32
H = T // 2


def build_fused(debug=False):
    nc = bass.Bass("TRN2", target_bir_lowering=False)
    gi = lambda n, s: nc.dram_tensor(n, list(s), F32, kind="ExternalInput").ap()
    x = gi("x", (T, D))
    own0 = nc.dram_tensor("own0", [1, 1], I32, kind="ExternalInput").ap()
    basis = gi("basis", (D, 64))
    qc, kc, vc, oc = (gi(n, (128, 64)) for n in ("qc", "kc", "vc", "oc"))
    decay = gi("decay", (1, 1))
    oscale = gi("oscale", (1, 1))
    f0_W1 = gi("f0_W1", (8, D, 2048))
    f0_W2 = gi("f0_W2", (8, 2048, D))
    f0_b1r = gi("f0_b1r", (128, 8 * NF))
    f0_b2 = gi("f0_b2", (8, D))
    f0_rW = gi("f0_rW", (D, 8))
    f0_rb = gi("f0_rb", (1, 8))
    f0_g = gi("f0_g", (128, 8))
    f0_b = gi("f0_b", (128, 8))
    ipT = gi("ipT", (D, 2 * DI))
    convw_r = gi("convw_r", (128, 64))
    convb_r = gi("convb_r", (128, 16))
    xpT = gi("xpT", (DI, 96))
    dtpT = gi("dtpT", (64, DI))
    dtb_r = gi("dtb_r", (128, 16))
    normw_r = gi("normw_r", (128, 8))
    A_r = gi("A_r", (128, 16 * DS))
    D_r = gi("D_r", (128, 16))
    opT = gi("opT", (DI, D))
    f1_W1 = gi("f1_W1", (16, D, 2048))
    f1_W2 = gi("f1_W2", (16, 2048, D))
    f1_b1r = gi("f1_b1r", (128, 16 * NF))
    f1_b2 = gi("f1_b2", (16, D))
    f1_rW = gi("f1_rW", (D, 16))
    f1_rb = gi("f1_rb", (1, 16))

    kind_i = "ExternalOutput" if debug else "Internal"
    it_ = lambda n, s: nc.dram_tensor(n, list(s), F32, kind=kind_i).ap()
    x1T = it_("x1T", (D, T))
    x2T = it_("x2T", (D, 3 + T))
    uT = it_("uT", (DI, T))
    deltaT = it_("deltaT", (DI, T))
    zTg = it_("zTg", (DI, T))
    xdbc = it_("xdbc", (96, T))
    y2T = it_("y2T", (DI, T))
    y2own = it_("y2own", (DI, H))
    x2own = it_("x2own", (D, H))
    x3T = it_("x3T", (D, H))
    x4T = nc.dram_tensor("x4T", [D, H], F32, kind="ExternalOutput").ap()

    from contextlib import contextmanager

    @contextmanager
    def stage_names(tag):
        orig_sb, orig_ps = nc.sbuf_tensor, nc.psum_tensor
        nc.sbuf_tensor = lambda name, *a, **k: orig_sb(f"{tag}_{name}", *a, **k)
        nc.psum_tensor = lambda name, *a, **k: orig_ps(f"{tag}_{name}", *a, **k)
        try:
            yield
        finally:
            del nc.sbuf_tensor, nc.psum_tensor

    with ExitStack() as st_:
        e = st_.enter_context
        block = e(nc.Block())
        p = Prog(nc, st_)
        offs = e(nc.sbuf_tensor("offs", [1, 1], I32))
        z3 = e(nc.sbuf_tensor("z3", [128, 8 * 3], F32))
        own_cell = {}

        def barrier():
            snap = dict(p.cnt)
            for eng in ENGINES:
                for s_, v in snap.items():
                    if v:
                        p.wait(eng, s_, v)

        p.dma("sync", offs[:], own0)
        p.wait("sync", "ld", p.cnt["ld"])

        def _load_off(s):
            own_cell["v"] = nc.values_load(
                offs[0:1, 0:1], engines=[mybir.EngineType.SP],
                min_val=0, max_val=H, skip_runtime_bounds_check=True)
        p.emit("sync", _load_off)
        ve = p.emit("vector", lambda v: v.memset(z3[:], 0.0), inc="ve")
        p.wait("sync", "ve", ve)
        p.dma("sync", x2T[:, 0:3].rearrange("(j p) t -> p j t", j=8),
              z3.ap().rearrange("p (j t) -> p j t", j=8), inc="st")
        barrier()

        with stage_names("s1"):
            emit_l0attn(nc, p, x, basis, qc, kc, vc, oc, decay, oscale, x1T, 0, T)
        barrier()
        with stage_names("s2"):
            emit_ffn(nc, p, 8, True, T, x1T, f0_W1, f0_W2, f0_b1r, f0_b2,
                     f0_rW, f0_rb, x2T[:, 3:3 + T], g_r=f0_g, b_r=f0_b)
        barrier()
        with stage_names("s3"):
            emit_min(nc, p, T, x2T, ipT, convw_r, convb_r, xpT, dtpT, dtb_r,
                     normw_r, uT, deltaT, xdbc, zTg)
        barrier()
        with stage_names("s4"):
            emit_scan(nc, p, T, 16, deltaT, uT, zTg, xdbc[64:80, :],
                      xdbc[80:96, :], A_r, D_r, y2T)
        barrier()
        p.emit("sync", lambda s: s.dma_start(
            out=y2own, in_=y2T[:, bass.ds(own_cell["v"], H)]), inc="ld", amt=16)
        p.emit("sync", lambda s: s.dma_start(
            out=x2own, in_=x2T[:, bass.ds(own_cell["v"] + 3, H)]), inc="ld", amt=16)
        p.wait("sync", "ld", p.cnt["ld"])
        barrier()
        with stage_names("s5"):
            emit_mout(nc, p, H, y2own, x2own, opT, x3T)
        barrier()
        with stage_names("s6"):
            emit_ffn(nc, p, 16, False, H, x3T, f1_W1, f1_W2, f1_b1r, f1_b2,
                     f1_rW, f1_rb, x4T)
        p.wait("gpsimd", "st", p.cnt["st"])
        p.replay(block)
    return nc


def make_in_maps(ins, x_batches):
    E0, E1 = 8, 16
    cw = ins["L1_attn_conv_w"][:, 0, :]
    A_log = ins["L1_attn_A_log"]
    common = dict(
        basis=ins["L0_attn_basis"],
        qc=ins["L0_attn_q_coeffs"], kc=ins["L0_attn_k_coeffs"],
        vc=ins["L0_attn_v_coeffs"], oc=ins["L0_attn_o_coeffs"],
        decay=np.asarray(ins["L0_attn_decay"]).reshape(1, 1),
        oscale=np.asarray(ins["L0_attn_out_scale"]).reshape(1, 1),
        f0_W1=ins["L0_ffn_W1"], f0_W2=ins["L0_ffn_W2"],
        f0_b1r=ins["L0_ffn_b1"].reshape(E0, NF, 128).transpose(2, 0, 1).reshape(128, E0 * NF).copy(),
        f0_b2=ins["L0_ffn_b2"], f0_rW=ins["L0_ffn_rW"],
        f0_rb=ins["L0_ffn_rb"].reshape(1, E0),
        f0_g=ins["L0_ffn_ln_g"].reshape(8, 128).T.copy(),
        f0_b=ins["L0_ffn_ln_b"].reshape(8, 128).T.copy(),
        ipT=np.ascontiguousarray(ins["L1_attn_in_proj_w"].T),
        convw_r=cw.reshape(16, 128, 4).transpose(1, 0, 2).reshape(128, 64).copy(),
        convb_r=ins["L1_attn_conv_b"].reshape(16, 128).T.copy(),
        xpT=np.ascontiguousarray(ins["L1_attn_x_proj_w"].T),
        dtpT=np.ascontiguousarray(ins["L1_attn_dt_proj_w"].T),
        dtb_r=ins["L1_attn_dt_proj_b"].reshape(16, 128).T.copy(),
        normw_r=ins["L1_attn_norm_w"].reshape(8, 128).T.copy(),
        A_r=A_log.reshape(16, 128, DS).transpose(1, 0, 2).reshape(128, 16 * DS).copy(),
        D_r=ins["L1_attn_D_param"].reshape(16, 128).T.copy(),
        opT=np.ascontiguousarray(ins["L1_attn_out_proj_w"].T),
        f1_W1=ins["L1_ffn_W1"], f1_W2=ins["L1_ffn_W2"],
        f1_b1r=ins["L1_ffn_b1"].reshape(E1, NF, 128).transpose(2, 0, 1).reshape(128, E1 * NF).copy(),
        f1_b2=ins["L1_ffn_b2"], f1_rW=ins["L1_ffn_Wr"],
        f1_rb=ins["L1_ffn_br"].reshape(1, E1),
    )
    maps = []
    for c in range(8):
        b, h = c // 2, c % 2
        maps.append(dict(x=np.ascontiguousarray(x_batches[b]),
                         own0=np.array([[h * H]], np.int32), **common))
    return maps



_NC_CACHE = {}


def kernel(**ins):
    x = np.asarray(ins["L0_attn_x"])
    if "fused" not in _NC_CACHE:
        _NC_CACHE["fused"] = build_fused()
    nc = _NC_CACHE["fused"]
    maps = make_in_maps(ins, x)
    res = run_bass_kernel_spmd(nc, maps, core_ids=list(range(8)))
    x4 = np.empty_like(x)
    for c in range(8):
        b, h = c // 2, c % 2
        x4[b, h * H:(h + 1) * H] = res.results[c]["x4T"].T
    return x4


# version 08
# speedup vs baseline: 1.1584x; 1.0806x over previous
import sys
for _p in ("/opt/trn_rl_repo",):
    if _p not in sys.path:
        sys.path.append(_p)
import numpy as np
import concourse.bass as bass
import concourse.mybir as mybir
from concourse.bass_utils import run_bass_kernel_spmd
from concourse.alu_op_type import AluOpType as Op
from contextlib import ExitStack

F32 = mybir.dt.float32
BF16 = mybir.dt.bfloat16
AF = mybir.ActivationFunctionType
AX = mybir.AxisListType

ENGINES = ("sync", "tensor", "vector", "scalar", "gpsimd")


class Prog:
    def __init__(self, nc, stack):
        self.nc = nc
        e = stack.enter_context
        self.sems = {n: e(nc.semaphore(f"sem_{n}")) for n in
                     ("ld", "st", "te", "ve", "se", "gp", "ld2")}
        self.cnt = {n: 0 for n in self.sems}
        self.ops = {n: [] for n in ENGINES}

    def emit(self, eng, fn, inc=None, amt=1):
        if inc is not None:
            self.cnt[inc] += amt
            sem = self.sems[inc]

            def wrapped(e, fn=fn, sem=sem, amt=amt):
                ins = fn(e)
                ins.then_inc(sem, amt)
            self.ops[eng].append(wrapped)
            return self.cnt[inc]
        else:
            self.ops[eng].append(fn)
            return None

    def wait(self, eng, sem_name, val):
        sem = self.sems[sem_name]
        self.ops[eng].append(lambda e, sem=sem, val=val: e.wait_ge(sem, val))

    def drain(self, eng):
        m = {"vector": "ve", "scalar": "se", "tensor": "te", "gpsimd": "gp"}
        s = m.get(eng)
        if s is not None:
            self.wait(eng, s, self.cnt[s])

    def dma(self, eng, out, in_, inc="ld", **kw):
        return self.emit(eng, lambda e, o=out, i=in_: e.dma_start(out=o, in_=i, **kw),
                         inc=inc, amt=16)

    def replay(self, block):
        for name in ENGINES:
            ops = self.ops[name]

            def make(ops):
                def run(engine):
                    for f in ops:
                        f(engine)
                return run
            getattr(block, name)(make(ops))


T = 8192
D = 1024
S = 128
C = 128
HF = 2048
CH = 512
Q = 1024
EPS = 1e-5
NF = 16
DI = 2048
DS = 16
DTR = 64
RMS_EPS = 1.1920929e-07
LN_EPS = 1e-5


def emit_l0attn(nc, p, x, basis, qc, kc, vc, oc, decay, oscale, x1T,
                own_start, own_end, debug=False):
    n_own0 = own_start // S
    n_all = own_end // S
    with ExitStack() as st_:
        e = st_.enter_context
        sb_basis = e(nc.sbuf_tensor("sb_basis", [128, 8 * 64], F32))
        sb_basisT = e(nc.sbuf_tensor("sb_basisT", [64, D], F32))
        sb_c = {n: e(nc.sbuf_tensor(f"sb_{n}c", [C, 64], F32)) for n in "qkvo"}
        sb_cT = {n: e(nc.sbuf_tensor(f"sb_{n}cT", [64, C], F32)) for n in "qkvo"}
        w_T_f = {n: e(nc.sbuf_tensor(f"w_{n}T_f", [C, D], F32)) for n in "qkv"}
        w_T_b = {n: e(nc.sbuf_tensor(f"w_{n}T_b", [C, D], BF16)) for n in "qkvo"}
        w_st = {n: e(nc.sbuf_tensor(f"w_{n}st", [128, 8 * C], BF16)) for n in "qkv"}
        ident_f = e(nc.sbuf_tensor("ident_f", [128, 128], F32))
        ident_b = e(nc.sbuf_tensor("ident_b", [128, 128], BF16))
        iota_sc = e(nc.sbuf_tensor("iota_sc", [128, 128], F32))
        ones_f = e(nc.sbuf_tensor("ones_f", [1, 128], F32))
        sc11 = {n: e(nc.sbuf_tensor(f"sc11_{n}", [1, 1], F32)) for n in
                ("decay", "oscale", "t", "t1", "sp", "logd")}
        bc = {n: e(nc.sbuf_tensor(f"bc_{n}", [128, 1], F32)) for n in
              ("logd", "dS", "drev", "os", "eps", "mx", "nmx", "ssum", "rs",
               "s1", "mean", "v2", "lnv", "rstd")}
        Dcol = e(nc.sbuf_tensor("Dcol", [128, S], F32))
        maskE = e(nc.sbuf_tensor("maskE", [128, S], F32))
        binm = e(nc.sbuf_tensor("binm", [128, S], F32))
        mask_f = e(nc.sbuf_tensor("mask_f", [128, S], F32))
        M_f = e(nc.sbuf_tensor("M_f", [C, C], F32))
        M_b = e(nc.sbuf_tensor("M_b", [C, C], BF16))
        xf = [e(nc.sbuf_tensor(f"xf{i}", [S, D], F32)) for i in range(2)]
        xb = e(nc.sbuf_tensor("xb", [S, D], BF16))
        xT = e(nc.sbuf_tensor("xT", [128, 8 * S], BF16))
        kT_b = e(nc.sbuf_tensor("kT_b", [C, S], BF16))
        k_s = e(nc.sbuf_tensor("k_s", [S, C], BF16))
        vT_b = e(nc.sbuf_tensor("vT_b", [C, S], BF16))
        v_b = e(nc.sbuf_tensor("v_b", [S, C], BF16))
        qT_b = e(nc.sbuf_tensor("qT_b", [C, S], BF16))
        qTs_b = e(nc.sbuf_tensor("qTs_b", [C, S], BF16))
        P_b = e(nc.sbuf_tensor("P_b", [S, S], BF16))
        ret_b = e(nc.sbuf_tensor("ret_b", [C, S], BF16))
        t_x1 = e(nc.sbuf_tensor("t_x1", [S, D], F32))
        scr = e(nc.sbuf_tensor("scr", [S, D], F32))
        x1c = e(nc.sbuf_tensor("x1c", [S, D], F32))
        x1tc = e(nc.sbuf_tensor("x1tc", [128, 8 * S], F32))
        p_trf = e(nc.psum_tensor("p_trf", [128, 128], F32))
        p_trb = [e(nc.psum_tensor(f"p_trb{i}", [128, 128], BF16)) for i in range(2)]
        p_proj = e(nc.psum_tensor("p_proj", [128, S], F32))
        p_sc = e(nc.psum_tensor("p_sc", [128, S], F32))
        p_ret = e(nc.psum_tensor("p_ret", [C, S], F32))
        p_o = e(nc.psum_tensor("p_o", [128, 1024], F32))


        p.dma("sync", sb_basis.ap().rearrange("p (j n) -> p j n", j=8),
              basis.rearrange("(j p) n -> p j n", j=8))
        for n, t in (("q", qc), ("k", kc), ("v", vc), ("o", oc)):
            p.dma("sync", sb_c[n][:], t)
        p.dma("sync", sc11["decay"][:], decay)
        p.dma("sync", sc11["oscale"][:], oscale)
        ld_setup = p.cnt["ld"]

        p.emit("gpsimd", lambda g: g.iota(iota_sc[:], pattern=[[-1, 128]], base=0,
                                          channel_multiplier=1,
                                          allow_small_or_imprecise_dtypes=True), inc="gp")
        p.wait("vector", "gp", p.cnt["gp"])
        p.emit("vector", lambda v: v.tensor_scalar(out=ident_f[:], in0=iota_sc[:],
                                                   scalar1=0.0, scalar2=None, op0=Op.is_equal), inc="ve")
        p.drain("vector")
        p.emit("vector", lambda v: v.tensor_copy(ident_b[:], ident_f[:]), inc="ve")
        p.emit("vector", lambda v: v.memset(ones_f[:], 1.0), inc="ve")
        ve_ident = p.cnt["ve"]

        p.wait("tensor", "ld", ld_setup)
        p.wait("tensor", "ve", ve_ident)
        ev_after_tr = []
        for j in range(8):
            te = p.emit("tensor", lambda t, j=j: t.transpose(
                p_trf[0:64, 0:128], sb_basis[:, j * 64:(j + 1) * 64], ident_f[:]), inc="te")
            p.wait("vector", "te", te)
            ve = p.emit("vector", lambda v, j=j: v.tensor_copy(
                sb_basisT[:, j * 128:(j + 1) * 128], p_trf[0:64, 0:128]), inc="ve")
            p.wait("tensor", "ve", ve)
        for n in "qkvo":
            te = p.emit("tensor", lambda t, n=n: t.transpose(
                p_trf[0:64, 0:128], sb_c[n][:], ident_f[:]), inc="te")
            p.wait("vector", "te", te)
            ve = p.emit("vector", lambda v, n=n: v.tensor_copy(
                sb_cT[n][:], p_trf[0:64, 0:128]), inc="ve")
            p.wait("tensor", "ve", ve)

        for n in "qkvo":
            for blk in range(2):
                te = p.emit("tensor", lambda t, n=n, blk=blk: t.matmul(
                    p_o[:, blk * 512:(blk + 1) * 512], sb_cT[n][:],
                    sb_basisT[:, blk * 512:(blk + 1) * 512],
                    start=True, stop=True), inc="te")
            p.wait("vector", "te", p.cnt["te"])
            if n == "o":
                ve = p.emit("vector", lambda v: v.tensor_copy(w_T_b["o"][:], p_o[:]), inc="ve")
                p.wait("tensor", "ve", ve)
            else:
                p.emit("vector", lambda v: v.reduce_max(bc["mx"][:], p_o[:], axis=AX.X), inc="ve")
                p.drain("vector")
                ve = p.emit("vector", lambda v: v.tensor_scalar(
                    out=bc["nmx"][:], in0=bc["mx"][:], scalar1=-1.0, scalar2=None, op0=Op.mult), inc="ve")
                p.wait("scalar", "ve", ve)
                se = p.emit("scalar", lambda s, n=n: s.activation(
                    out=w_T_f[n][:], in_=p_o[:], func=AF.Exp, bias=bc["nmx"][:, 0:1],
                    accum_out=bc["ssum"][:, 0:1]), inc="se")
                p.wait("vector", "se", se)
                p.emit("vector", lambda v: v.reciprocal(bc["rs"][:], bc["ssum"][:]), inc="ve")
                p.drain("vector")
                p.emit("vector", lambda v, n=n: v.tensor_scalar(
                    out=w_T_f[n][:], in0=w_T_f[n][:], scalar1=bc["rs"][:, 0:1],
                    scalar2=None, op0=Op.mult), inc="ve")
                p.drain("vector")
                ve = p.emit("vector", lambda v, n=n: v.tensor_copy(w_T_b[n][:], w_T_f[n][:]), inc="ve")
                p.wait("tensor", "ve", ve)

        for n in "qkv":
            for j in range(8):
                te = p.emit("tensor", lambda t, n=n, j=j: t.transpose(
                    p_trb[j % 2][:], w_T_b[n][:, j * 128:(j + 1) * 128], ident_b[:]), inc="te")
                p.wait("vector", "te", te)
                ve = p.emit("vector", lambda v, n=n, j=j: v.tensor_copy(
                    w_st[n][:, j * 128:(j + 1) * 128], p_trb[j % 2][:]), inc="ve")
                if j >= 1:
                    p.wait("tensor", "ve", ve - 1)
            p.wait("tensor", "ve", p.cnt["ve"])

        p.wait("scalar", "ld", ld_setup)
        p.emit("scalar", lambda s: s.activation(out=sc11["t"][:], in_=sc11["decay"][:],
                                                func=AF.Exp, scale=-1.0), inc="se")
        p.wait("vector", "se", p.cnt["se"])
        p.emit("vector", lambda v: v.tensor_scalar(out=sc11["t1"][:], in0=sc11["t"][:],
                                                   scalar1=1.0, scalar2=None, op0=Op.add), inc="ve")
        p.wait("scalar", "ve", p.cnt["ve"])
        p.emit("scalar", lambda s: s.activation(out=sc11["sp"][:], in_=sc11["t1"][:],
                                                func=AF.Ln), inc="se")
        p.wait("vector", "se", p.cnt["se"])
        ve = p.emit("vector", lambda v: v.tensor_scalar(out=sc11["logd"][:], in0=sc11["sp"][:],
                                                        scalar1=-1.0, scalar2=None, op0=Op.mult), inc="ve")
        p.wait("tensor", "ve", ve)
        te = p.emit("tensor", lambda t: t.matmul(p_trf[:, 0:1], ones_f[:], sc11["logd"][:],
                                                 start=True, stop=True), inc="te")
        p.wait("vector", "te", te)
        ve = p.emit("vector", lambda v: v.tensor_copy(bc["logd"][:], p_trf[:, 0:1]), inc="ve")
        p.drain("vector")
        p.wait("tensor", "ve", ve)
        te = p.emit("tensor", lambda t: t.matmul(p_trf[:, 0:1], ones_f[:], sc11["oscale"][:],
                                                 start=True, stop=True), inc="te")
        p.wait("vector", "te", te)
        ve = p.emit("vector", lambda v: v.tensor_copy(bc["os"][:], p_trf[:, 0:1]), inc="ve")
        p.wait("tensor", "ve", ve)

        p.emit("gpsimd", lambda g: g.iota(Dcol[:], pattern=[[1, S]], base=0,
                                          channel_multiplier=0,
                                          allow_small_or_imprecise_dtypes=True), inc="gp")
        p.emit("gpsimd", lambda g: g.iota(maskE[:], pattern=[[1, S]], base=-1,
                                          channel_multiplier=-1,
                                          allow_small_or_imprecise_dtypes=True), inc="gp")
        p.emit("gpsimd", lambda g: g.iota(bc["drev"][:], pattern=[[0, 1]], base=S - 1,
                                          channel_multiplier=-1,
                                          allow_small_or_imprecise_dtypes=True), inc="gp")
        gp_io = p.cnt["gp"]
        p.wait("vector", "gp", gp_io)
        p.emit("vector", lambda v: v.tensor_scalar(out=binm[:], in0=maskE[:],
                                                   scalar1=0.0, scalar2=None, op0=Op.is_ge), inc="ve")
        p.drain("vector")
        p.emit("vector", lambda v: v.tensor_tensor(out=maskE[:], in0=maskE[:], in1=binm[:],
                                                   op=Op.mult), inc="ve")
        p.wait("scalar", "gp", gp_io)
        p.wait("scalar", "ve", p.cnt["ve"])
        p.emit("scalar", lambda s: s.activation(out=Dcol[:], in_=Dcol[:], func=AF.Exp,
                                                scale=bc["logd"][:, 0:1]), inc="se")
        p.emit("scalar", lambda s: s.activation(out=maskE[:], in_=maskE[:], func=AF.Exp,
                                                scale=bc["logd"][:, 0:1]), inc="se")
        p.emit("scalar", lambda s: s.activation(out=bc["drev"][:], in_=bc["drev"][:], func=AF.Exp,
                                                scale=bc["logd"][:, 0:1]), inc="se")
        p.wait("vector", "se", p.cnt["se"])
        p.emit("vector", lambda v: v.tensor_tensor(out=mask_f[:], in0=maskE[:], in1=binm[:],
                                                   op=Op.mult), inc="ve")
        p.emit("vector", lambda v: v.tensor_scalar(out=bc["dS"][:], in0=bc["logd"][:],
                                                   scalar1=float(S), scalar2=None, op0=Op.mult), inc="ve")
        p.wait("scalar", "ve", p.cnt["ve"])
        p.emit("scalar", lambda s: s.activation(out=bc["dS"][:], in_=bc["dS"][:], func=AF.Exp), inc="se")
        p.emit("vector", lambda v: v.memset(bc["eps"][:], 1e-5), inc="ve")
        p.emit("vector", lambda v: v.memset(M_f[:], 0.0), inc="ve")
        p.emit("vector", lambda v: v.memset(M_b[:], 0.0), inc="ve")
        setup_se = p.cnt["se"]
        setup_ve = p.cnt["ve"]
        p.wait("tensor", "se", setup_se)
        p.wait("tensor", "ve", setup_ve)
        p.wait("sync", "ve", setup_ve)
        xf_free = [None, None]
        last_x1_store = 0
        for c in range(n_all):
            own = c >= n_own0
            par = c % 2
            t0 = c * S
            if xf_free[par] is not None:
                p.wait("sync", "ve", xf_free[par])
            p.dma("sync", xf[par][:], x[t0:t0 + S, :])
            ld_x = p.cnt["ld"]
            p.wait("vector", "ld", ld_x)
            ve = p.emit("vector", lambda v, par=par: v.tensor_copy(xb[:], xf[par][:]), inc="ve")
            xf_free[par] = ve
            p.wait("tensor", "ve", ve)
            for j in range(8):
                te = p.emit("tensor", lambda t, j=j: t.transpose(
                    p_trb[j % 2][:], xb[:, j * 128:(j + 1) * 128], ident_b[:]), inc="te")
                p.wait("vector", "te", te)
                ve = p.emit("vector", lambda v, j=j: v.tensor_copy(
                    xT[:, j * S:(j + 1) * S], p_trb[j % 2][:]), inc="ve")
                if j >= 1:
                    p.wait("tensor", "ve", ve - 1)
            p.wait("tensor", "ve", p.cnt["ve"])

            names = ["k", "v"] + (["q"] if own else [])
            for n in names:
                for j in range(8):
                    te = p.emit("tensor", lambda t, n=n, j=j: t.matmul(
                        p_proj[:], w_st[n][:, j * 128:(j + 1) * 128],
                        xT[:, j * S:(j + 1) * S],
                        start=(j == 0), stop=(j == 7)), inc="te")
                p.wait("vector", "te", p.cnt["te"])
                if n == "k":
                    ve = p.emit("vector", lambda v: v.tensor_copy(kT_b[:], p_proj[:]), inc="ve")
                elif n == "v":
                    ve = p.emit("vector", lambda v: v.tensor_copy(vT_b[:], p_proj[:]), inc="ve")
                else:
                    p.emit("vector", lambda v: v.tensor_copy(qT_b[:], p_proj[:]), inc="ve")
                    ve = p.emit("vector", lambda v: v.tensor_tensor(
                        out=qTs_b[:], in0=p_proj[:], in1=Dcol[:], op=Op.mult), inc="ve")
                p.wait("tensor", "ve", ve)
                if n in "kv":
                    te = p.emit("tensor", lambda t, n=n: t.transpose(
                        p_trb[0][:], kT_b[:] if n == "k" else vT_b[:], ident_b[:]), inc="te")
                    p.wait("vector", "te", te)
                    if n == "k":
                        ve = p.emit("vector", lambda v: v.tensor_scalar(
                            out=k_s[:], in0=p_trb[0][:], scalar1=bc["drev"][:, 0:1],
                            scalar2=None, op0=Op.mult), inc="ve")
                    else:
                        ve = p.emit("vector", lambda v: v.tensor_copy(v_b[:], p_trb[0][:]), inc="ve")
                    p.wait("tensor", "ve", ve)

            if own:
                te = p.emit("tensor", lambda t: t.matmul(p_sc[:], kT_b[:], qT_b[:],
                                                         start=True, stop=True), inc="te")
                p.wait("vector", "te", te)
                ve = p.emit("vector", lambda v: v.tensor_tensor(out=P_b[:], in0=p_sc[:],
                                                                in1=mask_f[:], op=Op.mult), inc="ve")
                p.wait("tensor", "ve", ve)
                p.emit("tensor", lambda t: t.matmul(p_ret[:], v_b[:], P_b[:],
                                                    start=True, stop=False), inc="te")
                te = p.emit("tensor", lambda t: t.matmul(p_ret[:], M_b[:], qTs_b[:],
                                                         start=False, stop=True), inc="te")
                p.wait("vector", "te", te)
                ve = p.emit("vector", lambda v: v.tensor_copy(ret_b[:], p_ret[:]), inc="ve")
                p.wait("tensor", "ve", ve)
                for blk in range(2):
                    te = p.emit("tensor", lambda t, blk=blk: t.matmul(
                        p_o[:, blk * 512:(blk + 1) * 512], ret_b[:],
                        w_T_b["o"][:, blk * 512:(blk + 1) * 512],
                        start=True, stop=True), inc="te")
                p.wait("vector", "te", p.cnt["te"])
                xf_free[par] = p.emit("vector", lambda v, par=par: v.scalar_tensor_tensor(
                    out=t_x1[:], in0=p_o[:], scalar=bc["os"][:, 0:1], in1=xf[par][:],
                    op0=Op.mult, op1=Op.add), inc="ve")
                p.drain("vector")
                p.emit("vector", lambda v: v.tensor_reduce(out=bc["s1"][:], in_=t_x1[:],
                                                           axis=AX.X, op=Op.add), inc="ve")
                p.drain("vector")
                p.emit("vector", lambda v: v.tensor_scalar(
                    out=bc["mean"][:], in0=bc["s1"][:], scalar1=1.0 / D, scalar2=None,
                    op0=Op.mult), inc="ve")
                p.drain("vector")
                ve = p.emit("vector", lambda v: v.tensor_scalar(
                    out=t_x1[:], in0=t_x1[:], scalar1=bc["mean"][:, 0:1], scalar2=None,
                    op0=Op.subtract), inc="ve")
                p.wait("scalar", "ve", ve)
                se = p.emit("scalar", lambda s: s.activation(out=scr[:], in_=t_x1[:],
                                                             func=AF.Square,
                                                             accum_out=bc["v2"][:, 0:1]), inc="se")
                p.drain("scalar")
                p.emit("scalar", lambda s: s.activation(out=bc["lnv"][:], in_=bc["v2"][:],
                                                        func=AF.Ln, scale=1.0 / D,
                                                        bias=bc["eps"][:, 0:1]), inc="se")
                p.drain("scalar")
                se = p.emit("scalar", lambda s: s.activation(out=bc["rstd"][:], in_=bc["lnv"][:],
                                                             func=AF.Exp, scale=-0.5), inc="se")
                p.wait("vector", "se", se)
                if last_x1_store:
                    p.wait("vector", "st", last_x1_store)
                ve = p.emit("vector", lambda v: v.tensor_scalar(
                    out=x1c[:], in0=t_x1[:], scalar1=bc["rstd"][:, 0:1], scalar2=None,
                    op0=Op.mult), inc="ve")
                p.drain("vector")
                p.wait("tensor", "ve", ve)
                for j in range(8):
                    te = p.emit("tensor", lambda t, j=j: t.transpose(
                        p_trf[:], x1c[:, j * 128:(j + 1) * 128], ident_f[:]), inc="te")
                    p.wait("vector", "te", te)
                    ve = p.emit("vector", lambda v, j=j: v.tensor_copy(
                        x1tc[:, j * S:(j + 1) * S], p_trf[:]), inc="ve")
                    p.wait("tensor", "ve", ve)
                p.wait("sync", "ve", ve)
                last_x1_store = p.dma(
                    "sync", x1T[:, t0 - own_start:t0 - own_start + S].rearrange(
                        "(j p) t -> p j t", j=8),
                    x1tc.ap().rearrange("p (j t) -> p j t", j=8), inc="st")

            te = p.emit("tensor", lambda t: t.matmul(p_sc[:], k_s[:], v_b[:],
                                                     start=True, stop=True), inc="te")
            p.wait("vector", "te", te)
            p.emit("vector", lambda v: v.tensor_scalar(out=M_f[:], in0=M_f[:],
                                                       scalar1=bc["dS"][:, 0:1], scalar2=None,
                                                       op0=Op.mult), inc="ve")
            p.drain("vector")
            p.emit("vector", lambda v: v.tensor_tensor(out=M_f[:], in0=M_f[:], in1=p_sc[:],
                                                       op=Op.add), inc="ve")
            p.drain("vector")
            ve = p.emit("vector", lambda v: v.tensor_copy(M_b[:], M_f[:]), inc="ve")
            p.wait("tensor", "ve", ve)

    return



def emit_ffn(nc, p, E, pre_ln, T_own, xT, W1, W2, b1r, b2, rW, rb, outT,
             g_r=None, b_r=None, debug=False):
    n_q = T_own // Q
    n_ch = Q // CH

    with ExitStack() as st_:
        e = st_.enter_context
        xq = e(nc.sbuf_tensor("xq", [128, 8 * Q], F32))
        zb = e(nc.sbuf_tensor("zb", [128, 8 * Q], BF16))
        mixT = e(nc.sbuf_tensor("mixT", [128, 8 * Q], F32))
        w1b = e(nc.sbuf_tensor("w1b", [128, 8 * HF], BF16))
        w2b = e(nc.sbuf_tensor("w2b", [128, NF * D], BF16))
        hid = e(nc.sbuf_tensor("hid", [128, NF * CH], BF16))
        b1s = e(nc.sbuf_tensor("b1s", [128, E * NF], F32))
        b2s = e(nc.sbuf_tensor("b2s", [E, D], BF16))
        rWs = e(nc.sbuf_tensor("rWs", [128, 8 * E], F32))
        rbs = e(nc.sbuf_tensor("rbs", [1, E], F32))
        rb_bc = e(nc.sbuf_tensor("rb_bc", [128, E], F32))
        if pre_ln:
            gs = e(nc.sbuf_tensor("gs", [128, 8], F32))
            bs = e(nc.sbuf_tensor("bs", [128, 8], F32))
        ident_f = e(nc.sbuf_tensor("ident_f", [128, 128], F32))
        ones_f = e(nc.sbuf_tensor("ones_f", [1, 128], F32))
        ones_col = e(nc.sbuf_tensor("ones_col", [128, 1], F32))
        scr_g = [e(nc.sbuf_tensor(f"scr_g{i}", [128, CH], F32)) for i in range(2)]
        scr2 = e(nc.sbuf_tensor("scr2", [128, CH], F32))
        lgacc = e(nc.sbuf_tensor("lgacc", [128, 4 * E], F32))
        st2 = e(nc.sbuf_tensor("st2", [1, CH], F32))
        mean1 = e(nc.sbuf_tensor("mean1", [1, CH], F32))
        nmean1 = e(nc.sbuf_tensor("nmean1", [1, CH], F32))
        var1 = e(nc.sbuf_tensor("var1", [1, CH], F32))
        rstd1 = e(nc.sbuf_tensor("rstd1", [1, CH], F32))
        nmr1 = e(nc.sbuf_tensor("nmr1", [1, CH], F32))
        eps_t = e(nc.sbuf_tensor("eps_t", [1, 1], F32))
        nm_bc = e(nc.sbuf_tensor("nm_bc", [128, CH], F32))
        rs_bc = e(nc.sbuf_tensor("rs_bc", [128, CH], F32))
        lg = e(nc.sbuf_tensor("lg", [128, E], F32))
        m1 = e(nc.sbuf_tensor("m1", [128, 1], F32))
        nm1 = e(nc.sbuf_tensor("nm1", [128, 1], F32))
        l2 = e(nc.sbuf_tensor("l2", [128, E], F32))
        m2 = e(nc.sbuf_tensor("m2", [128, 1], F32))
        d21 = e(nc.sbuf_tensor("d21", [128, 1], F32))
        em = e(nc.sbuf_tensor("em", [128, 1], F32))
        den = e(nc.sbuf_tensor("den", [128, 1], F32))
        rden = e(nc.sbuf_tensor("rden", [128, 1], F32))
        sexp = e(nc.sbuf_tensor("sexp", [128, E], F32))
        ind = e(nc.sbuf_tensor("ind", [128, E], F32))
        wv = e(nc.sbuf_tensor("wv", [128, E], F32))
        wT_b = e(nc.sbuf_tensor("wT_b", [E, Q], BF16))
        w_bc = e(nc.sbuf_tensor("w_bc", [128, CH], F32))
        sel_f = e(nc.sbuf_tensor("sel_f", [E, E * 128], BF16))
        out2 = [e(nc.sbuf_tensor(f"out2_{i}", [128, CH], F32)) for i in range(2)]

        p_stat = e(nc.psum_tensor("p_stat", [128, CH], F32))
        p_st1 = p_stat[0:1, :]
        p_st2 = p_stat[64:65, :]
        p_bc = e(nc.psum_tensor("p_bc", [128, CH], F32))
        p_hid = [e(nc.psum_tensor(f"p_hid{i}", [128, CH], F32)) for i in range(3)]
        p_mix = [e(nc.psum_tensor(f"p_mix{i}", [128, CH], F32)) for i in range(2)]
        p_r = e(nc.psum_tensor("p_r", [128, 128], F32))


        p.dma("sync", b1s[:], b1r)
        p.dma("gpsimd", b2s[:], b2, inc="ld2")
        p.dma("gpsimd", rWs.ap().rearrange("p (j e) -> p j e", j=8),
              rW.rearrange("(j p) e -> p j e", j=8), inc="ld2")
        p.dma("sync", rbs[:], rb)
        if pre_ln:
            p.dma("sync", gs[:], g_r)
            p.dma("sync", bs[:], b_r)
        ld0 = p.cnt["ld"]
        ld20 = p.cnt["ld2"]
        p.emit("gpsimd", lambda g: g.iota(ident_f[:], pattern=[[-1, 128]], base=0,
                                          channel_multiplier=1,
                                          allow_small_or_imprecise_dtypes=True), inc="gp")
        p.emit("gpsimd", lambda g: g.iota(sel_f[:], pattern=[[-1, E], [0, 128]], base=0,
                                          channel_multiplier=1,
                                          allow_small_or_imprecise_dtypes=True), inc="gp")
        p.wait("vector", "gp", p.cnt["gp"])
        p.emit("vector", lambda v: v.tensor_scalar(out=sel_f[:], in0=sel_f[:],
                                                   scalar1=0.0, scalar2=None,
                                                   op0=Op.is_equal), inc="ve")
        p.drain("vector")
        p.emit("vector", lambda v: v.tensor_scalar(out=ident_f[:], in0=ident_f[:],
                                                   scalar1=0.0, scalar2=None,
                                                   op0=Op.is_equal), inc="ve")
        p.drain("vector")
        p.emit("vector", lambda v: v.memset(ones_f[:], 1.0), inc="ve")
        p.emit("vector", lambda v: v.memset(ones_col[:], 1.0), inc="ve")
        p.emit("vector", lambda v: v.memset(eps_t[:], EPS), inc="ve")
        ve0 = p.cnt["ve"]
        p.wait("tensor", "ld", ld0)
        p.wait("tensor", "ve", ve0)
        te = p.emit("tensor", lambda t: t.matmul(p_r[:, 0:E], ones_f[:], rbs[:],
                                                 start=True, stop=True), inc="te")
        p.wait("vector", "te", te)
        ve = p.emit("vector", lambda v: v.tensor_copy(rb_bc[:], p_r[:, 0:E]), inc="ve")
        p.drain("vector")
        p.wait("tensor", "ve", ve)

        def ln_stats(tile_of):
            aps = [tile_of(j) for j in range(8)]
            p.wait("tensor", "ve", p.cnt["ve"])
            p.wait("scalar", "ve", p.cnt["ve"])
            for j in range(8):
                p.emit("tensor", lambda t, j=j, ap=aps[j]: t.matmul(
                    p_st1, ones_col[:], ap,
                    start=(j == 0), stop=(j == 7)), inc="te")
            last_sq_mm = None
            for j in range(8):
                if last_sq_mm is not None:
                    p.wait("scalar", "te", last_sq_mm)
                se = p.emit("scalar", lambda s, ap=aps[j]: s.activation(
                    out=scr2[:], in_=ap, func=AF.Square), inc="se")
                p.wait("tensor", "se", se)
                last_sq_mm = p.emit("tensor", lambda t, j=j: t.matmul(
                    p_st2, ones_col[:], scr2[:],
                    start=(j == 0), stop=(j == 7)), inc="te")
            p.wait("vector", "te", p.cnt["te"])
            p.emit("vector", lambda v: v.tensor_scalar(out=mean1[:], in0=p_st1,
                                                       scalar1=1.0 / D, scalar2=None,
                                                       op0=Op.mult), inc="ve")
            p.emit("vector", lambda v: v.tensor_scalar(out=st2[:], in0=p_st2,
                                                       scalar1=1.0 / D, scalar2=None,
                                                       op0=Op.mult), inc="ve")
            p.drain("vector")
            p.emit("vector", lambda v: v.tensor_tensor(out=var1[:], in0=mean1[:], in1=mean1[:],
                                                       op=Op.mult), inc="ve")
            p.drain("vector")
            p.emit("vector", lambda v: v.tensor_tensor(out=var1[:], in0=st2[:], in1=var1[:],
                                                       op=Op.subtract), inc="ve")
            p.emit("vector", lambda v: v.tensor_scalar(out=nmean1[:], in0=mean1[:],
                                                       scalar1=-1.0, scalar2=None,
                                                       op0=Op.mult), inc="ve")
            ve = p.cnt["ve"]
            p.drain("vector")
            p.wait("scalar", "ve", ve)
            p.emit("scalar", lambda s: s.activation(out=rstd1[:], in_=var1[:], func=AF.Ln,
                                                    bias=eps_t[:, 0:1]), inc="se")
            p.drain("scalar")
            se = p.emit("scalar", lambda s: s.activation(out=rstd1[:], in_=rstd1[:],
                                                         func=AF.Exp, scale=-0.5), inc="se")
            p.wait("tensor", "se", se)
            p.wait("tensor", "ve", p.cnt["ve"])
            for src, dst in ((nmean1, nm_bc), (rstd1, rs_bc)):
                te = p.emit("tensor", lambda t, src=src: t.matmul(
                    p_bc[:], ones_f[:], src[:], start=True, stop=True), inc="te")
                p.wait("vector", "te", te)
                ve = p.emit("vector", lambda v, dst=dst: v.tensor_copy(dst[:], p_bc[:]), inc="ve")
                p.drain("vector")
                p.wait("tensor", "ve", ve)

        last_store = 0
        w1_prev = None
        w2_prev = None
        prev_gelu = {0: None, 1: None, 2: None}
        prev_add = {0: None, 1: None}
        for q in range(n_q):
            tq = q * Q
            p.wait("sync", "ve", p.cnt["ve"])
            p.wait("sync", "te", p.cnt["te"])
            p.dma("sync", xq.ap().rearrange("p (j t) -> p j t", j=8),
                  xT[:, tq:tq + Q].rearrange("(j p) t -> p j t", j=8))
            ldq = p.cnt["ld"]
            p.wait("vector", "ld", ldq)
            p.wait("tensor", "ld", ldq)
            p.wait("scalar", "ld", ldq)
            p.wait("vector", "te", p.cnt["te"])

            for ch in range(n_ch):
                co = ch * CH

                def xq_tile(j, co=co):
                    return xq[:, j * Q + co: j * Q + co + CH]
                if pre_ln:
                    ln_stats(xq_tile)
                    p.emit("vector", lambda v: v.tensor_tensor(out=nmr1[:], in0=nmean1[:],
                                                               in1=rstd1[:], op=Op.mult), inc="ve")
                    ve = p.cnt["ve"]
                    p.drain("vector")
                    p.wait("tensor", "ve", ve)
                    te = p.emit("tensor", lambda t: t.matmul(p_bc[:], ones_f[:], nmr1[:],
                                                             start=True, stop=True), inc="te")
                    p.wait("vector", "te", te)
                    ve = p.emit("vector", lambda v: v.tensor_copy(nm_bc[:], p_bc[:]), inc="ve")
                    p.drain("vector")
                    p.wait("tensor", "ve", ve)
                p.wait("tensor", "ld2", ld20)
                rbank = [p_r, p_bc]
                for j in range(8):
                    if pre_ln:
                        p.emit("vector", lambda v, ap=xq_tile(j): v.tensor_tensor(
                            out=scr2[:], in0=ap, in1=rs_bc[:], op=Op.mult), inc="ve")
                        p.drain("vector")
                        p.emit("vector", lambda v: v.tensor_tensor(
                            out=scr2[:], in0=scr2[:], in1=nm_bc[:], op=Op.add), inc="ve")
                        p.drain("vector")
                        p.emit("vector", lambda v, j=j: v.tensor_scalar(
                            out=scr2[:], in0=scr2[:],
                            scalar1=gs[:, j:j + 1], scalar2=bs[:, j:j + 1],
                            op0=Op.mult, op1=Op.add), inc="ve")
                        p.drain("vector")
                    else:
                        p.emit("vector", lambda v, ap=xq_tile(j): v.tensor_copy(
                            scr2[:], ap), inc="ve")
                        p.drain("vector")
                    ve_z = p.emit("vector", lambda v, j=j, co=co: v.tensor_copy(
                        zb[:, j * Q + co: j * Q + co + CH], scr2[:]), inc="ve")
                    p.drain("vector")
                    p.wait("tensor", "ve", ve_z)
                    for sub in range(4):
                        bk = rbank[sub % 2]
                        te = p.emit("tensor", lambda t, j=j, sub=sub, bk=bk: t.matmul(
                            bk[:, 0:E], scr2[:, sub * 128:(sub + 1) * 128],
                            rWs[:, j * E:(j + 1) * E],
                            start=True, stop=True), inc="te")
                        p.wait("vector", "te", te)
                        if j == 0:
                            ve = p.emit("vector", lambda v, sub=sub, bk=bk: v.tensor_copy(
                                lgacc[:, sub * E:(sub + 1) * E], bk[:, 0:E]), inc="ve")
                        else:
                            ve = p.emit("vector", lambda v, sub=sub, bk=bk: v.tensor_tensor(
                                out=lgacc[:, sub * E:(sub + 1) * E],
                                in0=lgacc[:, sub * E:(sub + 1) * E],
                                in1=bk[:, 0:E], op=Op.add), inc="ve")
                        p.drain("vector")
                        if sub >= 1:
                            p.wait("tensor", "ve", ve - 1)
                    p.wait("tensor", "ve", p.cnt["ve"])

                for sc_ in range(CH // 128):
                    so = co + sc_ * 128
                    p.emit("vector", lambda v, sc_=sc_: v.tensor_tensor(
                        out=lg[:], in0=lgacc[:, sc_ * E:(sc_ + 1) * E],
                        in1=rb_bc[:], op=Op.add), inc="ve")
                    ve = p.cnt["ve"]
                    p.drain("vector")
                    p.emit("vector", lambda v: v.reduce_max(m1[:], lg[:], axis=AX.X), inc="ve")
                    p.drain("vector")
                    p.emit("vector", lambda v: v.tensor_scalar(out=nm1[:], in0=m1[:],
                                                               scalar1=-1.0, scalar2=None,
                                                               op0=Op.mult), inc="ve")
                    p.emit("vector", lambda v: v.scalar_tensor_tensor(
                        out=l2[:], in0=lg[:], scalar=m1[:, 0:1], in1=lg[:],
                        op0=Op.is_ge, op1=Op.bypass), inc="ve")
                    p.drain("vector")
                    p.emit("vector", lambda v: v.scalar_tensor_tensor(
                        out=l2[:], in0=l2[:], scalar=-1e30, in1=lg[:],
                        op0=Op.mult, op1=Op.add), inc="ve")
                    p.drain("vector")
                    p.emit("vector", lambda v: v.reduce_max(m2[:], l2[:], axis=AX.X), inc="ve")
                    p.drain("vector")
                    p.emit("vector", lambda v: v.tensor_scalar(out=d21[:], in0=m2[:],
                                                               scalar1=m1[:, 0:1], scalar2=None,
                                                               op0=Op.subtract), inc="ve")
                    p.emit("vector", lambda v: v.tensor_scalar(out=ind[:], in0=lg[:],
                                                               scalar1=m2[:, 0:1], scalar2=None,
                                                               op0=Op.is_ge), inc="ve")
                    ve = p.cnt["ve"]
                    p.drain("vector")
                    p.wait("scalar", "ve", ve)
                    p.emit("scalar", lambda s: s.activation(out=sexp[:], in_=lg[:], func=AF.Exp,
                                                            bias=nm1[:, 0:1]), inc="se")
                    p.drain("scalar")
                    se = p.emit("scalar", lambda s: s.activation(out=em[:], in_=d21[:],
                                                                 func=AF.Exp), inc="se")
                    p.wait("vector", "se", se)
                    p.emit("vector", lambda v: v.tensor_scalar(out=den[:], in0=em[:],
                                                               scalar1=1.0, scalar2=None,
                                                               op0=Op.add), inc="ve")
                    p.drain("vector")
                    p.emit("vector", lambda v: v.reciprocal(rden[:], den[:]), inc="ve")
                    p.drain("vector")
                    p.emit("vector", lambda v: v.tensor_tensor(out=wv[:], in0=sexp[:], in1=ind[:],
                                                               op=Op.mult), inc="ve")
                    p.drain("vector")
                    ve = p.emit("vector", lambda v: v.tensor_scalar(out=wv[:], in0=wv[:],
                                                                    scalar1=rden[:, 0:1],
                                                                    scalar2=None, op0=Op.mult), inc="ve")
                    p.drain("vector")
                    p.wait("tensor", "ve", ve)
                    te = p.emit("tensor", lambda t: t.transpose(p_r[0:E, 0:128], wv[:, 0:E],
                                                                ident_f[:]), inc="te")
                    p.wait("vector", "te", te)
                    ve = p.emit("vector", lambda v, so=so: v.tensor_copy(
                        wT_b[0:E, so:so + 128], p_r[0:E, 0:128]), inc="ve")
                    p.drain("vector")
                    p.wait("tensor", "ve", ve)


            for ch in range(n_ch):
                co = ch * CH
                for h in range(8):
                    p.wait("tensor", "ve", p.cnt["ve"])
                    p.emit("tensor", lambda t, h=h, co=co: t.matmul(
                        p_mix[h % 2][:], b2s[:, h * 128:(h + 1) * 128],
                        wT_b[0:E, co:co + CH], start=True, stop=True), inc="te")
                    p.wait("vector", "te", p.cnt["te"])
                    p.emit("vector", lambda v, h=h, co=co: v.tensor_copy(
                        mixT[:, h * Q + co:h * Q + co + CH], p_mix[h % 2][:]), inc="ve")
            p.wait("tensor", "ve", p.cnt["ve"])


            for ex in range(E):
                if w1_prev is not None:
                    p.wait("gpsimd", "te", w1_prev)
                p.dma("gpsimd", w1b.ap().rearrange("p (j f) -> p j f", j=8),
                      W1[ex].rearrange("(j p) f -> p j f", j=8), inc="ld2")
                w1_ld = p.cnt["ld2"]
                if w2_prev is not None:
                    p.wait("gpsimd", "te", w2_prev)
                p.dma("gpsimd", w2b.ap().rearrange("p (j d) -> p j d", j=NF),
                      W2[ex].rearrange("(j p) d -> p j d", j=NF), inc="ld2")
                w2_ld = p.cnt["ld2"]
                p.wait("tensor", "ld2", w1_ld)
                for ch in range(n_ch):
                    co = ch * CH
                    p.wait("tensor", "ve", p.cnt["ve"])
                    te = p.emit("tensor", lambda t, ex=ex, co=co: t.matmul(
                        p_bc[:], sel_f[0:E, ex * 128:(ex + 1) * 128],
                        wT_b[0:E, co:co + CH],
                        start=True, stop=True), inc="te")
                    p.wait("vector", "te", te)
                    ve = p.emit("vector", lambda v: v.tensor_copy(w_bc[:], p_bc[:]), inc="ve")
                    p.drain("vector")
                    p.wait("tensor", "ve", ve)
                    mult_of = {}
                    for f in range(NF):
                        if prev_gelu[f % 3] is not None:
                            p.wait("tensor", "se", prev_gelu[f % 3])
                        for j in range(8):
                            p.emit("tensor", lambda t, f=f, j=j, co=co: t.matmul(
                                p_hid[f % 3][:],
                                w1b[:, j * HF + f * 128: j * HF + f * 128 + 128],
                                zb[:, j * Q + co: j * Q + co + CH],
                                start=(j == 0), stop=(j == 7)), inc="te")
                        p.wait("scalar", "te", p.cnt["te"])
                        if f >= 2:
                            p.wait("scalar", "ve", mult_of[f - 2])
                        se = p.emit("scalar", lambda s, f=f, ex=ex: s.activation(
                            out=scr_g[f % 2][:], in_=p_hid[f % 3][:], func=AF.Gelu,
                            bias=b1s[:, ex * NF + f: ex * NF + f + 1]), inc="se")
                        prev_gelu[f % 3] = se
                        p.wait("vector", "se", se)
                        mult_of[f] = p.emit("vector", lambda v, f=f: v.tensor_tensor(
                            out=hid[:, f * CH:(f + 1) * CH], in0=scr_g[f % 2][:],
                            in1=w_bc[:], op=Op.mult), inc="ve")
                    p.wait("tensor", "ve", p.cnt["ve"])
                    if ch == n_ch - 1:
                        w1_prev = p.cnt["te"]
                    p.wait("tensor", "ld2", w2_ld)
                    for h in range(8):
                        if prev_add[h % 2] is not None:
                            p.wait("tensor", "ve", prev_add[h % 2])
                        for f in range(NF):
                            p.emit("tensor", lambda t, h=h, f=f: t.matmul(
                                p_mix[h % 2][:],
                                w2b[:, f * D + h * 128: f * D + h * 128 + 128],
                                hid[:, f * CH:(f + 1) * CH],
                                start=(f == 0), stop=(f == NF - 1)), inc="te")
                        p.wait("vector", "te", p.cnt["te"])
                        prev_add[h % 2] = p.emit("vector", lambda v, h=h, co=co: v.tensor_tensor(
                            out=mixT[:, h * Q + co:h * Q + co + CH],
                            in0=mixT[:, h * Q + co:h * Q + co + CH],
                            in1=p_mix[h % 2][:], op=Op.add), inc="ve")
                    p.wait("tensor", "ve", p.cnt["ve"])
                w2_prev = p.cnt["te"]


            p.drain("vector")
            for ch in range(n_ch):
                co = ch * CH
                for j in range(8):
                    p.emit("vector", lambda v, j=j, co=co: v.scalar_tensor_tensor(
                        out=mixT[:, j * Q + co:j * Q + co + CH],
                        in0=xq[:, j * Q + co:j * Q + co + CH], scalar=2.0,
                        in1=mixT[:, j * Q + co:j * Q + co + CH],
                        op0=Op.mult, op1=Op.add), inc="ve")
                p.drain("vector")
                p.wait("tensor", "ve", p.cnt["ve"])

                def mix_tile(j, co=co):
                    return mixT[:, j * Q + co: j * Q + co + CH]
                ln_stats(mix_tile)
                st_of = {}
                for j in range(8):
                    if j >= 2:
                        p.wait("vector", "st", st_of[j - 2])
                    elif last_store:
                        p.wait("vector", "st", last_store)
                    p.emit("vector", lambda v, ap=mix_tile(j): v.tensor_tensor(
                        out=scr2[:], in0=ap, in1=nm_bc[:], op=Op.add), inc="ve")
                    p.drain("vector")
                    ve = p.emit("vector", lambda v, j=j: v.tensor_tensor(
                        out=out2[j % 2][:], in0=scr2[:], in1=rs_bc[:],
                        op=Op.mult), inc="ve")
                    p.drain("vector")
                    p.wait("sync", "ve", ve)
                    st_of[j] = p.dma(
                        "sync", outT[j * 128:(j + 1) * 128, tq + co:tq + co + CH],
                        out2[j % 2][:], inc="st")
                last_store = st_of[7]

    return


def emit_mout(nc, p, T_own, y2T, x2T, opT, x3T):
    n_ch = T_own // CH
    with ExitStack() as st_:
        e = st_.enter_context
        opb = e(nc.sbuf_tensor("opb", [128, 16 * D], BF16))
        yq = e(nc.sbuf_tensor("yq", [128, 16 * CH], F32))
        yb = e(nc.sbuf_tensor("yb", [128, 16 * CH], BF16))
        xq = e(nc.sbuf_tensor("xq", [128, 8 * CH], F32))
        mix = e(nc.sbuf_tensor("mix", [128, 8 * CH], F32))
        scr2 = e(nc.sbuf_tensor("scr2", [128, CH], F32))
        ones_col = e(nc.sbuf_tensor("ones_col", [128, 1], F32))
        ones_f = e(nc.sbuf_tensor("ones_f", [1, 128], F32))
        eps_t = e(nc.sbuf_tensor("eps_t", [1, 1], F32))
        st2 = e(nc.sbuf_tensor("st2", [1, CH], F32))
        mean1 = e(nc.sbuf_tensor("mean1", [1, CH], F32))
        nmean1 = e(nc.sbuf_tensor("nmean1", [1, CH], F32))
        var1 = e(nc.sbuf_tensor("var1", [1, CH], F32))
        rstd1 = e(nc.sbuf_tensor("rstd1", [1, CH], F32))
        nm_bc = e(nc.sbuf_tensor("nm_bc", [128, CH], F32))
        rs_bc = e(nc.sbuf_tensor("rs_bc", [128, CH], F32))
        out2 = [e(nc.sbuf_tensor(f"out2_{i}", [128, CH], F32)) for i in range(2)]
        p_st1 = e(nc.psum_tensor("p_st1", [1, CH], F32))
        p_st2 = e(nc.psum_tensor("p_st2", [1, CH], F32))
        p_bc = e(nc.psum_tensor("p_bc", [128, CH], F32))
        p_o = [e(nc.psum_tensor(f"p_o{i}", [128, CH], F32)) for i in range(2)]

        p.dma("gpsimd", opb.ap().rearrange("p (j d) -> p j d", j=16),
              opT.rearrange("(j p) d -> p j d", j=16), inc="ld2")
        ld2w = p.cnt["ld2"]
        p.emit("vector", lambda v: v.memset(ones_col[:], 1.0), inc="ve")
        p.emit("vector", lambda v: v.memset(ones_f[:], 1.0), inc="ve")
        p.emit("vector", lambda v: v.memset(eps_t[:], LN_EPS), inc="ve")
        ve0 = p.cnt["ve"]
        p.wait("tensor", "ve", ve0)
        p.wait("tensor", "ld2", ld2w)

        def ln_stats(aps):
            p.wait("tensor", "ve", p.cnt["ve"])
            p.wait("scalar", "ve", p.cnt["ve"])
            for j in range(8):
                p.emit("tensor", lambda t, ap=aps[j], j=j: t.matmul(
                    p_st1[:], ones_col[:], ap, start=(j == 0), stop=(j == 7)), inc="te")
            last_sq = None
            for j in range(8):
                if last_sq is not None:
                    p.wait("scalar", "te", last_sq)
                se = p.emit("scalar", lambda s, ap=aps[j]: s.activation(
                    out=scr2[:], in_=ap, func=AF.Square), inc="se")
                p.wait("tensor", "se", se)
                last_sq = p.emit("tensor", lambda t, j=j: t.matmul(
                    p_st2[:], ones_col[:], scr2[:], start=(j == 0), stop=(j == 7)), inc="te")
            p.wait("vector", "te", p.cnt["te"])
            p.emit("vector", lambda v: v.tensor_scalar(out=mean1[:], in0=p_st1[:],
                                                       scalar1=1.0 / D, scalar2=None,
                                                       op0=Op.mult), inc="ve")
            p.emit("vector", lambda v: v.tensor_scalar(out=st2[:], in0=p_st2[:],
                                                       scalar1=1.0 / D, scalar2=None,
                                                       op0=Op.mult), inc="ve")
            p.drain("vector")
            p.emit("vector", lambda v: v.tensor_tensor(out=var1[:], in0=mean1[:], in1=mean1[:],
                                                       op=Op.mult), inc="ve")
            p.drain("vector")
            p.emit("vector", lambda v: v.tensor_tensor(out=var1[:], in0=st2[:], in1=var1[:],
                                                       op=Op.subtract), inc="ve")
            p.emit("vector", lambda v: v.tensor_scalar(out=nmean1[:], in0=mean1[:],
                                                       scalar1=-1.0, scalar2=None,
                                                       op0=Op.mult), inc="ve")
            ve = p.cnt["ve"]
            p.drain("vector")
            p.wait("scalar", "ve", ve)
            p.emit("scalar", lambda s: s.activation(out=rstd1[:], in_=var1[:], func=AF.Ln,
                                                    bias=eps_t[:, 0:1]), inc="se")
            p.drain("scalar")
            se = p.emit("scalar", lambda s: s.activation(out=rstd1[:], in_=rstd1[:],
                                                         func=AF.Exp, scale=-0.5), inc="se")
            p.wait("tensor", "se", se)
            p.wait("tensor", "ve", p.cnt["ve"])
            for src_, dst in ((nmean1, nm_bc), (rstd1, rs_bc)):
                te = p.emit("tensor", lambda t, src_=src_: t.matmul(
                    p_bc[:], ones_f[:], src_[:], start=True, stop=True), inc="te")
                p.wait("vector", "te", te)
                ve = p.emit("vector", lambda v, dst=dst: v.tensor_copy(dst[:], p_bc[:]), inc="ve")
                p.drain("vector")
                p.wait("tensor", "ve", ve)

        last_store = 0
        for c in range(n_ch):
            t0 = c * CH
            p.wait("sync", "ve", p.cnt["ve"])
            p.wait("sync", "te", p.cnt["te"])
            p.dma("sync", yq.ap().rearrange("p (j t) -> p j t", j=16),
                  y2T[:, t0:t0 + CH].rearrange("(j p) t -> p j t", j=16))
            p.dma("sync", xq.ap().rearrange("p (j t) -> p j t", j=8),
                  x2T[:, t0:t0 + CH].rearrange("(j p) t -> p j t", j=8))
            ldc = p.cnt["ld"]
            p.wait("vector", "ld", ldc)
            for it in range(16):
                p.emit("vector", lambda v, it=it: v.tensor_copy(
                    yb[:, it * CH:(it + 1) * CH], yq[:, it * CH:(it + 1) * CH]), inc="ve")
            p.drain("vector")
            p.wait("tensor", "ve", p.cnt["ve"])
            for h in range(8):
                for it in range(16):
                    p.emit("tensor", lambda t, h=h, it=it: t.matmul(
                        p_o[h % 2][:], opb[:, it * D + h * 128: it * D + h * 128 + 128],
                        yb[:, it * CH:(it + 1) * CH],
                        start=(it == 0), stop=(it == 15)), inc="te")
                p.wait("vector", "te", p.cnt["te"])
                ve = p.emit("vector", lambda v, h=h: v.scalar_tensor_tensor(
                    out=mix[:, h * CH:(h + 1) * CH], in0=xq[:, h * CH:(h + 1) * CH],
                    scalar=2.0, in1=p_o[h % 2][:], op0=Op.mult, op1=Op.add), inc="ve")
                p.drain("vector")
                if h >= 1:
                    p.wait("tensor", "ve", ve - 1)
            p.wait("tensor", "ve", p.cnt["ve"])
            aps = [mix[:, j * CH:(j + 1) * CH] for j in range(8)]
            ln_stats(aps)
            st_of = {}
            for j in range(8):
                if j >= 2:
                    p.wait("vector", "st", st_of[j - 2])
                elif last_store:
                    p.wait("vector", "st", last_store)
                p.emit("vector", lambda v, ap=aps[j]: v.tensor_tensor(
                    out=scr2[:], in0=ap, in1=nm_bc[:], op=Op.add), inc="ve")
                p.drain("vector")
                ve = p.emit("vector", lambda v, j=j: v.tensor_tensor(
                    out=out2[j % 2][:], in0=scr2[:], in1=rs_bc[:], op=Op.mult), inc="ve")
                p.drain("vector")
                p.wait("sync", "ve", ve)
                st_of[j] = p.dma("sync", x3T[j * 128:(j + 1) * 128, t0:t0 + CH],
                                 out2[j % 2][:], inc="st")
            last_store = st_of[7]
    return


RMS_EPS = 1.1920929e-07
DTR = 64
def emit_minscan(nc, p, T, x2T, ipT, ipbf, convw_r, convb_r, xpT, dtpT, dtb_r,
                 normw_r, A_r, D_r, y2T, D=1024, DI=2048):
    n_ch = T // CH
    NIT = DI // 128
    NP = NIT // 2
    with ExitStack() as st_:
        e = st_.enter_context
        ipb4 = [e(nc.sbuf_tensor(f"ipb{i}", [128, 8 * 128], BF16)) for i in range(3)]
        xpb = e(nc.sbuf_tensor("xpb", [128, 16 * 112], BF16))
        dtpb = e(nc.sbuf_tensor("dtpb", [DTR, DI], BF16))
        cw = e(nc.sbuf_tensor("cw", [128, 16 * 4], F32))
        cb = e(nc.sbuf_tensor("cb", [128, 16], F32))
        dtb = e(nc.sbuf_tensor("dtb", [128, 16], F32))
        nw = e(nc.sbuf_tensor("nw", [128, 8], F32))
        A_s = e(nc.sbuf_tensor("A_s", [128, NIT * DS], F32))
        D_s = e(nc.sbuf_tensor("D_s", [128, NIT], F32))
        xq = e(nc.sbuf_tensor("xq", [128, 8 * CH], F32))
        xh3 = e(nc.sbuf_tensor("xh3", [128, 8 * 3], F32))
        xnb = e(nc.sbuf_tensor("xnb", [128, 8 * (CH + 3)], BF16))
        scrq = [e(nc.sbuf_tensor(f"scrq{i}", [128, CH], F32)) for i in range(2)]
        scr3 = e(nc.sbuf_tensor("scr3", [128, 3], F32))
        ones_col = e(nc.sbuf_tensor("ones_col", [128, 1], F32))
        ones_f = e(nc.sbuf_tensor("ones_f", [1, 128], F32))
        eps_t = e(nc.sbuf_tensor("eps_t", [1, 1], F32))
        ms1 = e(nc.sbuf_tensor("ms1", [1, CH], F32))
        rs_bc = e(nc.sbuf_tensor("rs_bc", [128, CH], F32))
        xsb2 = [e(nc.sbuf_tensor(f"xsb{i}", [128, CH + 3], F32)) for i in range(2)]
        uacc2 = [e(nc.sbuf_tensor(f"uacc{i}", [128, CH], F32)) for i in range(2)]
        ub = e(nc.sbuf_tensor("ub", [128, NIT * CH], BF16))
        zb = e(nc.sbuf_tensor("zb", [128, NIT * CH], BF16))
        dl = e(nc.sbuf_tensor("dl", [128, NIT * CH], F32))
        du = e(nc.sbuf_tensor("du", [128, NIT * CH], BF16))
        y_acc = e(nc.sbuf_tensor("y_acc", [128, NIT * CH], F32))
        dxb = e(nc.sbuf_tensor("dxb", [DTR, CH], BF16))
        bcb = e(nc.sbuf_tensor("bcb", [DS, 2 * CH], BF16))
        m2 = [e(nc.sbuf_tensor(f"m2_{i}", [128, CH], F32)) for i in range(2)]
        a2 = [e(nc.sbuf_tensor(f"a2_{i}", [128, CH], F32)) for i in range(2)]
        carry = e(nc.sbuf_tensor("carry", [128, NIT * DS], F32))
        sel16 = e(nc.sbuf_tensor("sel16", [DS, DS * 128], BF16))
        B_bc = e(nc.sbuf_tensor("B_bc", [128, CH], BF16))
        C_bc = e(nc.sbuf_tensor("C_bc", [128, CH], BF16))
        dAb = [e(nc.sbuf_tensor(f"dAb{i}", [128, CH], BF16)) for i in range(4)]
        dbu = [e(nc.sbuf_tensor(f"dbu{i}", [128, CH], BF16)) for i in range(2)]
        hh = [e(nc.sbuf_tensor(f"hh{i}", [128, CH], BF16)) for i in range(2)]
        tmp = [e(nc.sbuf_tensor(f"tmp{i}", [128, CH], BF16)) for i in range(4)]
        yy = [e(nc.sbuf_tensor(f"yy{i}", [128, CH], F32)) for i in range(2)]
        sz = [e(nc.sbuf_tensor(f"sz{i}", [128, CH], F32)) for i in range(2)]
        p_xz = [e(nc.psum_tensor(f"p_xz{i}", [128, CH], F32)) for i in range(2)]
        p_h3 = e(nc.psum_tensor("p_h3", [128, 8], F32))
        p_st = e(nc.psum_tensor("p_st", [1, CH], F32))
        p_bc = e(nc.psum_tensor("p_bc", [128, CH], F32))
        p_dbc = e(nc.psum_tensor("p_dbc", [112, CH], F32))
        p_b = e(nc.psum_tensor("p_b", [128, CH], F32))
        p_c = e(nc.psum_tensor("p_c", [128, CH], F32))

        p.dma("gpsimd", ipbf, ipT, inc="ld2")
        p.dma("gpsimd", xpb.ap().rearrange("p (j k) -> p j k", j=16),
              xpT.rearrange("(j p) k -> p j k", j=16), inc="ld2")
        p.dma("gpsimd", dtpb[:], dtpT, inc="ld2")
        ld2_w = p.cnt["ld2"]
        p.dma("sync", cw[:], convw_r)
        p.dma("sync", cb[:], convb_r)
        p.dma("sync", dtb[:], dtb_r)
        p.dma("sync", nw[:], normw_r)
        p.dma("sync", A_s[:], A_r)
        p.dma("sync", D_s[:], D_r)
        ld0 = p.cnt["ld"]
        p.emit("gpsimd", lambda g: g.iota(sel16[:], pattern=[[-1, DS], [0, 128]], base=0,
                                          channel_multiplier=1,
                                          allow_small_or_imprecise_dtypes=True), inc="gp")
        p.wait("vector", "gp", p.cnt["gp"])
        p.emit("vector", lambda v: v.tensor_scalar(out=sel16[:], in0=sel16[:],
                                                   scalar1=0.0, scalar2=None,
                                                   op0=Op.is_equal), inc="ve")
        p.drain("vector")
        p.emit("vector", lambda v: v.memset(carry[:], 0.0), inc="ve")
        p.emit("vector", lambda v: v.memset(ones_col[:], 1.0), inc="ve")
        p.emit("vector", lambda v: v.memset(ones_f[:], 1.0), inc="ve")
        p.emit("vector", lambda v: v.memset(eps_t[:], RMS_EPS), inc="ve")
        p.wait("scalar", "ld", ld0)
        se = p.emit("scalar", lambda s: s.activation(out=A_s[:], in_=A_s[:], func=AF.Exp), inc="se")
        p.wait("vector", "se", se)
        p.emit("vector", lambda v: v.tensor_scalar(out=A_s[:], in0=A_s[:], scalar1=-1.0,
                                                   scalar2=None, op0=Op.mult), inc="ve")
        p.drain("vector")
        ve0 = p.cnt["ve"]
        p.wait("tensor", "ve", ve0)
        p.wait("tensor", "ld", ld0)
        p.wait("scalar", "ve", ve0)
        p.wait("sync", "ld2", ld2_w)
        p.wait("vector", "ld2", ld2_w)
        p.wait("tensor", "ld2", ld2_w)

        p.dma("sync", xh3.ap().rearrange("p (j t) -> p j t", j=8),
              x2T[:, 0:3].rearrange("(j p) t -> p j t", j=8))
        ldh = p.cnt["ld"]
        p.wait("vector", "ld", ldh)
        p.wait("tensor", "ld", ldh)
        p.wait("scalar", "ld", ldh)
        last_sq = None
        for j in range(8):
            if last_sq is not None:
                p.wait("scalar", "te", last_sq)
            se = p.emit("scalar", lambda s, j=j: s.activation(
                out=scr3[:], in_=xh3[:, j * 3:(j + 1) * 3], func=AF.Square), inc="se")
            p.wait("tensor", "se", se)
            last_sq = p.emit("tensor", lambda t, j=j: t.matmul(
                p_st[:, 0:3], ones_col[:], scr3[:],
                start=(j == 0), stop=(j == 7)), inc="te")
        p.wait("vector", "te", p.cnt["te"])
        p.emit("vector", lambda v: v.tensor_scalar(out=ms1[:, 0:3], in0=p_st[:, 0:3],
                                                   scalar1=1.0 / D, scalar2=None,
                                                   op0=Op.mult), inc="ve")
        ve = p.cnt["ve"]
        p.drain("vector")
        p.wait("scalar", "ve", ve)
        p.emit("scalar", lambda s: s.activation(out=ms1[:, 0:3], in_=ms1[:, 0:3], func=AF.Ln,
                                                bias=eps_t[:, 0:1]), inc="se")
        p.drain("scalar")
        se = p.emit("scalar", lambda s: s.activation(out=ms1[:, 0:3], in_=ms1[:, 0:3],
                                                     func=AF.Exp, scale=-0.5), inc="se")
        p.wait("tensor", "se", se)
        te = p.emit("tensor", lambda t: t.matmul(p_bc[:, 0:3], ones_f[:], ms1[:, 0:3],
                                                 start=True, stop=True), inc="te")
        p.wait("vector", "te", te)
        ve = p.emit("vector", lambda v: v.tensor_copy(rs_bc[:, 0:3], p_bc[:, 0:3]), inc="ve")
        p.drain("vector")
        p.wait("tensor", "ve", ve)
        for j in range(8):
            p.emit("vector", lambda v, j=j: v.tensor_tensor(
                out=scr3[:], in0=xh3[:, j * 3:(j + 1) * 3],
                in1=rs_bc[:, 0:3], op=Op.mult), inc="ve")
            p.drain("vector")
            p.emit("vector", lambda v, j=j: v.tensor_scalar(
                out=xnb[:, j * (CH + 3): j * (CH + 3) + 3], in0=scr3[:],
                scalar1=nw[:, j:j + 1], scalar2=None, op0=Op.mult), inc="ve")
            p.drain("vector")

        ipmm = {}
        ipld = {}
        blk_i = 0
        sq_of = {}
        xn_of = {}
        ms1_ve = {}
        evac_bc = {}
        bcmm_of = {}
        abs_of = {}
        delta_ve = {}
        pair_ve = {}
        gp_pair = {}
        g_all = 0
        st_pair = {}
        tg = 0

        for c in range(n_ch):
            t0 = c * CH
            if c - 1 in sq_of:
                p.wait("sync", "se", sq_of[c - 1])
                p.wait("sync", "ve", xn_of[c - 1])
            p.dma("sync", xq.ap().rearrange("p (j t) -> p j t", j=8),
                  x2T[:, 3 + t0:3 + t0 + CH].rearrange("(j p) t -> p j t", j=8))
            ldc = p.cnt["ld"]

            def issue_blk(jt, blk0=blk_i):
                bi = blk0 + jt
                if bi - 3 >= 0:
                    p.wait("sync", "te", ipmm[bi - 3])
                p.wait("sync", "ld2", p.cnt["ld2"])
                ipld[bi] = p.dma(
                    "sync", ipb4[bi % 3].ap().rearrange("p (j k) -> p j k", j=8),
                    ipbf[:, jt * 128:(jt + 1) * 128].rearrange("(j p) k -> p j k", j=8),
                    inc="ld2")
            for jt in range(3):
                issue_blk(jt)

            p.wait("scalar", "ld", ldc)
            p.wait("vector", "ld", ldc)
            if c > 0:
                p.wait("vector", "te", ipmm[blk_i - 1])
                for j in range(8):
                    p.emit("vector", lambda v, j=j: v.tensor_copy(
                        xnb[:, j * (CH + 3): j * (CH + 3) + 3],
                        xnb[:, j * (CH + 3) + CH: j * (CH + 3) + CH + 3]), inc="ve")
                p.drain("vector")

            if c - 1 in ms1_ve:
                p.wait("tensor", "ve", ms1_ve[c - 1])
            stat_mm = {}
            for j in range(8):
                if j - 2 in stat_mm:
                    p.wait("scalar", "te", stat_mm[j - 2])
                se = p.emit("scalar", lambda s, j=j: s.activation(
                    out=scrq[j % 2][:], in_=xq[:, j * CH:(j + 1) * CH],
                    func=AF.Square), inc="se")
                p.wait("tensor", "se", se)
                stat_mm[j] = p.emit("tensor", lambda t, j=j: t.matmul(
                    p_st[:], ones_col[:], scrq[j % 2][:],
                    start=(j == 0), stop=(j == 7)), inc="te")
            sq_of[c] = p.cnt["se"]
            p.wait("vector", "te", p.cnt["te"])
            p.emit("vector", lambda v: v.tensor_scalar(out=ms1[:], in0=p_st[:],
                                                       scalar1=1.0 / D, scalar2=None,
                                                       op0=Op.mult), inc="ve")
            ms1_ve[c] = ve = p.cnt["ve"]
            p.drain("vector")
            p.wait("scalar", "ve", ve)
            p.emit("scalar", lambda s: s.activation(out=ms1[:], in_=ms1[:], func=AF.Ln,
                                                    bias=eps_t[:, 0:1]), inc="se")
            p.drain("scalar")
            se = p.emit("scalar", lambda s: s.activation(out=ms1[:], in_=ms1[:],
                                                         func=AF.Exp, scale=-0.5), inc="se")
            p.wait("tensor", "se", se)
            te = p.emit("tensor", lambda t: t.matmul(p_bc[:], ones_f[:], ms1[:],
                                                     start=True, stop=True), inc="te")
            p.wait("vector", "te", te)
            ve = p.emit("vector", lambda v: v.tensor_copy(rs_bc[:], p_bc[:]), inc="ve")
            p.drain("vector")
            p.wait("tensor", "ve", ve)
            for jp in range(4):
                for j in (2 * jp, 2 * jp + 1):
                    p.emit("vector", lambda v, j=j: v.tensor_tensor(
                        out=scrq[j % 2][:], in0=xq[:, j * CH:(j + 1) * CH],
                        in1=rs_bc[:], op=Op.mult), inc="ve")
                for j in (2 * jp, 2 * jp + 1):
                    p.emit("vector", lambda v, j=j: v.tensor_scalar(
                        out=xnb[:, j * (CH + 3) + 3:(j + 1) * (CH + 3)], in0=scrq[j % 2][:],
                        scalar1=nw[:, j:j + 1], scalar2=None, op0=Op.mult), inc="ve")
            xn_of[c] = ve_xn = p.cnt["ve"]
            p.wait("tensor", "ve", ve_xn)

            def mm_block(jt, bank, halo_slot, blk0):
                bi_ = blk0 + jt
                wb = ipb4[bi_ % 3]
                p.wait("tensor", "ld2", ipld[bi_])
                for j in range(8):
                    p.emit("tensor", lambda t, j=j, bank=bank, wb=wb: t.matmul(
                        p_xz[bank][:],
                        wb[:, j * 128:(j + 1) * 128],
                        xnb[:, j * (CH + 3) + 3: j * (CH + 3) + 3 + CH],
                        start=(j == 0), stop=(j == 7)), inc="te")
                if halo_slot is not None:
                    for j in range(8):
                        p.emit("tensor", lambda t, j=j, hs=halo_slot, wb=wb: t.matmul(
                            p_h3[:, hs * 4: hs * 4 + 3],
                            wb[:, j * 128:(j + 1) * 128],
                            xnb[:, j * (CH + 3): j * (CH + 3) + 3],
                            start=(j == 0), stop=(j == 7)), inc="te")
                ipmm[bi_] = p.cnt["te"]
                return ipmm[bi_]

            evac_ve = {}
            silu_of = {}
            for pk in range(8):
                jtA, jtB = 2 * pk, 2 * pk + 1
                if pk == 0 and 7 in abs_of:
                    p.wait("tensor", "se", abs_of[7])
                if pk - 1 in evac_ve:
                    p.wait("tensor", "ve", evac_ve[pk - 1])
                mm_block(jtA, 0, 0, blk_i)
                teB = mm_block(jtB, 1, 1, blk_i)
                for nb in (jtB + 2, jtB + 3):
                    if nb < 32:
                        issue_blk(nb)
                p.wait("vector", "te", teB)
                for i, (jt, it) in enumerate(((jtA, jtA), (jtB, jtB))):
                    p.emit("vector", lambda v, i=i: v.tensor_copy(
                        xsb2[i][:, 0:3], p_h3[:, i * 4: i * 4 + 3]), inc="ve")
                    p.emit("vector", lambda v, i=i: v.tensor_copy(
                        xsb2[i][:, 3:CH + 3], p_xz[i][:]), inc="ve")
                evac_ve[pk] = p.cnt["ve"]
                for i, it in ((0, jtA), (1, jtB)):
                    p.emit("vector", lambda v, i=i, it=it: v.tensor_scalar(
                        out=uacc2[i][:], in0=xsb2[i][:, 0:CH],
                        scalar1=cw[:, it * 4:it * 4 + 1],
                        scalar2=None, op0=Op.mult), inc="ve")
                for k in range(1, 4):
                    for i, it in ((0, jtA), (1, jtB)):
                        p.emit("vector", lambda v, i=i, it=it, k=k: v.scalar_tensor_tensor(
                            out=uacc2[i][:], in0=xsb2[i][:, k:k + CH],
                            scalar=cw[:, it * 4 + k:it * 4 + k + 1],
                            in1=uacc2[i][:], op0=Op.mult, op1=Op.add), inc="ve")
                taps_ve = p.cnt["ve"]
                p.wait("scalar", "ve", taps_ve)
                for i, it in ((0, jtA), (1, jtB)):
                    silu_of[pk] = p.emit("scalar", lambda s, i=i, it=it: s.activation(
                        out=ub[:, it * CH:(it + 1) * CH], in_=uacc2[i][:],
                        func=AF.Silu, bias=cb[:, it:it + 1]), inc="se")
                p.wait("vector", "se", silu_of[pk])
            for pk in range(8, 16):
                jtA, jtB = 2 * pk, 2 * pk + 1
                itA, itB = jtA - 16, jtB - 16
                if pk - 1 in evac_ve:
                    p.wait("tensor", "ve", evac_ve[pk - 1])
                mm_block(jtA, 0, None, blk_i)
                teB = mm_block(jtB, 1, None, blk_i)
                for nb in (jtB + 2, jtB + 3):
                    if nb < 32:
                        issue_blk(nb)
                p.wait("vector", "te", teB)
                for i, it in ((0, itA), (1, itB)):
                    p.emit("vector", lambda v, i=i, it=it: v.tensor_copy(
                        zb[:, it * CH:(it + 1) * CH], p_xz[i][:]), inc="ve")
                evac_ve[pk] = p.cnt["ve"]
            blk_i += 32

            p.wait("tensor", "se", silu_of[7])
            if c - 1 in evac_bc:
                pass
            for it in range(16):
                p.emit("tensor", lambda t, it=it: t.matmul(
                    p_dbc[:], xpb[:, it * 112:(it + 1) * 112],
                    ub[:, it * CH:(it + 1) * CH],
                    start=(it == 0), stop=(it == 15)), inc="te")
            te = p.cnt["te"]
            p.wait("vector", "te", te)
            p.emit("vector", lambda v: v.tensor_copy(dxb[:], p_dbc[0:DTR, :]), inc="ve")
            p.emit("vector", lambda v: v.tensor_copy(bcb[:, 0:CH], p_dbc[64:80, :]), inc="ve")
            p.emit("vector", lambda v: v.tensor_copy(bcb[:, CH:2 * CH], p_dbc[96:112, :]), inc="ve")
            evac_bc[c] = p.cnt["ve"]

            dladd = {}
            for pk in range(8):
                itA, itB = 2 * pk, 2 * pk + 1
                p.wait("tensor", "ve", evac_bc[c])
                if pk - 1 in abs_of:
                    p.wait("tensor", "se", abs_of[pk - 1])
                for i, it in ((0, itA), (1, itB)):
                    p.emit("tensor", lambda t, i=i, it=it: t.matmul(
                        p_xz[i][:], dtpb[:, it * 128:(it + 1) * 128], dxb[:],
                        start=True, stop=True), inc="te")
                te = p.cnt["te"]
                p.wait("scalar", "te", te)
                if pk - 1 in dladd:
                    p.wait("scalar", "ve", dladd[pk - 1])
                for i, it in ((0, itA), (1, itB)):
                    p.emit("scalar", lambda s, i=i, it=it: s.activation(
                        out=m2[i][:], in_=p_xz[i][:], func=AF.Relu,
                        bias=dtb[:, it:it + 1]), inc="se")
                for i, it in ((0, itA), (1, itB)):
                    p.emit("scalar", lambda s, i=i, it=it: s.activation(
                        out=a2[i][:], in_=p_xz[i][:], func=AF.Abs,
                        bias=dtb[:, it:it + 1]), inc="se")
                abs_of[pk] = p.cnt["se"]
                for i in (0, 1):
                    p.emit("scalar", lambda s, i=i: s.activation(
                        out=a2[i][:], in_=a2[i][:], func=AF.Exp, scale=-1.0), inc="se")
                for i in (0, 1):
                    p.emit("scalar", lambda s, i=i: s.activation(
                        out=a2[i][:], in_=a2[i][:], func=AF.Ln,
                        bias=ones_col[:, 0:1]), inc="se")
                se_sp = p.cnt["se"]
                p.wait("vector", "se", se_sp)
                for i, it in ((0, itA), (1, itB)):
                    p.emit("vector", lambda v, i=i, it=it: v.tensor_tensor(
                        out=dl[:, it * CH:(it + 1) * CH], in0=m2[i][:], in1=a2[i][:],
                        op=Op.add), inc="ve")
                for i, it in ((0, itA), (1, itB)):
                    p.emit("vector", lambda v, it=it: v.tensor_tensor(
                        out=du[:, it * CH:(it + 1) * CH], in0=dl[:, it * CH:(it + 1) * CH],
                        in1=ub[:, it * CH:(it + 1) * CH], op=Op.mult), inc="ve")
                dladd[pk] = p.cnt["ve"]
            delta_ve[c] = p.cnt["ve"]

            p.wait("scalar", "ve", delta_ve[c])
            for s in range(DS):
                p.wait("tensor", "ve", p.cnt["ve"])
                p.emit("tensor", lambda t, s=s: t.matmul(
                    p_b[:], sel16[:, s * 128:(s + 1) * 128], bcb[:, 0:CH],
                    start=True, stop=True), inc="te")
                te = p.emit("tensor", lambda t, s=s: t.matmul(
                    p_c[:], sel16[:, s * 128:(s + 1) * 128], bcb[:, CH:2 * CH],
                    start=True, stop=True), inc="te")
                if s == DS - 1:
                    bcmm_of[c] = p.cnt["te"]
                p.wait("vector", "te", te)
                p.emit("vector", lambda v: v.tensor_copy(B_bc[:], p_b[:]), inc="ve")
                p.emit("vector", lambda v: v.tensor_copy(C_bc[:], p_c[:]), inc="ve")
                for pk in range(NP):
                    g = g_all
                    itA, itB = 2 * pk, 2 * pk + 1
                    sA, sB = (2 * g) % 4, (2 * g + 1) % 4
                    if g - 2 in pair_ve:
                        p.wait("scalar", "ve", pair_ve[g - 2])
                    p.emit("scalar", lambda sc, it=itA, s=s, b=sA: sc.activation(
                        out=dAb[b][:], in_=dl[:, it * CH:(it + 1) * CH], func=AF.Exp,
                        scale=A_s[:, it * DS + s: it * DS + s + 1]), inc="se")
                    se_g = p.emit("scalar", lambda sc, it=itB, s=s, b=sB: sc.activation(
                        out=dAb[b][:], in_=dl[:, it * CH:(it + 1) * CH], func=AF.Exp,
                        scale=A_s[:, it * DS + s: it * DS + s + 1]), inc="se")
                    p.emit("vector", lambda v, it=itA: v.tensor_tensor(
                        out=dbu[0][:], in0=du[:, it * CH:(it + 1) * CH], in1=B_bc[:],
                        op=Op.mult), inc="ve")
                    p.emit("vector", lambda v, it=itB: v.tensor_tensor(
                        out=dbu[1][:], in0=du[:, it * CH:(it + 1) * CH], in1=B_bc[:],
                        op=Op.mult), inc="ve")
                    p.wait("vector", "se", se_g)
                    p.emit("vector", lambda v, it=itA, s=s, b=sA: v.tensor_tensor_scan(
                        out=hh[0][:], data0=dAb[b][:], data1=dbu[0][:],
                        initial=carry[:, it * DS + s: it * DS + s + 1],
                        op0=Op.mult, op1=Op.add), inc="ve")
                    p.emit("vector", lambda v, it=itB, s=s, b=sB: v.tensor_tensor_scan(
                        out=hh[1][:], data0=dAb[b][:], data1=dbu[1][:],
                        initial=carry[:, it * DS + s: it * DS + s + 1],
                        op0=Op.mult, op1=Op.add), inc="ve")
                    p.emit("vector", lambda v, it=itA, s=s: v.tensor_copy(
                        carry[:, it * DS + s: it * DS + s + 1], hh[0][:, CH - 1:CH]), inc="ve")
                    p.emit("vector", lambda v, it=itB, s=s: v.tensor_copy(
                        carry[:, it * DS + s: it * DS + s + 1], hh[1][:, CH - 1:CH]), inc="ve")
                    if g - 2 in gp_pair:
                        p.wait("vector", "gp", gp_pair[g - 2])
                    p.emit("vector", lambda v, b=sA: v.tensor_tensor(
                        out=tmp[b][:], in0=hh[0][:], in1=C_bc[:], op=Op.mult), inc="ve")
                    tv = p.emit("vector", lambda v, b=sB: v.tensor_tensor(
                        out=tmp[b][:], in0=hh[1][:], in1=C_bc[:], op=Op.mult), inc="ve")
                    pair_ve[g] = tv
                    p.wait("gpsimd", "ve", tv)
                    for it, b in ((itA, sA), (itB, sB)):
                        if s == 0:
                            gd = p.emit("gpsimd", lambda gq, it=it, b=b: gq.tensor_copy(
                                y_acc[:, it * CH:(it + 1) * CH], tmp[b][:]), inc="gp")
                        else:
                            gd = p.emit("gpsimd", lambda gq, it=it, b=b: gq.tensor_tensor(
                                out=y_acc[:, it * CH:(it + 1) * CH],
                                in0=y_acc[:, it * CH:(it + 1) * CH], in1=tmp[b][:],
                                op=Op.add), inc="gp")
                    gp_pair[g] = gd
                    g_all += 1
            p.wait("vector", "gp", p.cnt["gp"])
            p.wait("scalar", "ve", p.cnt["ve"])
            tail_mult = {}
            for pk in range(NP):
                itA, itB = 2 * pk, 2 * pk + 1
                if pk - 1 in tail_mult:
                    p.wait("scalar", "ve", tail_mult[pk - 1])
                for i, it in ((0, itA), (1, itB)):
                    p.emit("scalar", lambda sc, it=it, i=i: sc.activation(
                        out=sz[i][:], in_=zb[:, it * CH:(it + 1) * CH], func=AF.Silu), inc="se")
                se_t = p.cnt["se"]
                if tg - 1 in st_pair:
                    p.wait("vector", "st", st_pair[tg - 1])
                for i, it in ((0, itA), (1, itB)):
                    p.emit("vector", lambda v, it=it, i=i: v.scalar_tensor_tensor(
                        out=yy[i][:], in0=ub[:, it * CH:(it + 1) * CH],
                        scalar=D_s[:, it:it + 1], in1=y_acc[:, it * CH:(it + 1) * CH],
                        op0=Op.mult, op1=Op.add), inc="ve")
                p.wait("vector", "se", se_t)
                for i, it in ((0, itA), (1, itB)):
                    p.emit("vector", lambda v, i=i: v.tensor_tensor(
                        out=yy[i][:], in0=yy[i][:], in1=sz[i][:], op=Op.mult), inc="ve")
                tail_mult[pk] = ve = p.cnt["ve"]
                p.wait("sync", "ve", ve)
                for i, it in ((0, itA), (1, itB)):
                    last_st = p.dma("sync", y2T[it * 128:(it + 1) * 128, t0:t0 + CH],
                                    yy[i][:], inc="st")
                st_pair[tg] = last_st
                tg += 1
    return


I32 = mybir.dt.int32
H = T // 2


def build_fused(debug=False):
    nc = bass.Bass("TRN2", target_bir_lowering=False)
    gi = lambda n, s: nc.dram_tensor(n, list(s), F32, kind="ExternalInput").ap()
    x = gi("x", (T, D))
    own0 = nc.dram_tensor("own0", [1, 1], I32, kind="ExternalInput").ap()
    basis = gi("basis", (D, 64))
    qc, kc, vc, oc = (gi(n, (128, 64)) for n in ("qc", "kc", "vc", "oc"))
    decay = gi("decay", (1, 1))
    oscale = gi("oscale", (1, 1))
    f0_W1 = gi("f0_W1", (8, D, 2048))
    f0_W2 = gi("f0_W2", (8, 2048, D))
    f0_b1r = gi("f0_b1r", (128, 8 * NF))
    f0_b2 = gi("f0_b2", (8, D))
    f0_rW = gi("f0_rW", (D, 8))
    f0_rb = gi("f0_rb", (1, 8))
    f0_g = gi("f0_g", (128, 8))
    f0_b = gi("f0_b", (128, 8))
    ipT = gi("ipT", (D, 2 * DI))
    convw_r = gi("convw_r", (128, 64))
    convb_r = gi("convb_r", (128, 16))
    xpT = gi("xpT", (DI, 112))
    dtpT = gi("dtpT", (64, DI))
    dtb_r = gi("dtb_r", (128, 16))
    normw_r = gi("normw_r", (128, 8))
    A_r = gi("A_r", (128, 16 * DS))
    D_r = gi("D_r", (128, 16))
    opT = gi("opT", (DI, D))
    f1_W1 = gi("f1_W1", (16, D, 2048))
    f1_W2 = gi("f1_W2", (16, 2048, D))
    f1_b1r = gi("f1_b1r", (128, 16 * NF))
    f1_b2 = gi("f1_b2", (16, D))
    f1_rW = gi("f1_rW", (D, 16))
    f1_rb = gi("f1_rb", (1, 16))

    kind_i = "ExternalOutput" if debug else "Internal"
    it_ = lambda n, s: nc.dram_tensor(n, list(s), F32, kind=kind_i).ap()
    x1T = it_("x1T", (D, T))
    x2T = it_("x2T", (D, 3 + T))
    ipbf = nc.dram_tensor("ipbf", [D, 2 * DI], BF16, kind="Internal").ap()
    y2T = it_("y2T", (DI, T))
    y2own = it_("y2own", (DI, H))
    x2own = it_("x2own", (D, H))
    x3T = it_("x3T", (D, H))
    x4T = nc.dram_tensor("x4T", [D, H], F32, kind="ExternalOutput").ap()

    from contextlib import contextmanager

    @contextmanager
    def stage_names(tag):
        orig_sb, orig_ps = nc.sbuf_tensor, nc.psum_tensor
        nc.sbuf_tensor = lambda name, *a, **k: orig_sb(f"{tag}_{name}", *a, **k)
        nc.psum_tensor = lambda name, *a, **k: orig_ps(f"{tag}_{name}", *a, **k)
        try:
            yield
        finally:
            del nc.sbuf_tensor, nc.psum_tensor

    with ExitStack() as st_:
        e = st_.enter_context
        block = e(nc.Block())
        p = Prog(nc, st_)
        offs = e(nc.sbuf_tensor("offs", [1, 1], I32))
        z3 = e(nc.sbuf_tensor("z3", [128, 8 * 3], F32))
        own_cell = {}

        def barrier():
            snap = dict(p.cnt)
            for eng in ENGINES:
                for s_, v in snap.items():
                    if v:
                        p.wait(eng, s_, v)

        p.dma("sync", offs[:], own0)
        p.wait("sync", "ld", p.cnt["ld"])

        def _load_off(s):
            own_cell["v"] = nc.values_load(
                offs[0:1, 0:1], engines=[mybir.EngineType.SP],
                min_val=0, max_val=H, skip_runtime_bounds_check=True)
        p.emit("sync", _load_off)
        ve = p.emit("vector", lambda v: v.memset(z3[:], 0.0), inc="ve")
        p.wait("sync", "ve", ve)
        p.dma("sync", x2T[:, 0:3].rearrange("(j p) t -> p j t", j=8),
              z3.ap().rearrange("p (j t) -> p j t", j=8), inc="st")
        barrier()

        with stage_names("s1"):
            emit_l0attn(nc, p, x, basis, qc, kc, vc, oc, decay, oscale, x1T, 0, T)
        barrier()
        with stage_names("s2"):
            emit_ffn(nc, p, 8, True, T, x1T, f0_W1, f0_W2, f0_b1r, f0_b2,
                     f0_rW, f0_rb, x2T[:, 3:3 + T], g_r=f0_g, b_r=f0_b)
        barrier()
        with stage_names("s3"):
            emit_minscan(nc, p, T, x2T, ipT, ipbf, convw_r, convb_r, xpT, dtpT,
                         dtb_r, normw_r, A_r, D_r, y2T, D=D, DI=DI)
        barrier()
        p.emit("sync", lambda s: s.dma_start(
            out=y2own, in_=y2T[:, bass.ds(own_cell["v"], H)]), inc="ld", amt=16)
        p.emit("sync", lambda s: s.dma_start(
            out=x2own, in_=x2T[:, bass.ds(own_cell["v"] + 3, H)]), inc="ld", amt=16)
        p.wait("sync", "ld", p.cnt["ld"])
        barrier()
        with stage_names("s5"):
            emit_mout(nc, p, H, y2own, x2own, opT, x3T)
        barrier()
        with stage_names("s6"):
            emit_ffn(nc, p, 16, False, H, x3T, f1_W1, f1_W2, f1_b1r, f1_b2,
                     f1_rW, f1_rb, x4T)
        p.wait("gpsimd", "st", p.cnt["st"])
        p.replay(block)
    return nc


def _pad_xp(xpT):
    out = np.zeros((xpT.shape[0], 112), xpT.dtype)
    out[:, 0:80] = xpT[:, 0:80]
    out[:, 96:112] = xpT[:, 80:96]
    return out


def make_in_maps(ins, x_batches):
    E0, E1 = 8, 16
    cw = ins["L1_attn_conv_w"][:, 0, :]
    A_log = ins["L1_attn_A_log"]
    common = dict(
        basis=ins["L0_attn_basis"],
        qc=ins["L0_attn_q_coeffs"], kc=ins["L0_attn_k_coeffs"],
        vc=ins["L0_attn_v_coeffs"], oc=ins["L0_attn_o_coeffs"],
        decay=np.asarray(ins["L0_attn_decay"]).reshape(1, 1),
        oscale=np.asarray(ins["L0_attn_out_scale"]).reshape(1, 1),
        f0_W1=ins["L0_ffn_W1"], f0_W2=ins["L0_ffn_W2"],
        f0_b1r=ins["L0_ffn_b1"].reshape(E0, NF, 128).transpose(2, 0, 1).reshape(128, E0 * NF).copy(),
        f0_b2=ins["L0_ffn_b2"], f0_rW=ins["L0_ffn_rW"],
        f0_rb=ins["L0_ffn_rb"].reshape(1, E0),
        f0_g=ins["L0_ffn_ln_g"].reshape(8, 128).T.copy(),
        f0_b=ins["L0_ffn_ln_b"].reshape(8, 128).T.copy(),
        ipT=np.ascontiguousarray(ins["L1_attn_in_proj_w"].T),
        convw_r=cw.reshape(16, 128, 4).transpose(1, 0, 2).reshape(128, 64).copy(),
        convb_r=ins["L1_attn_conv_b"].reshape(16, 128).T.copy(),
        xpT=_pad_xp(ins["L1_attn_x_proj_w"].T),
        dtpT=np.ascontiguousarray(ins["L1_attn_dt_proj_w"].T),
        dtb_r=ins["L1_attn_dt_proj_b"].reshape(16, 128).T.copy(),
        normw_r=ins["L1_attn_norm_w"].reshape(8, 128).T.copy(),
        A_r=A_log.reshape(16, 128, DS).transpose(1, 0, 2).reshape(128, 16 * DS).copy(),
        D_r=ins["L1_attn_D_param"].reshape(16, 128).T.copy(),
        opT=np.ascontiguousarray(ins["L1_attn_out_proj_w"].T),
        f1_W1=ins["L1_ffn_W1"], f1_W2=ins["L1_ffn_W2"],
        f1_b1r=ins["L1_ffn_b1"].reshape(E1, NF, 128).transpose(2, 0, 1).reshape(128, E1 * NF).copy(),
        f1_b2=ins["L1_ffn_b2"], f1_rW=ins["L1_ffn_Wr"],
        f1_rb=ins["L1_ffn_br"].reshape(1, E1),
    )
    maps = []
    for c in range(8):
        b, h = c // 2, c % 2
        maps.append(dict(x=np.ascontiguousarray(x_batches[b]),
                         own0=np.array([[h * H]], np.int32), **common))
    return maps



_NC_CACHE = {}


def kernel(**ins):
    x = np.asarray(ins["L0_attn_x"])
    if "fused" not in _NC_CACHE:
        _NC_CACHE["fused"] = build_fused()
    nc = _NC_CACHE["fused"]
    maps = make_in_maps(ins, x)
    res = run_bass_kernel_spmd(nc, maps, core_ids=list(range(8)))
    x4 = np.empty_like(x)
    for c in range(8):
        b, h = c // 2, c % 2
        x4[b, h * H:(h + 1) * H] = res.results[c]["x4T"].T
    return x4
